# version 03
# baseline (speedup 1.0000x reference)
import numpy as np
import ml_dtypes
from contextlib import ExitStack

import concourse.bass as bass
import concourse.mybir as mybir
from concourse.bass_utils import run_bass_kernel_spmd

F32 = mybir.dt.float32
BF16 = mybir.dt.bfloat16
I8 = mybir.dt.int8
AF = mybir.ActivationFunctionType
OP = mybir.AluOpType
AX = mybir.AxisListType
ds = bass.ds

f32 = np.float32

B, S, D = 4, 8192, 1024
HPC = 4
NSEG = 16
TPC = 4096
NTILE = 32
II = 2816
NCH = 22
PREP_SLOTS = 3
EPS_ATT = f32(1e-6)
EPS_LN = 1e-5
MAGIC = 12582912.0
INV_SQRT_DK = f32(1.0 / np.sqrt(128.0))
INV127 = 1.0 / 127.0
CCR = PREP_SLOTS * 1024 * 2 + II

ENGS = ("sync", "gpsimd", "vector", "scalar", "tensor")


class Prog:
    def __init__(self):
        self.ops = {e: [] for e in ENGS}
        self.ctr = {}
        self.waited = {e: {} for e in ENGS}
        self.sems = {}

    def inc(self, sem, n=1):
        self.ctr[sem] = self.ctr.get(sem, 0) + n
        return self.ctr[sem]

    def val(self, sem):
        return self.ctr.get(sem, 0)

    def wait(self, eng, sem, v=None):
        if v is None:
            v = self.val(sem)
        if v <= 0:
            return
        if self.waited[eng].get(sem, -1) >= v:
            return
        self.waited[eng][sem] = v

        def fn(e, sem=sem, v=v):
            e.wait_ge(self.sems[sem], v)
        self.ops[eng].append(fn)


def build_program(debug=False, phases=4, seg_detail="full"):
    ph_dump = phases if isinstance(phases, str) else None
    if ph_dump is not None:
        phases = 3.5 if ph_dump == "x1" else 2.5
    nc = bass.Bass("TRN2", target_bir_lowering=False, detect_race_conditions=False)
    P = Prog()

    x_in = nc.dram_tensor("x", [S, D], F32, kind="ExternalInput").ap()
    wq_in = nc.dram_tensor("wq", [D, 512], F32, kind="ExternalInput").ap()
    wk_in = nc.dram_tensor("wk", [D, 512], F32, kind="ExternalInput").ap()
    wv_in = nc.dram_tensor("wv", [D, 512], F32, kind="ExternalInput").ap()
    bq_in = nc.dram_tensor("bq", [1, 512], F32, kind="ExternalInput").ap()
    bk_in = nc.dram_tensor("bk", [1, 512], F32, kind="ExternalInput").ap()
    bv_in = nc.dram_tensor("bv", [1, 512], F32, kind="ExternalInput").ap()
    beta_in = nc.dram_tensor("beta", [1, HPC], F32, kind="ExternalInput").ap()
    clipx_in = nc.dram_tensor("clipx", [1, 1], F32, kind="ExternalInput").ap()
    clipd_in = nc.dram_tensor("clipd", [1, 1], F32, kind="ExternalInput").ap()
    tril_in = nc.dram_tensor("tril", [128, 128], BF16, kind="ExternalInput").ap()
    wgp_in = nc.dram_tensor("wg_prep", [PREP_SLOTS * 128, D], F32, kind="ExternalInput").ap()
    wup_in = nc.dram_tensor("wu_prep", [PREP_SLOTS * 128, D], F32, kind="ExternalInput").ap()
    wdp_in = nc.dram_tensor("wd_prep", [128, II], F32, kind="ExternalInput").ap()

    x2_in = nc.dram_tensor("x2", [TPC, D], F32, kind="ExternalInput").ap()
    y_out = nc.dram_tensor("y", [TPC, D], F32, kind="ExternalOutput").ap()

    a_xchg = nc.dram_tensor("a_xchg", [S, D], BF16, kind="Internal", addr_space="Shared").ap()
    ax_priv = nc.dram_tensor("ax_priv", [NSEG, 4, 128, 512], BF16, kind="Internal").ap()
    a_loc = nc.dram_tensor("a_loc", [TPC, D], BF16, kind="Internal").ap()
    h_spill = nc.dram_tensor("h_spill", [NCH, 128, TPC], BF16, kind="Internal").ap()
    scr = nc.dram_tensor("scr", [1, 256], F32, kind="Internal").ap()
    cc_in = nc.dram_tensor("cc_in", [CCR, 128], BF16, kind="Internal")
    cc_out = nc.dram_tensor("cc_out", [8 * CCR, 128], BF16, kind="Internal")
    bar_in = nc.dram_tensor("bar_in", [1, 8], F32, kind="Internal")
    bar_out = nc.dram_tensor("bar_out", [1, 8], F32, kind="Internal")
    mx_in = nc.dram_tensor("mx_in", [1, 8], F32, kind="Internal")
    mx_out = nc.dram_tensor("mx_out", [1, 8], F32, kind="Internal")
    hm_in = nc.dram_tensor("hm_in", [1, 8], F32, kind="Internal")
    hm_out = nc.dram_tensor("hm_out", [1, 8], F32, kind="Internal")

    dbg = {}
    if debug:
        def dbg_t(name, shape, dt=F32):
            dbg[name] = nc.dram_tensor("dbg_" + name, shape, dt, kind="ExternalOutput").ap()
        dbg_t("qT", [128, HPC, 512])
        dbg_t("sqT", [128, HPC, 512])
        dbg_t("skT", [128, HPC, 512])
        dbg_t("v", [128, 4, HPC, 130])
        dbg_t("p", [128, 4, 512])
        dbg_t("M", [128, HPC, 130])
        dbg_t("astage", [128, 4, 512])
        dbg_t("r0", [128, 1024])
        dbg_t("gmax", [1, 8])
        dbg_t("h0", [128, 512])
        dbg_t("sg0", [128, 512])
        dbg_t("su0", [128, 512])
        dbg_t("gps", [128, 512])
        dbg_t("ups", [128, 512])
        dbg_t("xqT0", [128, 8, 128])
        dbg_t("wgt0", [128, 8, 128])
        dbg_t("hmax", [1, 8])
        dbg_t("kh", [128, NCH, 512])
        dbg_t("z0", [128, 1024])

    es0 = ExitStack()
    es1 = ExitStack()
    es2p = ExitStack()
    es2a = ExitStack()
    es2b = ExitStack()

    def mk_sb(stack):
        def sbf(name, shape, dt=F32):
            return stack.enter_context(nc.sbuf_tensor(name, shape, dt))
        return sbf

    sb0, sb1 = mk_sb(es0), mk_sb(es1)

    sgc = sb0("sgc", [1, 8])
    g_bc = sb0("g_bc", [128, HPC])
    gm1_bc = sb0("gm1_bc", [128, HPC])
    ones_r = sb0("ones_r", [1, 512], BF16)
    tril_bf = sb0("tril_bf", [128, 128], BF16)
    lmax = sb0("lmax", [1, 8])
    row128 = sb0("row128", [1, 128])
    sx_bc = sb0("sx_bc", [128, 1])
    rx_bc = sb0("rx_bc", [128, 1])
    sh_bc = sb0("sh_bc", [128, 1])
    rh_bc = sb0("rh_bc", [128, 1])
    st2 = sb0("st2", [128, 4])
    t_rstd = sb0("t_rstd", [128, 2])
    xmx = sb0("xmx", [128, 2 * NTILE])
    hmx = sb0("hmx", [128, 2 * 8 * NCH])

    psA = es0.enter_context(nc.psum_tensor("psA", [128, 2048], F32))
    psB = es0.enter_context(nc.psum_tensor("psB", [128, 2048], F32))

    for sn in ["s_ld", "s_gld", "s_tp", "s_mm", "s_ve", "s_se", "s_st", "s_cc", "s_ex"]:
        P.sems[sn] = es0.enter_context(nc.semaphore(sn))
    SEM = P.sems

    wq_bf = sb1("wq_bf", [128, 8, 512], BF16)
    wk_bf = sb1("wk_bf", [128, 8, 512], BF16)
    wv_bf = sb1("wv_bf", [128, 8, 512], BF16)
    bq_bf = sb1("bq_bf", [1, 512], BF16)
    bk_bf = sb1("bk_bf", [1, 512], BF16)
    bv_bf = sb1("bv_bf", [1, 512], BF16)
    x_tm = sb1("x_tm", [128, 2, 4, 1024], BF16)
    x_fm = sb1("x_fm", [128, 2, 8, 4, 128], BF16)
    q_bf = sb1("q_bf", [128, HPC, 512], BF16)
    k_bf = sb1("k_bf", [128, HPC, 512], BF16)
    sq_bf = sb1("sq_bf", [128, HPC, 512], BF16)
    sk_bf = sb1("sk_bf", [128, HPC, 512], BF16)
    v_bf = sb1("v_bf", [128, 4, HPC, 130], BF16)
    sk_tm = sb1("sk_tm", [128, HPC, 4, 128], BF16)
    p_bf = sb1("p_bf", [128, 4, 512], BF16)
    pT_bf = sb1("pT_bf", [128, 4, 4, 128], BF16)
    vd_bf = sb1("vd_bf", [128, 4, 130], BF16)
    M_f32 = sb1("M_f32", [128, HPC, 130])
    M_bf = sb1("M_bf", [128, HPC, 130], BF16)
    t_scr1 = sb1("t_scr1", [128, 2048])
    t_scr2 = sb1("t_scr2", [128, 2048])
    t_den = sb1("t_den", [128, 8])
    t_rq = sb1("t_rq", [128, 8])
    t_rqg = sb1("t_rqg", [128, 4])
    t_rkn = sb1("t_rkn", [128, 4])
    t_rd = sb1("t_rd", [128, 4])
    t_r2g = sb1("t_r2g", [128, 4])
    t_nm = sb1("t_nm", [128, 4])
    a_stage = sb1("a_stage", [128, 2, 4, 512], BF16)
    wp_sb = sb1("wp_sb", [128, PREP_SLOTS, 1024])
    wup_sb = sb1("wup_sb", [128, PREP_SLOTS, 1024])
    wdp_sb = sb1("wdp_sb", [128, II])
    wqz = sb1("wqz", [128, 1024], BF16)
    wdz = sb1("wdz", [128, II], BF16)
    wT_sb = sb1("wT_sb", [128, 8, 128], BF16)
    wdT_s = sb1("wdT_s", [128, NCH, 128], BF16)
    t_rm = sb1("t_rm", [128, 4])

    def emit(eng_name, fn, inc=None, n=1, waits=()):
        for (sm, v) in waits:
            P.wait(eng_name, sm, v)
        tgt = P.inc(inc, n) if inc else None

        def run(e, fn=fn, inc=inc, n=n):
            inst = fn(e)
            if inc is not None and inst is not None:
                inst.then_inc(SEM[inc], n)
        P.ops[eng_name].append(run)
        return tgt

    v_wgp = emit("sync", lambda e: e.dma_start(out=wp_sb[:], in_=wgp_in.rearrange("(c p) d -> p c d", p=128)), inc="s_ld", n=16)
    v_wup = emit("sync", lambda e: e.dma_start(out=wup_sb[:], in_=wup_in.rearrange("(c p) d -> p c d", p=128)), inc="s_ld", n=16)
    v_wdp = emit("sync", lambda e: e.dma_start(out=wdp_sb[:], in_=wdp_in), inc="s_ld", n=16)
    for (wi, wb) in ((wq_in, wq_bf), (wk_in, wk_bf), (wv_in, wv_bf)):
        emit("gpsimd", lambda e, wi=wi, wb=wb: e.dma_start(
            out=wb[:], in_=wi.rearrange("(c p) m -> p c m", p=128)), inc="s_gld", n=16)
    for (bi, bb) in ((bq_in, bq_bf), (bk_in, bk_bf), (bv_in, bv_bf)):
        emit("gpsimd", lambda e, bi=bi, bb=bb: e.dma_start(out=bb[:], in_=bi), inc="s_gld", n=16)
    v_wcast = P.val("s_gld")
    emit("sync", lambda e: e.dma_start(out=sgc[0:1, 0:4], in_=beta_in[0:1, :]), inc="s_ld", n=16)
    emit("sync", lambda e: e.dma_start(out=sgc[0:1, 4:5], in_=clipx_in[0:1, :]), inc="s_ld", n=16)
    emit("sync", lambda e: e.dma_start(out=sgc[0:1, 5:6], in_=clipd_in[0:1, :]), inc="s_ld", n=16)
    emit("sync", lambda e: e.dma_start(out=tril_bf[:], in_=tril_in), inc="s_ld", n=16)
    v_small = P.val("s_ld")
    emit("vector", lambda e: e.memset(ones_r[:], 1.0))
    emit("vector", lambda e: e.memset(p_bf.ap().rearrange("p a b -> p (a b)"), 0.0))
    emit("vector", lambda e: e.memset(v_bf.ap().rearrange("p a b c -> p (a b c)"), 1.0))
    emit("vector", lambda e: e.memset(vd_bf.ap().rearrange("p a b -> p (a b)"), 1.0))
    emit("vector", lambda e: e.memset(M_f32.ap().rearrange("p h c -> p (h c)"), 0.0))
    emit("vector", lambda e: e.memset(M_bf.ap().rearrange("p h c -> p (h c)"), 0.0), inc="s_ve")
    v_const = P.val("s_ve")
    emit("scalar", lambda e: e.activation(sgc[0:1, 6:7], sgc[0:1, 4:5], AF.Sigmoid),
         waits=[("s_ld", v_small)])
    emit("scalar", lambda e: e.activation(sgc[0:1, 7:8], sgc[0:1, 5:6], AF.Sigmoid))
    v_sig = emit("scalar", lambda e: e.activation(sgc[0:1, 0:4], sgc[0:1, 0:4], AF.Sigmoid), inc="s_se")
    v_gw = emit("gpsimd", lambda e: e.dma_start(out=scr[0:1, 0:8], in_=sgc[0:1, 0:8]),
                inc="s_ex", n=16, waits=[("s_se", v_sig)])
    v_gbc = emit("gpsimd", lambda e: e.dma_start(out=g_bc[:], in_=scr[0:1, 0:4].broadcast_to([128, 4])),
                 inc="s_ex", n=16, waits=[("s_ex", v_gw)])
    v_gm1 = emit("vector", lambda e: e.tensor_scalar(gm1_bc[:], g_bc[:], -1.0, 1.0, OP.mult, OP.add),
                 inc="s_ve", waits=[("s_ex", v_gbc)])

    def prep_one(kind, slot):
        if kind == "g":
            src_ap, row_base, wait_ld = wp_sb[:, slot, :], slot * 1024, v_wgp
        elif kind == "u":
            src_ap, row_base, wait_ld = wup_sb[:, slot, :], (PREP_SLOTS + slot) * 1024, v_wup
        else:
            src_ap, row_base, wait_ld = wdp_sb[:], 2 * PREP_SLOTS * 1024, v_wdp
        p1 = emit("vector", lambda e: e.tensor_reduce(t_rm[:, 0:1], src_ap, AX.X, OP.max,
                                                      apply_absolute_value=True),
                  waits=[("s_ld", wait_ld), ("s_tp", P.val("s_tp"))], inc="s_ve")
        p2 = emit("vector", lambda e: e.tensor_scalar(t_rm[:, 1:2], t_rm[:, 0:1], 1e-30, None, OP.max),
                  waits=[("s_ve", p1)], inc="s_ve")
        p3 = emit("vector", lambda e: e.reciprocal(out=t_rm[:, 2:3], in_=t_rm[:, 1:2]),
                  waits=[("s_ve", p2)], inc="s_ve")
        p4 = emit("vector", lambda e: e.tensor_scalar(t_rm[:, 2:3], t_rm[:, 2:3], 127.0, None, OP.mult),
                  waits=[("s_ve", p3)], inc="s_ve")
        p5 = emit("vector", lambda e: e.tensor_scalar(t_rm[:, 3:4], t_rm[:, 1:2], INV127, None, OP.mult),
                  waits=[("s_ve", p4)], inc="s_ve")
        if kind != "d":
            emit("vector", lambda e: e.tensor_scalar(t_scr1[:, 0:1024], src_ap, t_rm[:, 2:3], MAGIC, OP.mult, OP.add),
                 waits=[("s_ve", p5)])
            v_q = emit("vector", lambda e: e.tensor_scalar(wqz[:], t_scr1[:, 0:1024], MAGIC, t_rm[:, 3:4], OP.subtract, OP.mult), inc="s_ve")
            for half in range(2):
                emit("sync", lambda e, half=half: e.dma_start(
                    out=wT_sb[:, half * 4:half * 4 + 4, :],
                    in_=wqz[:, half * 512:(half + 1) * 512], transpose=True),
                    inc="s_tp", n=16, waits=[("s_ve", v_q)])
            v_t = P.val("s_tp")
            emit("gpsimd", lambda e, row_base=row_base: e.dma_start(
                out=cc_in.ap()[row_base:row_base + 1024, :].rearrange("(c p) i -> p c i", p=128),
                in_=wT_sb[:]), inc="s_ex", n=16, waits=[("s_tp", v_t)])
        else:
            for part, (o0, o1) in enumerate(((0, 1024), (1024, 2048), (2048, 2816))):
                w = o1 - o0
                emit("vector", lambda e, o0=o0, w=w: e.tensor_scalar(
                    t_scr1[:, 0:w], src_ap[:, o0:o0 + w], t_rm[:, 2:3], MAGIC, OP.mult, OP.add),
                    waits=[("s_ve", p5)] if o0 == 0 else ())
                emit("vector", lambda e, o0=o0, w=w: e.tensor_scalar(
                    wdz[:, o0:o0 + w], t_scr1[:, 0:w], MAGIC, t_rm[:, 3:4], OP.subtract, OP.mult),
                    inc="s_ve" if o1 == 2816 else None)
            v_q = P.val("s_ve")
            off = 0
            while off < II:
                wch = min(512, II - off)
                emit("sync", lambda e, off=off, wch=wch: e.dma_start(
                    out=wdT_s[:, off // 128: (off + wch) // 128, :],
                    in_=wdz[:, off:off + wch], transpose=True),
                    inc="s_tp", n=16, waits=[("s_ve", v_q)])
                off += wch
            v_t = P.val("s_tp")
            emit("gpsimd", lambda e, row_base=row_base: e.dma_start(
                out=cc_in.ap()[row_base:row_base + II, :].rearrange("(c p) i -> p c i", p=128),
                in_=wdT_s[:]), inc="s_ex", n=16, waits=[("s_tp", v_t)])

    for slot in range(PREP_SLOTS):
        prep_one("g", slot)
    for slot in range(PREP_SLOTS):
        prep_one("u", slot)
    prep_one("d", 0)
    v_prep = P.val("s_ex")
    emit("gpsimd", lambda e: e.collective_compute(
        "AllGather", OP.bypass, replica_groups=[list(range(8))],
        ins=[cc_in.ap().opt()], outs=[cc_out.ap().opt()]),
        inc="s_cc", waits=[("s_ex", v_prep)])
    V_AG = P.val("s_cc")

    pcol_holder = {}

    def g_pcol(e):
        if "v" not in pcol_holder:
            pid = e.partition_id()
            pcol_holder["v"] = e.snap((pid % 2) * 512, min_val=0, max_val=512)
        return pcol_holder["v"]

    v_Mbf = [v_const] * HPC
    v_pT_qc = [0, 0, 0, 0]
    v_head_vec_prev = v_const
    v_am_prev = 0
    v_comb_prev = v_const
    v_xfm_hist = [0, 0]
    v_mmv_hist = [0, 0]
    v_astore_hist = [0, 0]
    v_axp_hist = [0, 0]

    for s in range(NSEG if phases >= 2 else 0):
        sp = s % 2
        v_xld = emit("gpsimd", lambda e, s=s, sp=sp: e.dma_start(
            out=x_tm[:, sp, :, :],
            in_=x_in[s * 512:(s + 1) * 512, :].rearrange("(tc p) d -> p tc d", p=128)),
            inc="s_gld", n=16, waits=[("s_tp", v_xfm_hist[sp])])
        for tc in range(4):
            for dc in range(8):
                emit("sync", lambda e, sp=sp, tc=tc, dc=dc: e.dma_start(
                    out=x_fm[:, sp, dc, tc, :],
                    in_=x_tm[:, sp, tc, dc * 128:(dc + 1) * 128], transpose=True),
                    inc="s_tp", n=16,
                    waits=[("s_gld", v_xld), ("s_mm", v_mmv_hist[sp])])
        v_xfm = P.val("s_tp")
        v_xfm_hist[sp] = v_xfm
        if seg_detail == "xonly":
            P.wait("gpsimd", "s_tp", v_xfm)
            continue

        def proj_group(e, w_sb_, b_sb_, f, dst, sp):
            for d in range(8):
                e.matmul(dst, w_sb_[:, d, f * 128:(f + 1) * 128],
                         x_fm[:, sp, d, :, :].rearrange("p a b -> p (a b)"),
                         start=(d == 0), stop=False)
            return e.matmul(dst, b_sb_[0:1, f * 128:(f + 1) * 128], ones_r[:],
                            start=False, stop=True)

        _gates = {"qmm": 1, "qevac1": 1.2, "qevac2": 1.4, "qevac3": 1.6, "qevac": 2, "vmm": 3, "kevac": 4, "vevac": 5, "sk": 6, "proj": 7, "full": 99}
        _lvl = _gates.get(seg_detail, 99)
        q_waits = [("s_tp", v_xfm), ("s_gld", v_wcast),
                   ("s_ve", max(v_const, v_head_vec_prev)), ("s_se", v_am_prev)]
        for f in range(HPC):
            emit("tensor", lambda e, f=f, sp=sp: proj_group(e, wq_bf, bq_bf, f, psA[:, f * 512:(f + 1) * 512], sp),
                 inc="s_mm", waits=q_waits if f == 0 else ())
        v_mm_q = P.val("s_mm")
        for f in range(HPC):
            emit("tensor", lambda e, f=f, sp=sp: proj_group(e, wk_bf, bk_bf, f, psB[:, f * 512:(f + 1) * 512], sp),
                 inc="s_mm")
        v_mm_k = P.val("s_mm")

        if _lvl < 1.2:
            P.wait("gpsimd", "s_mm", P.val("s_mm"))
            continue
        v_qcopy = emit("scalar", lambda e: e.activation(q_bf.ap().rearrange("p h t -> p (h t)"), psA[:],
                                                        AF.Copy, scale=INV_SQRT_DK),
                       inc="s_se", waits=[("s_mm", v_mm_q)])
        if _lvl < 1.4:
            P.wait("gpsimd", "s_se", P.val("s_se"))
            continue
        v_minq = emit("vector", lambda e: e.tensor_scalar(t_scr1[:], psA[:], 0.0, None, OP.min),
                      inc="s_ve", waits=[("s_mm", v_mm_q), ("s_se", v_qcopy)])
        if _lvl < 1.6:
            P.wait("gpsimd", "s_se", P.val("s_se"))
            P.wait("gpsimd", "s_ve", P.val("s_ve"))
            continue
        v_exq = emit("scalar", lambda e: e.activation(t_scr2[:], t_scr1[:], AF.Exp),
                     inc="s_se", waits=[("s_ve", v_minq)])
        if _lvl < 2:
            P.wait("gpsimd", "s_se", P.val("s_se"))
            P.wait("gpsimd", "s_ve", P.val("s_ve"))
            continue
        v_evac_q = emit("vector", lambda e: e.scalar_tensor_tensor(
            sq_bf.ap().rearrange("p h t -> p (h t)"), psA[:], 0.0, t_scr2[:], OP.max, OP.add),
            inc="s_ve", waits=[("s_se", v_exq)])

        if _lvl < 3:
            P.wait("gpsimd", "s_ve", P.val("s_ve"))
            P.wait("gpsimd", "s_se", P.val("s_se"))
            continue
        for tc in range(4):
            def vgroup(e, tc=tc, sp=sp):
                for d in range(8):
                    e.matmul(psA[:, tc * 512:(tc + 1) * 512],
                             x_fm[:, sp, d, tc, :],
                             wv_bf[:, d, :], start=(d == 0), stop=False)
                return e.matmul(psA[:, tc * 512:(tc + 1) * 512], ones_r[0:1, 0:128],
                                bv_bf[:], start=False, stop=True)
            emit("tensor", vgroup, inc="s_mm",
                 waits=[("s_ve", v_evac_q), ("s_se", v_exq)] if tc == 0 else ())
        v_mm_v = P.val("s_mm")
        v_mmv_hist[sp] = v_mm_v

        if _lvl < 4:
            P.wait("gpsimd", "s_mm", P.val("s_mm"))
            continue
        v_kcopy = emit("scalar", lambda e: e.activation(k_bf.ap().rearrange("p h t -> p (h t)"), psB[:],
                                                        AF.Copy, scale=1.0),
                       inc="s_se", waits=[("s_mm", v_mm_k)])
        v_mink = emit("vector", lambda e: e.tensor_scalar(t_scr1[:], psB[:], 0.0, None, OP.min),
                      inc="s_ve", waits=[("s_mm", v_mm_k), ("s_se", v_kcopy)])
        v_exk = emit("scalar", lambda e: e.activation(t_scr2[:], t_scr1[:], AF.Exp),
                     inc="s_se", waits=[("s_ve", v_mink)])
        v_evac_k = emit("vector", lambda e: e.scalar_tensor_tensor(
            sk_bf.ap().rearrange("p h t -> p (h t)"), psB[:], 0.0, t_scr2[:], OP.max, OP.add),
            inc="s_ve", waits=[("s_se", v_exk)])

        if _lvl < 5:
            P.wait("gpsimd", "s_ve", P.val("s_ve"))
            P.wait("gpsimd", "s_se", P.val("s_se"))
            continue
        for tc in range(4):
            emit("vector", lambda e, tc=tc: e.tensor_copy(
                out=v_bf[:, tc, :, 0:128],
                in_=psA.ap().rearrange("p (tc h f) -> p tc h f", tc=4, h=HPC)[:, tc, :, :]),
                inc="s_ve" if tc == 3 else None,
                waits=[("s_mm", v_mm_v)] if tc == 0 else ())
        v_evac_v = P.val("s_ve")

        if _lvl < 6:
            P.wait("gpsimd", "s_ve", P.val("s_ve"))
            continue
        for h in range(HPC):
            emit("sync", lambda e, h=h: e.dma_start(
                out=sk_tm[:, h, :, :], in_=sk_bf[:, h, :], transpose=True),
                inc="s_tp", n=16, waits=[("s_ve", v_evac_k)])
        v_sktm = P.val("s_tp")

        if debug and s == 0:
            emit("gpsimd", lambda e: e.dma_start(out=dbg["qT"], in_=q_bf[:]), inc="s_st", n=16,
                 waits=[("s_se", v_exq)])
            emit("gpsimd", lambda e: e.dma_start(out=dbg["sqT"], in_=sq_bf[:]), inc="s_st", n=16,
                 waits=[("s_ve", v_evac_q)])
            emit("gpsimd", lambda e: e.dma_start(out=dbg["skT"], in_=sk_bf[:]), inc="s_st", n=16,
                 waits=[("s_ve", v_evac_k)])
            emit("gpsimd", lambda e: e.dma_start(out=dbg["v"], in_=v_bf[:]), inc="s_st", n=16,
                 waits=[("s_ve", v_evac_v)])

        if seg_detail == "proj":
            P.wait("gpsimd", "s_ve", P.val("s_ve"))
            P.wait("gpsimd", "s_tp", P.val("s_tp"))
            continue
        for h in range(HPC):
            def numq_group(e, h=h):
                for qc in range(4):
                    inst = e.matmul(psA[:, qc * 256:qc * 256 + 130],
                                    sq_bf[:, h, qc * 128:(qc + 1) * 128],
                                    M_bf[:, h, :], start=True, stop=True)
                return inst
            emit("tensor", numq_group, inc="s_mm",
                 waits=[("s_ve", max(v_evac_q, v_evac_v, v_Mbf[h])),
                        ("s_se", v_am_prev)])

            def numk_group(e, h=h):
                for kc in range(4):
                    inst = e.matmul(psA[:, 1024 + kc * 256:1024 + kc * 256 + 130],
                                    sk_bf[:, h, kc * 128:(kc + 1) * 128],
                                    M_bf[:, h, :], start=True, stop=True)
                return inst
            v_numk = emit("tensor", numk_group, inc="s_mm", waits=[("s_ve", v_evac_k)])

            v_dense = emit("scalar", lambda e: e.add(
                t_den[:], psA.ap().rearrange("p (c u) -> p c u", u=256)[:, :, 128], EPS_ATT),
                inc="s_se", waits=[("s_mm", v_numk), ("s_ve", max(v_Mbf))])
            if s == 1 and h == 0 and isinstance(debug, str) and debug == "cap2":
                emit("vector", lambda e: e.tensor_copy(out=t_scr1[:, 1024:1032], in_=t_den[:]),
                     inc="s_ve", waits=[("s_se", v_dense)])
                emit("vector", lambda e: e.tensor_copy(out=t_scr1[:, 1032:1162], in_=M_bf[:, 0, :]))
                emit("vector", lambda e: e.tensor_copy(out=t_scr1[:, 1162:1292], in_=psA[:, 1024:1154]))
                emit("vector", lambda e: e.tensor_copy(out=t_scr1[:, 1300:1820],
                                                       in_=M_bf.ap().rearrange("p h c -> p (h c)")))
                emit("vector", lambda e: e.tensor_copy(out=t_scr1[:, 1820:1950],
                                                       in_=M_f32[:, 0, :]))
            v_rq = emit("vector", lambda e: e.reciprocal(out=t_rq[:], in_=t_den[:]),
                        waits=[("s_se", v_dense)], inc="s_ve")
            emit("vector", lambda e, h=h: e.tensor_scalar(t_rqg[:], t_rq[:, 0:4], g_bc[:, h:h + 1], None, OP.mult),
                 waits=[("s_ex", v_gbc), ("s_ve", v_rq)])
            v_rkn = emit("vector", lambda e: e.tensor_scalar(t_rkn[:], t_rq[:, 4:8], -1.0, None, OP.mult),
                         inc="s_ve")
            if s == 1 and h == 0 and isinstance(debug, str) and debug == "cap4":
                emit("vector", lambda e: e.tensor_copy(out=t_scr1[:, 1024:1032], in_=t_den[:]))
                emit("vector", lambda e: e.tensor_copy(out=t_scr1[:, 1032:1040], in_=t_rq[:]))
                emit("vector", lambda e: e.tensor_copy(out=t_scr1[:, 1040:1044], in_=t_rkn[:]))
                emit("vector", lambda e: e.tensor_copy(out=t_scr1[:, 1044:1048], in_=t_rqg[:]))
            for kc in range(4):
                emit("vector", lambda e, kc=kc, h=h: e.scalar_tensor_tensor(
                    vd_bf[:, kc, 0:128], psA[:, 1024 + kc * 256:1024 + kc * 256 + 128],
                    t_rkn[:, kc:kc + 1], v_bf[:, kc, h, 0:128], OP.mult, OP.add),
                    inc="s_ve" if kc == 3 else None,
                    waits=[("s_ve", v_rkn)] if kc == 0 else ())
            v_vd = P.val("s_ve")

            if s == 1 and h == 0 and isinstance(debug, str) and debug == "cap4":
                emit("vector", lambda e: e.tensor_copy(out=t_scr1[:, 1048:1178], in_=vd_bf[:, 0, :]))
                emit("vector", lambda e: e.tensor_copy(out=t_scr1[:, 1178:1308], in_=vd_bf[:, 3, :]))

            def mup_group(e, h=h):
                for kc in range(4):
                    inst = e.matmul(psA[:, 1024:1154], sk_tm[:, h, kc, :],
                                    vd_bf[:, kc, :], start=(kc == 0), stop=(kc == 3))
                return inst
            v_mup = emit("tensor", mup_group, inc="s_mm",
                         waits=[("s_ve", v_vd), ("s_tp", v_sktm)])
            if s == 0 and h == 0 and isinstance(debug, str) and debug == "cap":
                emit("vector", lambda e: e.tensor_copy(out=t_scr1[:, 1900:2030], in_=psA[:, 1024:1154]),
                     inc="s_ve", waits=[("s_mm", v_mup)])
                emit("vector", lambda e: e.tensor_copy(out=t_scr1[:, 1380:1900],
                                                       in_=vd_bf.ap().rearrange("p a b -> p (a b)")))
                emit("vector", lambda e: e.tensor_copy(out=t_scr1[:, 868:1380],
                                                       in_=sk_tm[:, 0, :, :].rearrange("p a b -> p (a b)")))
            if s == 1 and h == 0 and isinstance(debug, str) and debug == "cap3":
                emit("vector", lambda e: e.tensor_copy(out=t_scr1[:, 1024:1154], in_=psA[:, 1024:1154]),
                     waits=[("s_mm", v_mup)])
            emit("vector", lambda e, h=h: e.tensor_tensor(M_f32[:, h, :], M_f32[:, h, :],
                                                          psA[:, 1024:1154], OP.add),
                 waits=[("s_mm", v_mup)])
            v_Mbf[h] = emit("vector", lambda e, h=h: e.tensor_copy(out=M_bf[:, h, :], in_=M_f32[:, h, :]),
                            inc="s_ve")
            if s == 1 and isinstance(debug, str) and debug == "cap3":
                off = 1160 + 140 * h
                emit("vector", lambda e, off=off: e.tensor_copy(out=t_scr1[:, off:off + 130],
                                                                in_=M_f32[:, 0, :]))

            for qc in range(4):
                nk = (qc + 1) * 128
                boff = (qc % 2) * 512
                w = []
                if h == 0 and qc == 0:
                    w = [("s_se", v_exk), ("s_ve", v_evac_k)]
                elif qc >= 2:
                    w = [("s_se", P.val("s_se"))]
                v_sc = emit("tensor", lambda e, h=h, qc=qc, nk=nk, boff=boff: e.matmul(
                    psB[:, boff:boff + nk], q_bf[:, h, qc * 128:(qc + 1) * 128],
                    k_bf[:, h, 0:nk], start=True, stop=True),
                    inc="s_mm", waits=w)
                v_nm = emit("vector", lambda e, qc=qc, nk=nk, boff=boff: e.tensor_reduce(
                    t_nm[:, qc:qc + 1], psB[:, boff:boff + nk], AX.X, OP.max, negate=True),
                    inc="s_ve", waits=[("s_mm", v_sc)])
                v_exp = emit("scalar", lambda e, qc=qc, nk=nk, boff=boff: e.activation(
                    p_bf[:, qc, 0:nk], psB[:, boff:boff + nk], AF.Exp,
                    bias=t_nm[:, qc:qc + 1]),
                    inc="s_se", waits=[("s_ve", v_nm)])
                v_msk = emit("vector", lambda e, qc=qc: e.tensor_tensor(
                    p_bf[:, qc, qc * 128:(qc + 1) * 128], p_bf[:, qc, qc * 128:(qc + 1) * 128],
                    tril_bf[:], OP.mult), inc="s_ve", waits=[("s_se", v_exp)])
                emit("sync", lambda e, qc=qc, nk=nk: e.dma_start(
                    out=pT_bf[:, qc, 0:qc + 1, :],
                    in_=p_bf[:, qc, 0:nk], transpose=True),
                    inc="s_tp", n=16, waits=[("s_ve", v_msk)])
                v_pT_qc[qc] = P.val("s_tp")

            for qc in range(4):
                def pv_group(e, h=h, qc=qc):
                    for kc in range(qc + 1):
                        inst = e.matmul(psB[:, 1024 + qc * 256:1024 + qc * 256 + 130],
                                        pT_bf[:, qc, kc, :],
                                        v_bf[:, kc, h, :], start=(kc == 0), stop=(kc == qc))
                    return inst
                emit("tensor", pv_group, inc="s_mm",
                     waits=[("s_tp", v_pT_qc[qc])])
            v_pv = P.val("s_mm")

            v_rd = emit("vector", lambda e: e.reciprocal(
                out=t_rd[:], in_=psB.ap().rearrange("p (c u) -> p c u", u=256)[:, 4:8, 128]),
                waits=[("s_mm", v_pv)], inc="s_ve")
            v_r2 = emit("vector", lambda e, h=h: e.tensor_scalar(t_r2g[:], t_rd[:], gm1_bc[:, h:h + 1], None, OP.mult),
                        inc="s_ve", waits=[("s_ve", max(v_gm1, v_rd))])
            for qc in range(4):
                emit("scalar", lambda e, qc=qc: e.activation(
                    t_scr2[:, qc * 128:(qc + 1) * 128], psA[:, qc * 256:qc * 256 + 128],
                    AF.Identity, scale=t_rqg[:, qc:qc + 1]),
                    waits=[("s_ve", v_r2)] if qc == 0 else (),
                    inc="s_se" if qc == 3 else None)
            v_am = P.val("s_se")
            v_am_prev = v_am
            for qc in range(4):
                emit("vector", lambda e, h=h, qc=qc, sp=sp: e.scalar_tensor_tensor(
                    a_stage[:, sp, qc, h * 128:(h + 1) * 128],
                    psB[:, 1024 + qc * 256:1024 + qc * 256 + 128],
                    t_r2g[:, qc:qc + 1], t_scr2[:, qc * 128:(qc + 1) * 128], OP.mult, OP.add),
                    waits=([("s_ve", v_r2)] if qc == 0 else []) +
                          [("s_se", v_am), ("s_ld", v_axp_hist[sp])],
                    inc="s_ve" if qc == 3 else None)
            v_comb_prev = P.val("s_ve")
            if s == NSEG - 1 and h == 0 and isinstance(debug, str) and debug == "cap6":
                emit("vector", lambda e, sp=sp: e.tensor_copy(
                    out=t_scr1[:, 1024:1536].rearrange("p (q f) -> p q f", q=4),
                    in_=a_stage[:, sp, :, 0:128]))
                emit("vector", lambda e: e.tensor_copy(out=t_scr1[:, 1536:1540], in_=t_r2g[:]))
                emit("vector", lambda e: e.tensor_copy(out=t_scr1[:, 1540:1544], in_=t_rqg[:]))
                emit("vector", lambda e: e.tensor_copy(out=t_scr1[:, 1544:1674],
                                                       in_=psB[:, 1024:1154]))
                emit("vector", lambda e: e.tensor_copy(out=t_scr1[:, 1674:1804],
                                                       in_=psB[:, 1792:1922]))
                emit("vector", lambda e: e.tensor_copy(out=t_scr1[:, 1804:1932],
                                                       in_=t_scr2[:, 384:512]))

            if debug and s == 0 and h == 0:
                emit("gpsimd", lambda e: e.dma_start(out=dbg["p"], in_=p_bf[:]), inc="s_st", n=16,
                     waits=[("s_ve", v_comb_prev)])

        v_head_vec_prev = P.val("s_ve")

        for qc in range(4):
            emit("sync", lambda e, s=s, sp=sp, qc=qc: e.dma_start(
                out=ax_priv[s, qc], in_=a_stage[:, sp, qc, :]),
                inc="s_ld", n=16, waits=[("s_ve", v_head_vec_prev)] if qc == 0 else ())
        v_axp = P.val("s_ld")
        v_axp_hist[sp] = v_axp

        def astore(e, s=s):
            pc = g_pcol(e)
            return e.dma_start(
                out=a_xchg[s * 512:(s + 1) * 512, ds(pc, 512)],
                in_=ax_priv[s].rearrange("qc p f -> (qc p) f"))
        emit("gpsimd", astore, inc="s_st", n=16, waits=[("s_ld", v_axp)])
        v_astore_hist[sp] = P.val("s_st")

        if debug and s == 0:
            emit("gpsimd", lambda e: e.dma_start(out=dbg["astage"], in_=a_stage[:, 0, :, :]),
                 inc="s_st", n=16, waits=[("s_ve", v_head_vec_prev)])
            v_astore_hist[sp] = P.val("s_st")

    if debug:
        emit("gpsimd", lambda e: e.dma_start(out=dbg["M"], in_=M_f32[:]), inc="s_st", n=16,
             waits=[("s_ve", max(v_Mbf))])

    v_all_store = P.val("s_st")
    v_barw = emit("gpsimd", lambda e: e.dma_start(out=bar_in.ap(), in_=sgc[0:1, 0:8]),
                  inc="s_ex", n=16, waits=[("s_st", v_all_store)])
    emit("gpsimd", lambda e: e.collective_compute(
        "AllReduce", OP.add, replica_groups=[list(range(8))],
        ins=[bar_in.ap().opt()], outs=[bar_out.ap().opt()]),
        inc="s_cc", waits=[("s_ex", v_barw)])
    V_BAR = P.val("s_cc")

    def a_copy(e):
        pid = e.partition_id()
        off = e.snap((pid % 2) * TPC, min_val=0, max_val=TPC)
        return e.dma_start(out=a_loc, in_=a_xchg[ds(off, TPC), :])
    V_ACP = emit("gpsimd", a_copy, inc="s_gld", n=16, waits=[("s_cc", V_BAR)])

    if ph_dump == "vbuf":
        tmp_f = nc.sbuf_tensor("dumpV_f", [128, 2080], F32)
        with tmp_f as tf:
            v_c = emit("vector", lambda e: e.tensor_copy(
                out=tf[:], in_=v_bf.ap().rearrange("p a b c -> p (a b c)")),
                inc="s_ve", waits=[("s_ve", P.val("s_ve")), ("s_se", P.val("s_se")),
                                   ("s_mm", P.val("s_mm"))])
            emit("gpsimd", lambda e: e.dma_start(out=y_out[0:128, 0:1024], in_=tf[:, 0:1024]),
                 inc="s_st", n=16, waits=[("s_ve", v_c)])
            emit("gpsimd", lambda e: e.dma_start(out=y_out[128:256, 0:1024], in_=tf[:, 1024:2048]),
                 inc="s_st", n=16)
            v_d3 = emit("gpsimd", lambda e: e.dma_start(out=y_out[256:384, 0:32], in_=tf[:, 2048:2080]),
                 inc="s_st", n=16)
            v_c2 = emit("vector", lambda e: e.tensor_copy(
                out=tf[:, 0:520], in_=vd_bf.ap().rearrange("p a b -> p (a b)")),
                inc="s_ve", waits=[("s_st", v_d3)])
            emit("gpsimd", lambda e: e.dma_start(out=y_out[384:512, 0:520], in_=tf[:, 0:520]),
                 inc="s_st", n=16, waits=[("s_ve", v_c2)])
        P.wait("gpsimd", "s_st", P.val("s_st"))
        return _finalize(nc, P, dbg)

    if ph_dump == "M":
        tmp_f = nc.sbuf_tensor("dumpM_f", [128, HPC, 130], F32)
        with tmp_f as tf:
            v_c = emit("vector", lambda e: e.tensor_copy(
                out=tf.ap().rearrange("p h c -> p (h c)"),
                in_=M_f32.ap().rearrange("p h c -> p (h c)")),
                inc="s_ve", waits=[("s_ve", max(v_Mbf))])
            v_d1 = emit("gpsimd", lambda e: e.dma_start(
                out=y_out[0:128, 0:HPC * 130], in_=tf.ap().rearrange("p h c -> p (h c)")),
                inc="s_st", n=16, waits=[("s_ve", v_c)])
            v_c2 = emit("vector", lambda e: e.tensor_copy(
                out=tf.ap().rearrange("p h c -> p (h c)")[:, 0:512],
                in_=a_stage[:, (NSEG - 1) % 2, 0, :]), inc="s_ve",
                waits=[("s_st", v_d1)])
            emit("gpsimd", lambda e: e.dma_start(
                out=y_out[128:256, 0:512], in_=tf.ap().rearrange("p h c -> p (h c)")[:, 0:512]),
                inc="s_st", n=16, waits=[("s_ve", v_c2)])
            if isinstance(debug, str) and debug == "cap":
                emit("gpsimd", lambda e: e.dma_start(out=y_out[256:384, 0:1024], in_=t_scr1[:, 868:1892]),
                     inc="s_st", n=16)
                emit("gpsimd", lambda e: e.dma_start(out=y_out[384:512, 0:138], in_=t_scr1[:, 1892:2030]),
                     inc="s_st", n=16)
            if isinstance(debug, str) and debug == "cap2":
                emit("gpsimd", lambda e: e.dma_start(out=y_out[256:384, 0:268], in_=t_scr1[:, 1024:1292]),
                     inc="s_st", n=16)
                emit("gpsimd", lambda e: e.dma_start(out=y_out[384:512, 0:650], in_=t_scr1[:, 1300:1950]),
                     inc="s_st", n=16)
            if isinstance(debug, str) and debug in ("cap3", "cap4"):
                emit("gpsimd", lambda e: e.dma_start(out=y_out[256:384, 0:696], in_=t_scr1[:, 1024:1720]),
                     inc="s_st", n=16)
            if isinstance(debug, str) and debug == "cap6":
                emit("gpsimd", lambda e: e.dma_start(out=y_out[256:384, 0:908], in_=t_scr1[:, 1024:1932]),
                     inc="s_st", n=16)
            if isinstance(debug, str) and debug == "cap5":
                v_c5 = emit("vector", lambda e: e.tensor_copy(
                    out=t_scr1[:], in_=a_stage[:, (NSEG - 1) % 2, :, :].rearrange("p q f -> p (q f)")),
                    inc="s_ve", waits=[("s_st", P.val("s_st"))])
                emit("gpsimd", lambda e: e.dma_start(out=y_out[256:384, 0:1024], in_=t_scr1[:, 0:1024]),
                     inc="s_st", n=16, waits=[("s_ve", v_c5)])
                emit("gpsimd", lambda e: e.dma_start(out=y_out[384:512, 0:1024], in_=t_scr1[:, 1024:2048]),
                     inc="s_st", n=16)
                v_c8 = emit("vector", lambda e: e.tensor_copy(out=t_scr1[:, 1024:1536], in_=t_scr2[:, 0:512]),
                            inc="s_ve", waits=[("s_st", P.val("s_st"))])
                emit("vector", lambda e: e.tensor_copy(out=t_scr1[:, 2040:2044], in_=t_nm[:]))
                emit("vector", lambda e: e.tensor_copy(out=t_scr1[:, 2044:2048], in_=t_rqg[:]))
                emit("vector", lambda e: e.tensor_copy(out=t_scr1[:, 2036:2040], in_=t_r2g[:]))
                v_c9 = emit("vector", lambda e: e.tensor_copy(
                    out=t_scr2[:, 0:1024], in_=psB[:, 1024:2048]), inc="s_ve")
                emit("gpsimd", lambda e: e.dma_start(out=y_out[640:768, 0:1024], in_=t_scr2[:, 0:1024]),
                     inc="s_st", n=16, waits=[("s_ve", v_c9)])
                emit("gpsimd", lambda e: e.dma_start(out=y_out[768:896, 0:512], in_=t_scr1[:, 1024:1536]),
                     inc="s_st", n=16)
                emit("gpsimd", lambda e: e.dma_start(out=y_out[896:1024, 0:12], in_=t_scr1[:, 2036:2048]),
                     inc="s_st", n=16)
                with nc.sbuf_tensor("dump_axp", [128, 2, 512], BF16) as axp_t:
                    v_c6 = emit("gpsimd", lambda e: e.dma_start(out=axp_t[:, 0, :], in_=ax_priv[NSEG - 1, 1]),
                                inc="s_gld", n=16)
                    emit("gpsimd", lambda e: e.dma_start(out=axp_t[:, 1, :], in_=ax_priv[NSEG - 1, 3]),
                         inc="s_gld", n=16)
                    v_c7 = emit("vector", lambda e: e.tensor_copy(
                        out=t_scr1[:, 0:1024], in_=axp_t.ap().rearrange("p a b -> p (a b)")),
                        inc="s_ve", waits=[("s_gld", P.val("s_gld")), ("s_st", P.val("s_st"))])
                    emit("gpsimd", lambda e: e.dma_start(out=y_out[512:640, 0:1024], in_=t_scr1[:, 0:1024]),
                         inc="s_st", n=16, waits=[("s_ve", v_c7)])
        P.wait("gpsimd", "s_st", P.val("s_st"))
        return _finalize(nc, P, dbg)

    if ph_dump in ("aloc", "ax"):
        dump_src = a_loc if ph_dump == "aloc" else a_xchg
        tmp_bf = nc.sbuf_tensor("dump_bf", [128, 1024], BF16)
        tmp_f = nc.sbuf_tensor("dump_f", [128, 2, 1024], F32)
        with tmp_bf as tbf, tmp_f as tf:
            v_pv = 0
            for t in range(NTILE):
                tb = t % 2
                v_l = emit("gpsimd", lambda e, t=t: e.dma_start(
                    out=tbf[:], in_=dump_src[t * 128:(t + 1) * 128, :]),
                    inc="s_gld", n=16, waits=[("s_gld", V_ACP), ("s_ve", v_pv)])
                v_c = emit("vector", lambda e, tb=tb: e.tensor_copy(out=tf[:, tb, :], in_=tbf[:]),
                           inc="s_ve", waits=[("s_gld", v_l), ("s_st", v_pv_st if t >= 2 else 0)])
                v_s = emit("gpsimd", lambda e, t=t, tb=tb: e.dma_start(
                    out=y_out[t * 128:(t + 1) * 128, :], in_=tf[:, tb, :]),
                    inc="s_st", n=16, waits=[("s_ve", v_c)])
                v_pv = v_c
                v_pv_st = P.val("s_st")
        P.wait("gpsimd", "s_st", P.val("s_st"))
        return _finalize(nc, P, dbg)

    es1.close()
    sb2p, sb2a = mk_sb(es2p), mk_sb(es2a)
    x1_sb = sb2p("x1_sb", [128, NTILE, 1024], BF16)
    x2_sb = sb2a("x2_sb", [128, 2, 1024])
    a2_sb = sb2a("a2_sb", [128, 2, 1024], BF16)
    r_sb = sb2a("r_sb", [128, 2, 1024])
    rsq_sb = sb2a("rsq_sb", [128, 1024])
    xq_bf = sb2a("xq_bf", [128, 2, 1024], BF16)
    xqT = sb2a("xqT", [128, 8, NTILE, 128], BF16)
    wgt_sb = sb2a("wgt_sb", [128, 2, 8, 128], BF16)
    wut_sb = sb2a("wut_sb", [128, 2, 8, 128], BF16)
    sg_sb = sb2a("sg_sb", [128, 512], BF16)
    su_sb = sb2a("su_sb", [128, 512], BF16)
    h_bf = sb2a("h_bf", [128, 4096], BF16)

    v_r_prev = [0, 0]
    v_x1_hist = [0] * NTILE
    v_xqT_tile = [0] * NTILE
    for t in range(NTILE if phases >= 3 else 0):
        tb = t % 2
        v_xl = emit("sync", lambda e, t=t, tb=tb: e.dma_start(
            out=x2_sb[:, tb, :],
            in_=x2_in[t * 128:(t + 1) * 128, :]),
            inc="s_ld", n=16, waits=[("s_cc", V_BAR), ("s_ve", v_r_prev[tb])])
        v_al = emit("gpsimd", lambda e, t=t, tb=tb: e.dma_start(
            out=a2_sb[:, tb, :],
            in_=a_loc[t * 128:(t + 1) * 128, :]),
            inc="s_gld", n=16, waits=[("s_gld", V_ACP), ("s_ve", v_r_prev[tb])])
        emit("vector", lambda e, tb=tb: e.scalar_tensor_tensor(
            r_sb[:, tb, :], x2_sb[:, tb, :], 0.0, a2_sb[:, tb, :], OP.add, OP.add,
            accum_out=st2[:, 0:1]),
            waits=[("s_ld", v_xl), ("s_gld", v_al)])
        v_acc = emit("vector", lambda e, tb=tb: e.scalar_tensor_tensor(
            rsq_sb[:], r_sb[:, tb, :], 0.0, r_sb[:, tb, :], OP.add, OP.mult,
            accum_out=st2[:, 1:2]), inc="s_ve")
        v1 = emit("vector", lambda e: e.tensor_scalar(st2[:, 0:1], st2[:, 0:1], float(1.0 / D), None, OP.mult),
                  waits=[("s_ve", v_acc)], inc="s_ve")
        v2 = emit("vector", lambda e: e.tensor_tensor(st2[:, 3:4], st2[:, 0:1], st2[:, 0:1], OP.mult),
                  waits=[("s_ve", v1)], inc="s_ve")
        v3 = emit("vector", lambda e: e.scalar_tensor_tensor(st2[:, 2:3], st2[:, 1:2], float(1.0 / D),
                                                             st2[:, 3:4], OP.mult, OP.subtract),
                  waits=[("s_ve", v2)], inc="s_ve")
        v4 = emit("vector", lambda e: e.tensor_scalar(st2[:, 2:3], st2[:, 2:3], EPS_LN, None, OP.add),
                  waits=[("s_ve", v3)], inc="s_ve")
        v_var = emit("vector", lambda e: e.reciprocal(out=st2[:, 2:3], in_=st2[:, 2:3]), inc="s_ve",
                     waits=[("s_ve", v4)])
        v_rst = emit("scalar", lambda e: e.activation(t_rstd[:, 0:1], st2[:, 2:3], AF.Sqrt),
                     inc="s_se", waits=[("s_ve", v_var)])
        v_bias = emit("vector", lambda e: e.tensor_scalar(t_rstd[:, 1:2], st2[:, 0:1], t_rstd[:, 0:1],
                                                          -1.0, OP.mult, OP.mult),
                      inc="s_ve", waits=[("s_se", v_rst)])
        v_x1t = emit("scalar", lambda e, t=t, tb=tb: e.activation(
            x1_sb[:, t, :], r_sb[:, tb, :], AF.Identity,
            bias=t_rstd[:, 1:2], scale=t_rstd[:, 0:1]),
            inc="s_se", waits=[("s_se", v_rst), ("s_ve", v_bias)])
        v_x1_hist[t] = v_x1t
        emit("vector", lambda e, t=t: e.tensor_scalar(
            rsq_sb[:], x1_sb[:, t, :], 1.0, None, OP.mult, OP.max,
            accum_out=xmx[:, 2 * t:2 * t + 1]), waits=[("s_se", v_x1t)])
        emit("vector", lambda e, t=t: e.tensor_scalar(
            rsq_sb[:], x1_sb[:, t, :], -1.0, None, OP.mult, OP.max,
            accum_out=xmx[:, 2 * t + 1:2 * t + 2]), inc="s_ve")
        v_r_prev[tb] = P.val("s_ve")
        if debug and t == 0:
            emit("gpsimd", lambda e: e.dma_start(out=dbg["r0"], in_=r_sb[:, 0, :]), inc="s_st", n=16,
                 waits=[("s_ve", v_r_prev[0])])

    if phases < 3:
        P.wait("gpsimd", "s_cc", P.val("s_cc"))
        P.wait("gpsimd", "s_gld", P.val("s_gld"))
        P.wait("gpsimd", "s_st", P.val("s_st"))
        return _finalize(nc, P, dbg)
    v_xm = emit("vector", lambda e: e.tensor_reduce(xmx[:, 0:1], xmx[:], AX.X, OP.max), inc="s_ve",
                waits=[("s_ve", P.val("s_ve"))])
    v_sw = emit("gpsimd", lambda e: e.dma_start(
        out=scr[0:1, 64:192].rearrange("o (p u) -> p (o u)", p=128), in_=xmx[:, 0:1]),
        inc="s_ex", n=16, waits=[("s_ve", v_xm)])
    v_sr = emit("gpsimd", lambda e: e.dma_start(out=row128[:], in_=scr[0:1, 64:192]),
                inc="s_gld", n=16, waits=[("s_ex", v_sw)])
    vl1 = emit("vector", lambda e: e.tensor_reduce(lmax[0:1, 0:1], row128[:], AX.X, OP.max),
               waits=[("s_gld", v_sr)], inc="s_ve")
    vl2 = emit("vector", lambda e: e.tensor_copy(out=lmax[0:1, 1:2], in_=lmax[0:1, 0:1]),
               waits=[("s_ve", vl1)], inc="s_ve")
    vl3 = emit("vector", lambda e: e.tensor_copy(out=lmax[0:1, 2:4], in_=lmax[0:1, 0:2]),
               waits=[("s_ve", vl2)], inc="s_ve")
    v_lm = emit("vector", lambda e: e.tensor_copy(out=lmax[0:1, 4:8], in_=lmax[0:1, 0:4]), inc="s_ve",
                waits=[("s_ve", vl3)])
    v_mxw = emit("gpsimd", lambda e: e.dma_start(out=mx_in.ap(), in_=lmax[:]),
                 inc="s_ex", n=16, waits=[("s_ve", v_lm)])
    emit("gpsimd", lambda e: e.collective_compute(
        "AllReduce", OP.max, replica_groups=[list(range(8))],
        ins=[mx_in.ap().opt()], outs=[mx_out.ap().opt()]),
        inc="s_cc", waits=[("s_ex", v_mxw)])
    V_MX = P.val("s_cc")
    v_gm = emit("gpsimd", lambda e: e.dma_start(out=lmax[0:1, 0:1], in_=mx_out.ap()[0:1, 0:1]),
                inc="s_gld", n=16, waits=[("s_cc", V_MX)])
    if debug:
        emit("gpsimd", lambda e: e.dma_start(out=dbg["gmax"], in_=lmax[:]), inc="s_st", n=16,
             waits=[("s_gld", v_gm)])
    vsx1 = emit("vector", lambda e: e.tensor_scalar(lmax[0:1, 1:2], lmax[0:1, 0:1], sgc[0:1, 6:7],
                                                    INV127, OP.mult, OP.mult),
                waits=[("s_gld", v_gm)], inc="s_ve")
    v_sx = emit("vector", lambda e: e.reciprocal(out=lmax[0:1, 2:3], in_=lmax[0:1, 1:2]), inc="s_ve",
                waits=[("s_ve", vsx1)])
    v_sxw = emit("gpsimd", lambda e: e.dma_start(out=scr[0:1, 8:10], in_=lmax[0:1, 1:3]),
                 inc="s_ex", n=16, waits=[("s_ve", v_sx)])
    emit("gpsimd", lambda e: e.dma_start(out=sx_bc[:], in_=scr[0:1, 8:9].broadcast_to([128, 1])),
         inc="s_gld", n=16, waits=[("s_ex", v_sxw)])
    v_sxb = emit("gpsimd", lambda e: e.dma_start(out=rx_bc[:], in_=scr[0:1, 9:10].broadcast_to([128, 1])),
                 inc="s_gld", n=16)

    v_tp_hist2 = [0, 0]
    for t in range(NTILE):
        tb = t % 2
        emit("vector", lambda e, t=t: e.tensor_scalar(rsq_sb[:], x1_sb[:, t, :], rx_bc[:], MAGIC, OP.mult, OP.add),
             waits=[("s_gld", v_sxb), ("s_tp", v_tp_hist2[tb]), ("s_se", v_x1_hist[t])])
        emit("vector", lambda e: e.tensor_scalar(rsq_sb[:], rsq_sb[:], MAGIC + 127.0, MAGIC - 128.0, OP.min, OP.max))
        v_q2 = emit("vector", lambda e, tb=tb: e.tensor_scalar(xq_bf[:, tb, :], rsq_sb[:], MAGIC, None, OP.subtract),
                    inc="s_ve")
        for dc in range(8):
            emit("sync", lambda e, t=t, tb=tb, dc=dc: e.dma_start(
                out=xqT[:, dc, t, :],
                in_=xq_bf[:, tb, dc * 128:(dc + 1) * 128], transpose=True),
                inc="s_tp", n=16, waits=[("s_ve", v_q2)])
        v_tp_hist2[tb] = P.val("s_tp")
        v_xqT_tile[t] = P.val("s_tp")
    v_xqT = P.val("s_tp")

    if ph_dump == "x1":
        tmp_f = nc.sbuf_tensor("dumpX_f", [128, 2048], F32)
        with tmp_f as tf:
            v_c = emit("vector", lambda e: e.tensor_copy(out=tf[:, 0:1024], in_=x1_sb[:, 0, :]),
                       inc="s_ve", waits=[("s_tp", v_xqT)])
            emit("vector", lambda e: e.tensor_copy(out=tf[:, 1024:2048], in_=x1_sb[:, 31, :]),
                 inc="s_ve")
            v_d1 = emit("gpsimd", lambda e: e.dma_start(out=y_out[0:128, 0:1024], in_=tf[:, 0:1024]),
                        inc="s_st", n=16, waits=[("s_ve", P.val("s_ve"))])
            emit("gpsimd", lambda e: e.dma_start(out=y_out[128:256, 0:1024], in_=tf[:, 1024:2048]),
                 inc="s_st", n=16)
            emit("gpsimd", lambda e: e.dma_start(out=y_out[256:257, 0:8], in_=lmax[0:1, :]),
                 inc="s_st", n=16)
            v_c2 = emit("vector", lambda e: e.tensor_copy(
                out=tf[:, 0:1024].rearrange("p (a bc) -> p a bc", a=2),
                in_=xqT[:, 0:2, 0:4, :].rearrange("p a b c -> p a (b c)")),
                inc="s_ve", waits=[("s_st", v_d1)])
            emit("gpsimd", lambda e: e.dma_start(out=y_out[384:512, 0:1024], in_=tf[:, 0:1024]),
                 inc="s_st", n=16, waits=[("s_ve", v_c2)])
            v_c3 = emit("vector", lambda e: e.tensor_copy(out=tf[:, 1024:2048], in_=r_sb[:, 0, :]),
                        inc="s_ve")
            emit("gpsimd", lambda e: e.dma_start(out=y_out[512:640, 0:1024], in_=tf[:, 1024:2048]),
                 inc="s_st", n=16, waits=[("s_ve", v_c3)])
            v_c4 = emit("vector", lambda e: e.tensor_copy(out=tf[:, 0:2], in_=t_rstd[:]), inc="s_ve",
                        waits=[("s_st", P.val("s_st"))])
            emit("vector", lambda e: e.tensor_copy(out=tf[:, 2:6], in_=st2[:]))
            v_c5 = emit("vector", lambda e: e.tensor_copy(out=tf[:, 6:70], in_=xmx[:]), inc="s_ve")
            emit("gpsimd", lambda e: e.dma_start(out=y_out[640:768, 0:70], in_=tf[:, 0:70]),
                 inc="s_st", n=16, waits=[("s_ve", v_c5)])
        P.wait("gpsimd", "s_st", P.val("s_st"))
        return _finalize(nc, P, dbg)

    chunk_slots = []
    for sslot in range(PREP_SLOTS):
        for r in range(8):
            ci = r + 8 * sslot
            if ci < NCH:
                chunk_slots.append((ci, r, sslot))
    chunk_slots.sort()

    if phases < 4:
        P.wait("gpsimd", "s_cc", P.val("s_cc"))
        P.wait("gpsimd", "s_tp", P.val("s_tp"))
        P.wait("gpsimd", "s_ve", P.val("s_ve"))
        return _finalize(nc, P, dbg)
    v_spill_prev = 0
    v_gse_slot = [0, 0]
    v_gu_hist = [0, 0]
    v_httail_prev = 0
    for idx, (ci, r_, s_) in enumerate(chunk_slots):
        cb = idx % 2
        grow = r_ * CCR + s_ * 1024
        urow = r_ * CCR + (PREP_SLOTS + s_) * 1024
        emit("sync", lambda e, grow=grow, cb=cb: e.dma_start(
            out=wgt_sb[:, cb, :, :],
            in_=cc_out.ap()[grow:grow + 1024, :].rearrange("(c p) i -> p c i", p=128)),
            inc="s_ld", n=16,
            waits=[("s_cc", max(V_AG, V_BAR)), ("s_mm", v_gu_hist[cb])])
        v_wu = emit("sync", lambda e, urow=urow, cb=cb: e.dma_start(
            out=wut_sb[:, cb, :, :],
            in_=cc_out.ap()[urow:urow + 1024, :].rearrange("(c p) i -> p c i", p=128)),
            inc="s_ld", n=16)
        for tt in range(8):
            goff = (tt % 2) * 1024

            def gu_group(e, cb, tt, goff, which):
                w = wgt_sb if which == 0 else wut_sb
                for hc in range(8):
                    inst = e.matmul(psB[:, goff + which * 512: goff + which * 512 + 512],
                                    w[:, cb, hc, :],
                                    xqT[:, hc, 4 * tt:4 * tt + 4, :].rearrange("p a b -> p (a b)"),
                                    start=(hc == 0), stop=(hc == 7))
                return inst
            emit("tensor", lambda e, cb=cb, tt=tt, goff=goff: gu_group(e, cb, tt, goff, 0),
                 inc="s_mm", waits=[("s_ld", v_wu), ("s_tp", v_xqT_tile[4 * tt + 3]),
                                    ("s_se", v_gse_slot[tt % 2])])
            v_gu = emit("tensor", lambda e, cb=cb, tt=tt, goff=goff: gu_group(e, cb, tt, goff, 1),
                        inc="s_mm")
            silu_w = [("s_mm", v_gu), ("s_gld", v_sxb), ("s_ve", v_httail_prev)]
            if debug and idx == 0 and tt == 1:
                silu_w.append(("s_st", P.val("s_st")))
            emit("scalar", lambda e, goff=goff: e.activation(sg_sb[:], psB[:, goff:goff + 512],
                                                             AF.Silu, scale=sx_bc[:]),
                 waits=silu_w)
            v_gse = emit("scalar", lambda e, goff=goff: e.activation(
                su_sb[:], psB[:, goff + 512:goff + 1024], AF.Identity, scale=sx_bc[:]),
                inc="s_se")
            v_gse_slot[tt % 2] = v_gse
            if debug and idx == 0 and tt == 0:
                v_cap = emit("vector", lambda e: e.tensor_copy(out=rsq_sb[:, 0:512], in_=psB[:, 0:512]),
                             inc="s_ve", waits=[("s_se", v_gse)])
                v_cap2 = emit("vector", lambda e: e.tensor_copy(out=x2_sb[:, 0, 0:512], in_=psB[:, 512:1024]),
                              inc="s_ve")
                emit("gpsimd", lambda e: e.dma_start(out=dbg["gps"], in_=rsq_sb[:, 0:512]),
                     inc="s_st", n=16, waits=[("s_ve", v_cap2)])
                emit("gpsimd", lambda e: e.dma_start(out=dbg["ups"], in_=x2_sb[:, 0, 0:512]),
                     inc="s_st", n=16)
                emit("gpsimd", lambda e: e.dma_start(out=dbg["sg0"], in_=sg_sb[:]),
                     inc="s_st", n=16, waits=[("s_se", v_gse)])
                emit("gpsimd", lambda e: e.dma_start(out=dbg["su0"], in_=su_sb[:]),
                     inc="s_st", n=16)
                emit("gpsimd", lambda e, cb=cb: e.dma_start(out=dbg["wgt0"], in_=wgt_sb[:, cb, :, :]),
                     inc="s_st", n=16)
                emit("gpsimd", lambda e: e.dma_start(out=dbg["xqT0"], in_=xqT[:, :, 0, :]),
                     inc="s_st", n=16)
            col = 2 * (idx * 8 + tt)
            emit("vector", lambda e, tt=tt: e.tensor_tensor(
                h_bf[:, tt * 512:(tt + 1) * 512], sg_sb[:], su_sb[:], OP.mult),
                waits=[("s_se", v_gse), ("s_st", v_spill_prev)])
            emit("vector", lambda e, tt=tt, col=col: e.tensor_scalar(
                rsq_sb[:, 0:512], h_bf[:, tt * 512:(tt + 1) * 512], 1.0, None, OP.mult, OP.max,
                accum_out=hmx[:, col:col + 1]))
            v_httail_prev = emit("vector", lambda e, tt=tt, col=col: e.tensor_scalar(
                rsq_sb[:, 0:512], h_bf[:, tt * 512:(tt + 1) * 512], -1.0, None, OP.mult, OP.max,
                accum_out=hmx[:, col + 1:col + 2]), inc="s_ve")
        v_gu_hist[cb] = P.val("s_mm")
        v_hprev = P.val("s_ve")
        emit("gpsimd", lambda e, ci=ci: e.dma_start(
            out=h_spill[ci, :, :], in_=h_bf[:]),
            inc="s_st", n=16, waits=[("s_ve", v_hprev)])
        v_spill_prev = P.val("s_st")
        if debug and idx == 0:
            emit("gpsimd", lambda e: e.dma_start(out=dbg["h0"], in_=h_bf[:, 0:512]),
                 inc="s_st", n=16, waits=[("s_ve", v_hprev)])
            v_spill_prev = P.val("s_st")
    v_hlast = P.val("s_ve")
    v_gu_end = P.val("s_mm")

    v_hm = emit("vector", lambda e: e.tensor_reduce(hmx[:, 0:1], hmx[:], AX.X, OP.max), inc="s_ve",
                waits=[("s_ve", v_hlast)])
    v_hw = emit("gpsimd", lambda e: e.dma_start(
        out=scr[0:1, 64:192].rearrange("o (p u) -> p (o u)", p=128), in_=hmx[:, 0:1]),
        inc="s_ex", n=16, waits=[("s_ve", v_hm)])
    v_hr = emit("gpsimd", lambda e: e.dma_start(out=row128[:], in_=scr[0:1, 64:192]),
                inc="s_gld", n=16, waits=[("s_ex", v_hw)])
    vh1 = emit("vector", lambda e: e.tensor_reduce(lmax[0:1, 0:1], row128[:], AX.X, OP.max),
               waits=[("s_gld", v_hr)], inc="s_ve")
    vh2 = emit("vector", lambda e: e.tensor_copy(out=lmax[0:1, 1:2], in_=lmax[0:1, 0:1]),
               waits=[("s_ve", vh1)], inc="s_ve")
    vh3 = emit("vector", lambda e: e.tensor_copy(out=lmax[0:1, 2:4], in_=lmax[0:1, 0:2]),
               waits=[("s_ve", vh2)], inc="s_ve")
    v_lh = emit("vector", lambda e: e.tensor_copy(out=lmax[0:1, 4:8], in_=lmax[0:1, 0:4]), inc="s_ve",
                waits=[("s_ve", vh3)])
    v_hmw = emit("gpsimd", lambda e: e.dma_start(out=hm_in.ap(), in_=lmax[:]),
                 inc="s_ex", n=16, waits=[("s_ve", v_lh)])
    emit("gpsimd", lambda e: e.collective_compute(
        "AllReduce", OP.max, replica_groups=[list(range(8))],
        ins=[hm_in.ap().opt()], outs=[hm_out.ap().opt()]),
        inc="s_cc", waits=[("s_ex", v_hmw)])
    V_HM = P.val("s_cc")
    v_ghm = emit("gpsimd", lambda e: e.dma_start(out=lmax[0:1, 0:1], in_=hm_out.ap()[0:1, 0:1]),
                 inc="s_gld", n=16, waits=[("s_cc", V_HM)])
    if debug:
        emit("gpsimd", lambda e: e.dma_start(out=dbg["hmax"], in_=lmax[:]), inc="s_st", n=16,
             waits=[("s_gld", v_ghm)])
    vsh1 = emit("vector", lambda e: e.tensor_scalar(lmax[0:1, 1:2], lmax[0:1, 0:1], sgc[0:1, 7:8],
                                                    INV127, OP.mult, OP.mult),
                waits=[("s_gld", v_ghm)], inc="s_ve")
    v_sh = emit("vector", lambda e: e.reciprocal(out=lmax[0:1, 2:3], in_=lmax[0:1, 1:2]), inc="s_ve",
                waits=[("s_ve", vsh1)])
    v_shw = emit("gpsimd", lambda e: e.dma_start(out=scr[0:1, 10:12], in_=lmax[0:1, 1:3]),
                 inc="s_ex", n=16, waits=[("s_ve", v_sh)])
    emit("gpsimd", lambda e: e.dma_start(out=sh_bc[:], in_=scr[0:1, 10:11].broadcast_to([128, 1])),
         inc="s_gld", n=16, waits=[("s_ex", v_shw)])
    v_shb = emit("gpsimd", lambda e: e.dma_start(out=rh_bc[:], in_=scr[0:1, 11:12].broadcast_to([128, 1])),
                 inc="s_gld", n=16)

    es2a.close()
    sb2b = mk_sb(es2b)
    wdT_sb = sb2b("wdT_sb", [128, NCH, 1024], BF16)
    h2_sb = sb2b("h2_sb", [128, NCH, 512], BF16)
    kh_bf = sb2b("kh_bf", [128, NCH, 512], BF16)
    z_sb = sb2b("z_sb", [128, 1024])
    zsq_sb = sb2b("zsq_sb", [128, 1024])
    y_sb = sb2b("y_sb", [128, 2, 1024])

    ccv = cc_out.ap().rearrange("(r q) j -> r q j", r=8)
    for ci, _, _ in chunk_slots:
        emit("gpsimd", lambda e, ci=ci: e.dma_start(
            out=wdT_sb[:, ci, :].rearrange("p (r j) -> p r j", r=8),
            in_=ccv[:, 6 * 1024 + ci * 128: 6 * 1024 + (ci + 1) * 128, :].transpose([1, 0, 2])),
            inc="s_gld", n=16,
            waits=[("s_cc", V_AG), ("s_mm", v_gu_end), ("s_ve", v_hlast),
                   ("s_se", P.val("s_se"))])
    v_wd = P.val("s_gld")

    v_all_spill = v_spill_prev
    v_yprev = [0, 0]
    v_dn_hist = 0
    v_evac_slot = [0, 0, 0, 0]
    v_yt_prev = 0
    v_quant_prev = 0
    for g in range(8):
        v_hl = emit("gpsimd", lambda e, g=g: e.dma_start(
            out=h2_sb[:],
            in_=h_spill[:, :, g * 512:(g + 1) * 512].transpose([1, 0, 2])),
            inc="s_gld", n=16,
            waits=[("s_st", v_all_spill), ("s_ve", v_quant_prev),
                   ("s_mm", v_gu_end)])
        P.wait("vector", "s_gld", max(v_hl, v_shb))
        P.wait("vector", "s_mm", v_dn_hist)
        for sub in range(11):
            emit("vector", lambda e, sub=sub: e.tensor_scalar(
                zsq_sb[:], h2_sb[:, 2 * sub:2 * sub + 2, :].rearrange("p c t -> p (c t)"),
                rh_bc[:], MAGIC, OP.mult, OP.add))
            emit("vector", lambda e: e.tensor_scalar(
                zsq_sb[:], zsq_sb[:], MAGIC + 127.0, MAGIC - 128.0, OP.min, OP.max))
            last = (sub == 10)
            emit("vector", lambda e, sub=sub: e.tensor_scalar(
                kh_bf[:, 2 * sub:2 * sub + 2, :].rearrange("p c t -> p (c t)"), zsq_sb[:],
                MAGIC, None, OP.subtract), inc="s_ve" if last else None)
        v_quant_prev = P.val("s_ve")
        v_kh = v_quant_prev
        if debug and g == 0:
            emit("gpsimd", lambda e: e.dma_start(out=dbg["kh"], in_=kh_bf[:]),
                 inc="s_st", n=16, waits=[("s_ve", v_kh)])
        for tc in range(4):
            t_glob = g * 4 + tc
            for hh in range(2):
                slot = (tc * 2 + hh) % 4

                def dn_group(e, tc=tc, hh=hh, slot=slot):
                    for ci in range(NCH):
                        inst = e.matmul(psA[:, slot * 512:(slot + 1) * 512],
                                        kh_bf[:, ci, tc * 128:(tc + 1) * 128],
                                        wdT_sb[:, ci, hh * 512:(hh + 1) * 512],
                                        start=(ci == 0), stop=(ci == NCH - 1))
                    return inst
                v_dn = emit("tensor", dn_group, inc="s_mm",
                            waits=[("s_ve", max(v_kh, v_evac_slot[slot])), ("s_gld", v_wd)])
                emit("vector", lambda e, t_glob=t_glob, hh=hh, slot=slot: e.scalar_tensor_tensor(
                    z_sb[:, hh * 512:(hh + 1) * 512], psA[:, slot * 512:(slot + 1) * 512],
                    sh_bc[:], x1_sb[:, t_glob, hh * 512:(hh + 1) * 512], OP.mult, OP.add,
                    accum_out=st2[:, hh:hh + 1]),
                    inc="s_ve",
                    waits=[("s_mm", v_dn), ("s_gld", v_shb), ("s_se", v_yt_prev)])
                v_evac_slot[slot] = P.val("s_ve")
            w1 = emit("vector", lambda e: e.tensor_tensor(st2[:, 0:1], st2[:, 0:1], st2[:, 1:2], OP.add),
                      waits=[("s_ve", P.val("s_ve"))], inc="s_ve")
            v_zacc = emit("vector", lambda e: e.scalar_tensor_tensor(
                zsq_sb[:], z_sb[:], 0.0, z_sb[:], OP.add, OP.mult, accum_out=st2[:, 1:2]),
                inc="s_ve")
            w2 = emit("vector", lambda e: e.tensor_scalar(st2[:, 0:1], st2[:, 0:1], float(1.0 / D), None, OP.mult),
                      waits=[("s_ve", v_zacc)], inc="s_ve")
            w3 = emit("vector", lambda e: e.tensor_tensor(st2[:, 3:4], st2[:, 0:1], st2[:, 0:1], OP.mult),
                      waits=[("s_ve", w2)], inc="s_ve")
            w4 = emit("vector", lambda e: e.scalar_tensor_tensor(st2[:, 2:3], st2[:, 1:2], float(1.0 / D),
                                                                 st2[:, 3:4], OP.mult, OP.subtract),
                      waits=[("s_ve", w3)], inc="s_ve")
            w5 = emit("vector", lambda e: e.tensor_scalar(st2[:, 2:3], st2[:, 2:3], EPS_LN, None, OP.add),
                      waits=[("s_ve", w4)], inc="s_ve")
            v_v2 = emit("vector", lambda e: e.reciprocal(out=st2[:, 2:3], in_=st2[:, 2:3]), inc="s_ve",
                        waits=[("s_ve", w5)])
            v_rs2 = emit("scalar", lambda e: e.activation(t_rstd[:, 0:1], st2[:, 2:3], AF.Sqrt),
                         inc="s_se", waits=[("s_ve", v_v2)])
            v_b2 = emit("vector", lambda e: e.tensor_scalar(t_rstd[:, 1:2], st2[:, 0:1], t_rstd[:, 0:1],
                                                            -1.0, OP.mult, OP.mult),
                        inc="s_ve", waits=[("s_se", v_rs2)])
            yb = t_glob % 2
            v_yt = emit("scalar", lambda e, yb=yb: e.activation(
                y_sb[:, yb, :], z_sb[:], AF.Identity, bias=t_rstd[:, 1:2], scale=t_rstd[:, 0:1]),
                inc="s_se", waits=[("s_se", v_rs2), ("s_ve", v_b2), ("s_st", v_yprev[yb])])
            v_yt_prev = v_yt
            if debug and t_glob == 0:
                emit("gpsimd", lambda e: e.dma_start(out=dbg["z0"], in_=z_sb[:]),
                     inc="s_st", n=16, waits=[("s_ve", v_b2)])
            emit("gpsimd", lambda e, t_glob=t_glob, yb=yb: e.dma_start(
                out=y_out[t_glob * 128:(t_glob + 1) * 128, :], in_=y_sb[:, yb, :]),
                inc="s_st", n=16, waits=[("s_se", v_yt)])
            v_yprev[yb] = P.val("s_st")
        v_dn_hist = P.val("s_mm")

    if debug in ("lite", "scr", "scr_nowrite"):
        lt_scr = nc.dram_tensor("lt_scr", [1, 256], F32, kind="ExternalOutput").ap()
        lt_aloc = (nc.dram_tensor("lt_aloc", [TPC, D], BF16, kind="ExternalOutput").ap()
                   if debug == "lite" else None)
        if debug == "lite":
            lt_x1 = nc.dram_tensor("lt_x1", [128, NTILE, 1024], BF16, kind="ExternalOutput").ap()
            lt_h = nc.dram_tensor("lt_h", [NCH, 128, TPC], BF16, kind="ExternalOutput").ap()
            lt_ax = nc.dram_tensor("lt_ax", [S, D], BF16, kind="ExternalOutput").ap()
        v_pre = P.val("s_st")
        if debug != "scr_nowrite":
            v_l1 = emit("gpsimd", lambda e: e.dma_start(out=row128[:], in_=scr[0:1, 0:128]),
                        inc="s_gld", n=16,
                        waits=[("s_gld", P.val("s_gld")), ("s_st", v_pre)])
            emit("gpsimd", lambda e: e.dma_start(out=lt_scr[0:1, 0:128], in_=row128[:]), inc="s_st", n=16,
                 waits=[("s_gld", v_l1)])
        if debug == "lite":
            emit("gpsimd", lambda e: e.dma_start(out=lt_aloc, in_=a_loc), inc="s_st", n=16)
            emit("gpsimd", lambda e: e.dma_start(out=lt_x1, in_=x1_sb[:]), inc="s_st", n=16,
                 waits=[("s_se", P.val("s_se"))])
            emit("gpsimd", lambda e: e.dma_start(out=lt_h, in_=h_spill), inc="s_st", n=16)
            emit("gpsimd", lambda e: e.dma_start(out=lt_ax, in_=a_xchg), inc="s_st", n=16)
    v_fin = P.val("s_st")
    P.wait("gpsimd", "s_st", v_fin)
    return _finalize(nc, P, dbg)


def _finalize(nc, P, dbg):
    with nc.Block() as block:
        @block.sync
        def _(e):
            for fn in P.ops["sync"]:
                fn(e)

        @block.gpsimd
        def _(e):
            for fn in P.ops["gpsimd"]:
                fn(e)

        @block.vector
        def _(e):
            for fn in P.ops["vector"]:
                fn(e)

        @block.scalar
        def _(e):
            for fn in P.ops["scalar"]:
                fn(e)

        @block.tensor
        def _(e):
            for fn in P.ops["tensor"]:
                fn(e)

    return nc, dbg


def make_in_maps(inputs):
    x = np.ascontiguousarray(inputs["L0_attn_x"], dtype=np.float32)
    Wq = np.asarray(inputs["L0_attn_Wq"], dtype=np.float32)
    Wk = np.asarray(inputs["L0_attn_Wk"], dtype=np.float32)
    Wv = np.asarray(inputs["L0_attn_Wv"], dtype=np.float32)
    bq = np.asarray(inputs["L0_attn_bq"], dtype=np.float32)
    bk = np.asarray(inputs["L0_attn_bk"], dtype=np.float32)
    bv = np.asarray(inputs["L0_attn_bv"], dtype=np.float32)
    beta = np.asarray(inputs["L0_attn_beta"], dtype=np.float32)
    clipx = np.asarray(inputs["L0_ffn_clip_x"], dtype=np.float32)
    clipd = np.asarray(inputs["L0_ffn_clip_down"], dtype=np.float32)
    wg = np.asarray(inputs["L0_ffn_w_gate"], dtype=np.float32)
    wu = np.asarray(inputs["L0_ffn_w_up"], dtype=np.float32)
    wd = np.asarray(inputs["L0_ffn_w_down"], dtype=np.float32)

    tril = np.tril(np.ones((128, 128))).astype(ml_dtypes.bfloat16)
    in_maps = []
    for c in range(8):
        b, p = c // 2, c % 2
        fs = slice(p * 512, (p + 1) * 512)

        def prep_rows(w, c=c):
            rows = []
            for sslot in range(PREP_SLOTS):
                ci = c + 8 * sslot
                if ci < NCH:
                    rows.append(w[ci * 128:(ci + 1) * 128, :])
                else:
                    rows.append(np.zeros((128, D), np.float32))
            return np.ascontiguousarray(np.concatenate(rows, axis=0))
        in_maps.append({
            "x": np.ascontiguousarray(x[b]),
            "x2": np.ascontiguousarray(x[b, p * TPC:(p + 1) * TPC, :]),
            "wq": np.ascontiguousarray(Wq[:, fs]),
            "wk": np.ascontiguousarray(Wk[:, fs]),
            "wv": np.ascontiguousarray(Wv[:, fs]),
            "bq": np.ascontiguousarray(bq[fs]).reshape(1, 512),
            "bk": np.ascontiguousarray(bk[fs]).reshape(1, 512),
            "bv": np.ascontiguousarray(bv[fs]).reshape(1, 512),
            "beta": np.ascontiguousarray(beta[p * 4:(p + 1) * 4]).reshape(1, 4),
            "clipx": np.ascontiguousarray(clipx).reshape(1, 1),
            "clipd": np.ascontiguousarray(clipd).reshape(1, 1),
            "tril": tril,
            "wg_prep": prep_rows(wg),
            "wu_prep": prep_rows(wu),
            "wd_prep": np.ascontiguousarray(wd[c * 128:(c + 1) * 128, :]),
        })
    return in_maps


def kernel(**inputs):
    in_maps = make_in_maps(inputs)
    nc, _ = build_program(debug=False)
    res = run_bass_kernel_spmd(nc, in_maps, core_ids=list(range(8)))
    out = np.empty((B, S, D), dtype=np.float32)
    for c in range(8):
        b, p = c // 2, c % 2
        out[b, p * TPC:(p + 1) * TPC, :] = res.results[c]["y"]
    return out


# version 04
# speedup vs baseline: 1.1005x; 1.1005x over previous
import numpy as np
import ml_dtypes
from contextlib import ExitStack

import concourse.bass as bass
import concourse.mybir as mybir
from concourse.bass_utils import run_bass_kernel_spmd

F32 = mybir.dt.float32
BF16 = mybir.dt.bfloat16
I8 = mybir.dt.int8
AF = mybir.ActivationFunctionType
OP = mybir.AluOpType
AX = mybir.AxisListType
ds = bass.ds

f32 = np.float32

B, S, D = 4, 8192, 1024
HPC = 4
NSEG = 16
TPC = 4096
NTILE = 32
II = 2816
NCH = 22
PREP_SLOTS = 3
EPS_ATT = f32(1e-6)
EPS_LN = 1e-5
MAGIC = 12582912.0
INV_SQRT_DK = f32(1.0 / np.sqrt(128.0))
INV127 = 1.0 / 127.0
CCR = PREP_SLOTS * 1024 * 2 + II

ENGS = ("sync", "gpsimd", "vector", "scalar", "tensor")


class Prog:
    def __init__(self):
        self.ops = {e: [] for e in ENGS}
        self.ctr = {}
        self.waited = {e: {} for e in ENGS}
        self.sems = {}

    def inc(self, sem, n=1):
        self.ctr[sem] = self.ctr.get(sem, 0) + n
        return self.ctr[sem]

    def val(self, sem):
        return self.ctr.get(sem, 0)

    def wait(self, eng, sem, v=None):
        if v is None:
            v = self.val(sem)
        if v <= 0:
            return
        if self.waited[eng].get(sem, -1) >= v:
            return
        self.waited[eng][sem] = v

        def fn(e, sem=sem, v=v):
            e.wait_ge(self.sems[sem], v)
        self.ops[eng].append(fn)


def build_program(debug=False, phases=4, seg_detail="full"):
    ph_dump = phases if isinstance(phases, str) else None
    if ph_dump is not None:
        phases = 3.5 if ph_dump == "x1" else 2.5
    nc = bass.Bass("TRN2", target_bir_lowering=False, detect_race_conditions=False)
    P = Prog()

    x_in = nc.dram_tensor("x", [S, D], F32, kind="ExternalInput").ap()
    wq_in = nc.dram_tensor("wq", [D, 512], F32, kind="ExternalInput").ap()
    wk_in = nc.dram_tensor("wk", [D, 512], F32, kind="ExternalInput").ap()
    wv_in = nc.dram_tensor("wv", [D, 512], F32, kind="ExternalInput").ap()
    bq_in = nc.dram_tensor("bq", [1, 512], F32, kind="ExternalInput").ap()
    bk_in = nc.dram_tensor("bk", [1, 512], F32, kind="ExternalInput").ap()
    bv_in = nc.dram_tensor("bv", [1, 512], F32, kind="ExternalInput").ap()
    beta_in = nc.dram_tensor("beta", [1, HPC], F32, kind="ExternalInput").ap()
    clipx_in = nc.dram_tensor("clipx", [1, 1], F32, kind="ExternalInput").ap()
    clipd_in = nc.dram_tensor("clipd", [1, 1], F32, kind="ExternalInput").ap()
    tril_in = nc.dram_tensor("tril", [128, 128], BF16, kind="ExternalInput").ap()
    wgp_in = nc.dram_tensor("wg_prep", [PREP_SLOTS * 128, D], F32, kind="ExternalInput").ap()
    wup_in = nc.dram_tensor("wu_prep", [PREP_SLOTS * 128, D], F32, kind="ExternalInput").ap()
    wdp_in = nc.dram_tensor("wd_prep", [128, II], F32, kind="ExternalInput").ap()

    x2_in = nc.dram_tensor("x2", [TPC, D], F32, kind="ExternalInput").ap()
    y_out = nc.dram_tensor("y", [TPC, D], F32, kind="ExternalOutput").ap()

    a_xchg = nc.dram_tensor("a_xchg", [S, D], BF16, kind="Internal", addr_space="Shared").ap()
    ax_priv = nc.dram_tensor("ax_priv", [NSEG, 4, 128, 512], BF16, kind="Internal").ap()
    a_loc = nc.dram_tensor("a_loc", [TPC, D], BF16, kind="Internal").ap()
    h_spill = nc.dram_tensor("h_spill", [NCH, 128, TPC], BF16, kind="Internal").ap()
    scr = nc.dram_tensor("scr", [1, 256], F32, kind="Internal").ap()
    cc_in = nc.dram_tensor("cc_in", [CCR, 128], BF16, kind="Internal")
    cc_out = nc.dram_tensor("cc_out", [8 * CCR, 128], BF16, kind="Internal")
    bar_in = nc.dram_tensor("bar_in", [1, 8], F32, kind="Internal")
    bar_out = nc.dram_tensor("bar_out", [1, 8], F32, kind="Internal")
    mx_in = nc.dram_tensor("mx_in", [1, 8], F32, kind="Internal")
    mx_out = nc.dram_tensor("mx_out", [1, 8], F32, kind="Internal")
    hm_in = nc.dram_tensor("hm_in", [1, 8], F32, kind="Internal")
    hm_out = nc.dram_tensor("hm_out", [1, 8], F32, kind="Internal")

    dbg = {}
    if debug:
        def dbg_t(name, shape, dt=F32):
            dbg[name] = nc.dram_tensor("dbg_" + name, shape, dt, kind="ExternalOutput").ap()
        dbg_t("qT", [128, HPC, 512])
        dbg_t("sqT", [128, HPC, 512])
        dbg_t("skT", [128, HPC, 512])
        dbg_t("v", [128, 4, HPC, 130])
        dbg_t("p", [128, 4, 512])
        dbg_t("M", [128, HPC, 130])
        dbg_t("astage", [128, 4, 512])
        dbg_t("r0", [128, 1024])
        dbg_t("gmax", [1, 8])
        dbg_t("h0", [128, 512])
        dbg_t("sg0", [128, 512])
        dbg_t("su0", [128, 512])
        dbg_t("gps", [128, 512])
        dbg_t("ups", [128, 512])
        dbg_t("xqT0", [128, 8, 128])
        dbg_t("wgt0", [128, 8, 128])
        dbg_t("hmax", [1, 8])
        dbg_t("kh", [128, NCH, 512])
        dbg_t("z0", [128, 1024])

    es0 = ExitStack()
    es1 = ExitStack()
    es2p = ExitStack()
    es2a = ExitStack()
    es2b = ExitStack()

    def mk_sb(stack):
        def sbf(name, shape, dt=F32):
            return stack.enter_context(nc.sbuf_tensor(name, shape, dt))
        return sbf

    sb0, sb1 = mk_sb(es0), mk_sb(es1)

    sgc = sb0("sgc", [1, 8])
    g_bc = sb0("g_bc", [128, HPC])
    gm1_bc = sb0("gm1_bc", [128, HPC])
    ones_r = sb0("ones_r", [1, 512], BF16)
    tril_bf = sb0("tril_bf", [128, 128], BF16)
    lmax = sb0("lmax", [1, 8])
    row128 = sb0("row128", [1, 128])
    sx_bc = sb0("sx_bc", [128, 1])
    rx_bc = sb0("rx_bc", [128, 1])
    sh_bc = sb0("sh_bc", [128, 1])
    rh_bc = sb0("rh_bc", [128, 1])
    st2 = sb0("st2", [128, 4])
    t_rstd = sb0("t_rstd", [128, 2])
    xmx = sb0("xmx", [128, 2 * NTILE])
    hmx = sb0("hmx", [128, 2 * 8 * NCH])

    psA = es0.enter_context(nc.psum_tensor("psA", [128, 2048], F32))
    psB = es0.enter_context(nc.psum_tensor("psB", [128, 2048], F32))

    for sn in ["s_ld", "s_gld", "s_tp", "s_mm", "s_ve", "s_se", "s_st", "s_cc", "s_ex"]:
        P.sems[sn] = es0.enter_context(nc.semaphore(sn))
    SEM = P.sems

    wq_bf = sb1("wq_bf", [128, 8, 512], BF16)
    wk_bf = sb1("wk_bf", [128, 8, 512], BF16)
    wv_bf = sb1("wv_bf", [128, 8, 512], BF16)
    bq_bf = sb1("bq_bf", [1, 512], BF16)
    bk_bf = sb1("bk_bf", [1, 512], BF16)
    bv_bf = sb1("bv_bf", [1, 512], BF16)
    x_tm = sb1("x_tm", [128, 2, 4, 1024], BF16)
    x_fm = sb1("x_fm", [128, 2, 4, 8, 128], BF16)
    q_bf = sb1("q_bf", [128, HPC, 512], BF16)
    k_bf = sb1("k_bf", [128, HPC, 512], BF16)
    sq_bf = sb1("sq_bf", [128, HPC, 512], BF16)
    sk_bf = sb1("sk_bf", [128, HPC, 512], BF16)
    v_bf = sb1("v_bf", [128, 4, HPC, 130], BF16)
    sk_tm = sb1("sk_tm", [128, HPC, 4, 128], BF16)
    p_bf = sb1("p_bf", [128, 4, 512], BF16)
    pT_bf = sb1("pT_bf", [128, 4, 4, 128], BF16)
    vd_bf = sb1("vd_bf", [128, 4, 130], BF16)
    M_f32 = sb1("M_f32", [128, HPC, 130])
    M_bf = sb1("M_bf", [128, HPC, 130], BF16)
    t_scr1 = sb1("t_scr1", [128, 2048])
    t_scr2 = sb1("t_scr2", [128, 2048])
    t_den = sb1("t_den", [128, 8])
    t_rq = sb1("t_rq", [128, 8])
    t_rqg = sb1("t_rqg", [128, 4])
    t_rkn = sb1("t_rkn", [128, 4])
    t_rd = sb1("t_rd", [128, 4])
    t_r2g = sb1("t_r2g", [128, 4])
    t_nm = sb1("t_nm", [128, 4])
    a_stage = sb1("a_stage", [128, 2, 4, 512], BF16)
    wp_sb = sb1("wp_sb", [128, PREP_SLOTS, 1024])
    wup_sb = sb1("wup_sb", [128, PREP_SLOTS, 1024])
    wdp_sb = sb1("wdp_sb", [128, II])
    wqz = sb1("wqz", [128, 1024], BF16)
    wdz = sb1("wdz", [128, II], BF16)
    wT_sb = sb1("wT_sb", [128, 8, 128], BF16)
    wdT_s = sb1("wdT_s", [128, NCH, 128], BF16)
    t_rm = sb1("t_rm", [128, 4])

    def emit(eng_name, fn, inc=None, n=1, waits=()):
        for (sm, v) in waits:
            P.wait(eng_name, sm, v)
        tgt = P.inc(inc, n) if inc else None

        def run(e, fn=fn, inc=inc, n=n):
            inst = fn(e)
            if inc is not None and inst is not None:
                inst.then_inc(SEM[inc], n)
        P.ops[eng_name].append(run)
        return tgt

    v_wgp = emit("sync", lambda e: e.dma_start(out=wp_sb[:], in_=wgp_in.rearrange("(c p) d -> p c d", p=128)), inc="s_ld", n=16)
    v_wup = emit("sync", lambda e: e.dma_start(out=wup_sb[:], in_=wup_in.rearrange("(c p) d -> p c d", p=128)), inc="s_ld", n=16)
    v_wdp = emit("sync", lambda e: e.dma_start(out=wdp_sb[:], in_=wdp_in), inc="s_ld", n=16)
    for (wi, wb) in ((wq_in, wq_bf), (wk_in, wk_bf), (wv_in, wv_bf)):
        emit("gpsimd", lambda e, wi=wi, wb=wb: e.dma_start(
            out=wb[:], in_=wi.rearrange("(c p) m -> p c m", p=128)), inc="s_gld", n=16)
    for (bi, bb) in ((bq_in, bq_bf), (bk_in, bk_bf), (bv_in, bv_bf)):
        emit("gpsimd", lambda e, bi=bi, bb=bb: e.dma_start(out=bb[:], in_=bi), inc="s_gld", n=16)
    v_wcast = P.val("s_gld")
    emit("sync", lambda e: e.dma_start(out=sgc[0:1, 0:4], in_=beta_in[0:1, :]), inc="s_ld", n=16)
    emit("sync", lambda e: e.dma_start(out=sgc[0:1, 4:5], in_=clipx_in[0:1, :]), inc="s_ld", n=16)
    emit("sync", lambda e: e.dma_start(out=sgc[0:1, 5:6], in_=clipd_in[0:1, :]), inc="s_ld", n=16)
    emit("sync", lambda e: e.dma_start(out=tril_bf[:], in_=tril_in), inc="s_ld", n=16)
    v_small = P.val("s_ld")
    emit("vector", lambda e: e.memset(ones_r[:], 1.0))
    emit("vector", lambda e: e.memset(p_bf.ap().rearrange("p a b -> p (a b)"), 0.0))
    emit("vector", lambda e: e.memset(v_bf.ap().rearrange("p a b c -> p (a b c)"), 1.0))
    emit("vector", lambda e: e.memset(vd_bf.ap().rearrange("p a b -> p (a b)"), 1.0))
    emit("vector", lambda e: e.memset(M_f32.ap().rearrange("p h c -> p (h c)"), 0.0))
    emit("vector", lambda e: e.memset(M_bf.ap().rearrange("p h c -> p (h c)"), 0.0), inc="s_ve")
    v_const = P.val("s_ve")
    emit("scalar", lambda e: e.activation(sgc[0:1, 6:7], sgc[0:1, 4:5], AF.Sigmoid),
         waits=[("s_ld", v_small)])
    emit("scalar", lambda e: e.activation(sgc[0:1, 7:8], sgc[0:1, 5:6], AF.Sigmoid))
    v_sig = emit("scalar", lambda e: e.activation(sgc[0:1, 0:4], sgc[0:1, 0:4], AF.Sigmoid), inc="s_se")
    v_gw = emit("gpsimd", lambda e: e.dma_start(out=scr[0:1, 0:8], in_=sgc[0:1, 0:8]),
                inc="s_ex", n=16, waits=[("s_se", v_sig)])
    v_gbc = emit("gpsimd", lambda e: e.dma_start(out=g_bc[:], in_=scr[0:1, 0:4].broadcast_to([128, 4])),
                 inc="s_ex", n=16, waits=[("s_ex", v_gw)])
    v_gm1 = emit("vector", lambda e: e.tensor_scalar(gm1_bc[:], g_bc[:], -1.0, 1.0, OP.mult, OP.add),
                 inc="s_ve", waits=[("s_ex", v_gbc)])

    def prep_one(kind, slot):
        if kind == "g":
            src_ap, row_base, wait_ld = wp_sb[:, slot, :], slot * 1024, v_wgp
        elif kind == "u":
            src_ap, row_base, wait_ld = wup_sb[:, slot, :], (PREP_SLOTS + slot) * 1024, v_wup
        else:
            src_ap, row_base, wait_ld = wdp_sb[:], 2 * PREP_SLOTS * 1024, v_wdp
        p1 = emit("vector", lambda e: e.tensor_reduce(t_rm[:, 0:1], src_ap, AX.X, OP.max,
                                                      apply_absolute_value=True),
                  waits=[("s_ld", wait_ld), ("s_tp", P.val("s_tp"))], inc="s_ve")
        p2 = emit("vector", lambda e: e.tensor_scalar(t_rm[:, 1:2], t_rm[:, 0:1], 1e-30, None, OP.max),
                  waits=[("s_ve", p1)], inc="s_ve")
        p3 = emit("vector", lambda e: e.reciprocal(out=t_rm[:, 2:3], in_=t_rm[:, 1:2]),
                  waits=[("s_ve", p2)], inc="s_ve")
        p4 = emit("vector", lambda e: e.tensor_scalar(t_rm[:, 2:3], t_rm[:, 2:3], 127.0, None, OP.mult),
                  waits=[("s_ve", p3)], inc="s_ve")
        p5 = emit("vector", lambda e: e.tensor_scalar(t_rm[:, 3:4], t_rm[:, 1:2], INV127, None, OP.mult),
                  waits=[("s_ve", p4)], inc="s_ve")
        if kind != "d":
            emit("vector", lambda e: e.tensor_scalar(t_scr1[:, 0:1024], src_ap, t_rm[:, 2:3], MAGIC, OP.mult, OP.add),
                 waits=[("s_ve", p5)])
            v_q = emit("vector", lambda e: e.tensor_scalar(wqz[:], t_scr1[:, 0:1024], MAGIC, t_rm[:, 3:4], OP.subtract, OP.mult), inc="s_ve")
            for half in range(2):
                emit("sync", lambda e, half=half: e.dma_start(
                    out=wT_sb[:, half * 4:half * 4 + 4, :],
                    in_=wqz[:, half * 512:(half + 1) * 512], transpose=True),
                    inc="s_tp", n=16, waits=[("s_ve", v_q)])
            v_t = P.val("s_tp")
            emit("gpsimd", lambda e, row_base=row_base: e.dma_start(
                out=cc_in.ap()[row_base:row_base + 1024, :].rearrange("(c p) i -> p c i", p=128),
                in_=wT_sb[:]), inc="s_ex", n=16, waits=[("s_tp", v_t)])
        else:
            for part, (o0, o1) in enumerate(((0, 1024), (1024, 2048), (2048, 2816))):
                w = o1 - o0
                emit("vector", lambda e, o0=o0, w=w: e.tensor_scalar(
                    t_scr1[:, 0:w], src_ap[:, o0:o0 + w], t_rm[:, 2:3], MAGIC, OP.mult, OP.add),
                    waits=[("s_ve", p5)] if o0 == 0 else ())
                emit("vector", lambda e, o0=o0, w=w: e.tensor_scalar(
                    wdz[:, o0:o0 + w], t_scr1[:, 0:w], MAGIC, t_rm[:, 3:4], OP.subtract, OP.mult),
                    inc="s_ve" if o1 == 2816 else None)
            v_q = P.val("s_ve")
            off = 0
            while off < II:
                wch = min(512, II - off)
                emit("sync", lambda e, off=off, wch=wch: e.dma_start(
                    out=wdT_s[:, off // 128: (off + wch) // 128, :],
                    in_=wdz[:, off:off + wch], transpose=True),
                    inc="s_tp", n=16, waits=[("s_ve", v_q)])
                off += wch
            v_t = P.val("s_tp")
            emit("gpsimd", lambda e, row_base=row_base: e.dma_start(
                out=cc_in.ap()[row_base:row_base + II, :].rearrange("(c p) i -> p c i", p=128),
                in_=wdT_s[:]), inc="s_ex", n=16, waits=[("s_tp", v_t)])

    for slot in range(PREP_SLOTS):
        prep_one("g", slot)
    for slot in range(PREP_SLOTS):
        prep_one("u", slot)
    prep_one("d", 0)
    v_prep = P.val("s_ex")
    emit("gpsimd", lambda e: e.collective_compute(
        "AllGather", OP.bypass, replica_groups=[list(range(8))],
        ins=[cc_in.ap().opt()], outs=[cc_out.ap().opt()]),
        inc="s_cc", waits=[("s_ex", v_prep)])
    V_AG = P.val("s_cc")

    pcol_holder = {}

    def g_pcol(e):
        if "v" not in pcol_holder:
            pid = e.partition_id()
            pcol_holder["v"] = e.snap((pid % 2) * 512, min_val=0, max_val=512)
        return pcol_holder["v"]

    v_Mbf = [v_const] * HPC
    v_pT_qc = [0, 0, 0, 0]
    v_head_vec_prev = v_const
    v_am_prev = 0
    v_comb_prev = v_const
    v_xfm_hist = [0, 0]
    v_mmv_hist = [0, 0]
    v_astore_hist = [0, 0]
    v_axp_hist = [0, 0]

    for s in range(NSEG if phases >= 2 else 0):
        sp = s % 2
        v_xld = emit("gpsimd", lambda e, s=s, sp=sp: e.dma_start(
            out=x_tm[:, sp, :, :],
            in_=x_in[s * 512:(s + 1) * 512, :].rearrange("(tc p) d -> p tc d", p=128)),
            inc="s_gld", n=16, waits=[("s_tp", v_xfm_hist[sp])])
        for tc in range(4):
            for half in range(2):
                emit("sync", lambda e, sp=sp, tc=tc, half=half: e.dma_start(
                    out=x_fm[:, sp, tc, half * 4:half * 4 + 4, :],
                    in_=x_tm[:, sp, tc, half * 512:(half + 1) * 512], transpose=True),
                    inc="s_tp", n=16,
                    waits=[("s_gld", v_xld), ("s_mm", v_mmv_hist[sp])])
        v_xfm = P.val("s_tp")
        v_xfm_hist[sp] = v_xfm
        if seg_detail == "xonly":
            P.wait("gpsimd", "s_tp", v_xfm)
            continue

        def proj_group(e, w_sb_, b_sb_, f, dst, sp):
            for d in range(8):
                e.matmul(dst, w_sb_[:, d, f * 128:(f + 1) * 128],
                         x_fm[:, sp, :, d, :],
                         start=(d == 0), stop=False)
            return e.matmul(dst, b_sb_[0:1, f * 128:(f + 1) * 128], ones_r[:],
                            start=False, stop=True)

        _gates = {"qmm": 1, "qevac1": 1.2, "qevac2": 1.4, "qevac3": 1.6, "qevac": 2, "vmm": 3, "kevac": 4, "vevac": 5, "sk": 6, "proj": 7, "full": 99}
        _lvl = _gates.get(seg_detail, 99)
        q_waits = [("s_tp", v_xfm), ("s_gld", v_wcast),
                   ("s_ve", max(v_const, v_head_vec_prev)), ("s_se", v_am_prev)]
        for f in range(HPC):
            emit("tensor", lambda e, f=f, sp=sp: proj_group(e, wq_bf, bq_bf, f, psA[:, f * 512:(f + 1) * 512], sp),
                 inc="s_mm", waits=q_waits if f == 0 else ())
        v_mm_q = P.val("s_mm")
        for f in range(HPC):
            emit("tensor", lambda e, f=f, sp=sp: proj_group(e, wk_bf, bk_bf, f, psB[:, f * 512:(f + 1) * 512], sp),
                 inc="s_mm")
        v_mm_k = P.val("s_mm")

        if _lvl < 1.2:
            P.wait("gpsimd", "s_mm", P.val("s_mm"))
            continue
        v_qcopy = emit("scalar", lambda e: e.activation(q_bf.ap().rearrange("p h t -> p (h t)"), psA[:],
                                                        AF.Copy, scale=INV_SQRT_DK),
                       inc="s_se", waits=[("s_mm", v_mm_q)])
        if _lvl < 1.4:
            P.wait("gpsimd", "s_se", P.val("s_se"))
            continue
        v_minq = emit("vector", lambda e: e.tensor_scalar(t_scr1[:], psA[:], 0.0, None, OP.min),
                      inc="s_ve", waits=[("s_mm", v_mm_q), ("s_se", v_qcopy)])
        if _lvl < 1.6:
            P.wait("gpsimd", "s_se", P.val("s_se"))
            P.wait("gpsimd", "s_ve", P.val("s_ve"))
            continue
        v_exq = emit("scalar", lambda e: e.activation(t_scr2[:], t_scr1[:], AF.Exp),
                     inc="s_se", waits=[("s_ve", v_minq)])
        if _lvl < 2:
            P.wait("gpsimd", "s_se", P.val("s_se"))
            P.wait("gpsimd", "s_ve", P.val("s_ve"))
            continue
        v_evac_q = emit("vector", lambda e: e.scalar_tensor_tensor(
            sq_bf.ap().rearrange("p h t -> p (h t)"), psA[:], 0.0, t_scr2[:], OP.max, OP.add),
            inc="s_ve", waits=[("s_se", v_exq)])

        if _lvl < 3:
            P.wait("gpsimd", "s_ve", P.val("s_ve"))
            P.wait("gpsimd", "s_se", P.val("s_se"))
            continue
        for tc in range(4):
            def vgroup(e, tc=tc, sp=sp):
                for d in range(8):
                    e.matmul(psA[:, tc * 512:(tc + 1) * 512],
                             x_fm[:, sp, tc, d, :],
                             wv_bf[:, d, :], start=(d == 0), stop=False)
                return e.matmul(psA[:, tc * 512:(tc + 1) * 512], ones_r[0:1, 0:128],
                                bv_bf[:], start=False, stop=True)
            emit("tensor", vgroup, inc="s_mm",
                 waits=[("s_ve", v_evac_q), ("s_se", v_exq)] if tc == 0 else ())
        v_mm_v = P.val("s_mm")
        v_mmv_hist[sp] = v_mm_v

        if _lvl < 4:
            P.wait("gpsimd", "s_mm", P.val("s_mm"))
            continue
        v_kcopy = emit("scalar", lambda e: e.activation(k_bf.ap().rearrange("p h t -> p (h t)"), psB[:],
                                                        AF.Copy, scale=1.0),
                       inc="s_se", waits=[("s_mm", v_mm_k)])
        v_mink = emit("vector", lambda e: e.tensor_scalar(t_scr1[:], psB[:], 0.0, None, OP.min),
                      inc="s_ve", waits=[("s_mm", v_mm_k), ("s_se", v_kcopy)])
        v_exk = emit("scalar", lambda e: e.activation(t_scr2[:], t_scr1[:], AF.Exp),
                     inc="s_se", waits=[("s_ve", v_mink)])
        v_evac_k = emit("vector", lambda e: e.scalar_tensor_tensor(
            sk_bf.ap().rearrange("p h t -> p (h t)"), psB[:], 0.0, t_scr2[:], OP.max, OP.add),
            inc="s_ve", waits=[("s_se", v_exk)])

        if _lvl < 5:
            P.wait("gpsimd", "s_ve", P.val("s_ve"))
            P.wait("gpsimd", "s_se", P.val("s_se"))
            continue
        for tc in range(4):
            emit("vector", lambda e, tc=tc: e.tensor_copy(
                out=v_bf[:, tc, :, 0:128],
                in_=psA.ap().rearrange("p (tc h f) -> p tc h f", tc=4, h=HPC)[:, tc, :, :]),
                inc="s_ve" if tc == 3 else None,
                waits=[("s_mm", v_mm_v)] if tc == 0 else ())
        v_evac_v = P.val("s_ve")

        if _lvl < 6:
            P.wait("gpsimd", "s_ve", P.val("s_ve"))
            continue
        for h in range(HPC):
            emit("sync", lambda e, h=h: e.dma_start(
                out=sk_tm[:, h, :, :], in_=sk_bf[:, h, :], transpose=True),
                inc="s_tp", n=16, waits=[("s_ve", v_evac_k)])
        v_sktm = P.val("s_tp")

        if debug and s == 0:
            emit("gpsimd", lambda e: e.dma_start(out=dbg["qT"], in_=q_bf[:]), inc="s_st", n=16,
                 waits=[("s_se", v_exq)])
            emit("gpsimd", lambda e: e.dma_start(out=dbg["sqT"], in_=sq_bf[:]), inc="s_st", n=16,
                 waits=[("s_ve", v_evac_q)])
            emit("gpsimd", lambda e: e.dma_start(out=dbg["skT"], in_=sk_bf[:]), inc="s_st", n=16,
                 waits=[("s_ve", v_evac_k)])
            emit("gpsimd", lambda e: e.dma_start(out=dbg["v"], in_=v_bf[:]), inc="s_st", n=16,
                 waits=[("s_ve", v_evac_v)])

        if seg_detail == "proj":
            P.wait("gpsimd", "s_ve", P.val("s_ve"))
            P.wait("gpsimd", "s_tp", P.val("s_tp"))
            continue
        for h in range(HPC):
            def numq_group(e, h=h):
                for qc in range(4):
                    inst = e.matmul(psA[:, qc * 256:qc * 256 + 130],
                                    sq_bf[:, h, qc * 128:(qc + 1) * 128],
                                    M_bf[:, h, :], start=True, stop=True)
                return inst
            emit("tensor", numq_group, inc="s_mm",
                 waits=[("s_ve", max(v_evac_q, v_evac_v, v_Mbf[h])),
                        ("s_se", v_am_prev)])

            def numk_group(e, h=h):
                for kc in range(4):
                    inst = e.matmul(psA[:, 1024 + kc * 256:1024 + kc * 256 + 130],
                                    sk_bf[:, h, kc * 128:(kc + 1) * 128],
                                    M_bf[:, h, :], start=True, stop=True)
                return inst
            v_numk = emit("tensor", numk_group, inc="s_mm", waits=[("s_ve", v_evac_k)])

            v_dense = emit("scalar", lambda e: e.add(
                t_den[:], psA.ap().rearrange("p (c u) -> p c u", u=256)[:, :, 128], EPS_ATT),
                inc="s_se", waits=[("s_mm", v_numk), ("s_ve", max(v_Mbf))])
            if s == 1 and h == 0 and isinstance(debug, str) and debug == "cap2":
                emit("vector", lambda e: e.tensor_copy(out=t_scr1[:, 1024:1032], in_=t_den[:]),
                     inc="s_ve", waits=[("s_se", v_dense)])
                emit("vector", lambda e: e.tensor_copy(out=t_scr1[:, 1032:1162], in_=M_bf[:, 0, :]))
                emit("vector", lambda e: e.tensor_copy(out=t_scr1[:, 1162:1292], in_=psA[:, 1024:1154]))
                emit("vector", lambda e: e.tensor_copy(out=t_scr1[:, 1300:1820],
                                                       in_=M_bf.ap().rearrange("p h c -> p (h c)")))
                emit("vector", lambda e: e.tensor_copy(out=t_scr1[:, 1820:1950],
                                                       in_=M_f32[:, 0, :]))
            v_rq = emit("vector", lambda e: e.reciprocal(out=t_rq[:], in_=t_den[:]),
                        waits=[("s_se", v_dense)], inc="s_ve")
            emit("vector", lambda e, h=h: e.tensor_scalar(t_rqg[:], t_rq[:, 0:4], g_bc[:, h:h + 1], None, OP.mult),
                 waits=[("s_ex", v_gbc), ("s_ve", v_rq)])
            v_rkn = emit("vector", lambda e: e.tensor_scalar(t_rkn[:], t_rq[:, 4:8], -1.0, None, OP.mult),
                         inc="s_ve")
            if s == 1 and h == 0 and isinstance(debug, str) and debug == "cap4":
                emit("vector", lambda e: e.tensor_copy(out=t_scr1[:, 1024:1032], in_=t_den[:]))
                emit("vector", lambda e: e.tensor_copy(out=t_scr1[:, 1032:1040], in_=t_rq[:]))
                emit("vector", lambda e: e.tensor_copy(out=t_scr1[:, 1040:1044], in_=t_rkn[:]))
                emit("vector", lambda e: e.tensor_copy(out=t_scr1[:, 1044:1048], in_=t_rqg[:]))
            for kc in range(4):
                emit("vector", lambda e, kc=kc, h=h: e.scalar_tensor_tensor(
                    vd_bf[:, kc, 0:128], psA[:, 1024 + kc * 256:1024 + kc * 256 + 128],
                    t_rkn[:, kc:kc + 1], v_bf[:, kc, h, 0:128], OP.mult, OP.add),
                    inc="s_ve" if kc == 3 else None,
                    waits=[("s_ve", v_rkn)] if kc == 0 else ())
            v_vd = P.val("s_ve")

            if s == 1 and h == 0 and isinstance(debug, str) and debug == "cap4":
                emit("vector", lambda e: e.tensor_copy(out=t_scr1[:, 1048:1178], in_=vd_bf[:, 0, :]))
                emit("vector", lambda e: e.tensor_copy(out=t_scr1[:, 1178:1308], in_=vd_bf[:, 3, :]))

            def mup_group(e, h=h):
                for kc in range(4):
                    inst = e.matmul(psA[:, 1024:1154], sk_tm[:, h, kc, :],
                                    vd_bf[:, kc, :], start=(kc == 0), stop=(kc == 3))
                return inst
            v_mup = emit("tensor", mup_group, inc="s_mm",
                         waits=[("s_ve", v_vd), ("s_tp", v_sktm)])
            if s == 0 and h == 0 and isinstance(debug, str) and debug == "cap":
                emit("vector", lambda e: e.tensor_copy(out=t_scr1[:, 1900:2030], in_=psA[:, 1024:1154]),
                     inc="s_ve", waits=[("s_mm", v_mup)])
                emit("vector", lambda e: e.tensor_copy(out=t_scr1[:, 1380:1900],
                                                       in_=vd_bf.ap().rearrange("p a b -> p (a b)")))
                emit("vector", lambda e: e.tensor_copy(out=t_scr1[:, 868:1380],
                                                       in_=sk_tm[:, 0, :, :].rearrange("p a b -> p (a b)")))
            if s == 1 and h == 0 and isinstance(debug, str) and debug == "cap3":
                emit("vector", lambda e: e.tensor_copy(out=t_scr1[:, 1024:1154], in_=psA[:, 1024:1154]),
                     waits=[("s_mm", v_mup)])
            emit("vector", lambda e, h=h: e.tensor_tensor(M_f32[:, h, :], M_f32[:, h, :],
                                                          psA[:, 1024:1154], OP.add),
                 waits=[("s_mm", v_mup)])
            v_Mbf[h] = emit("vector", lambda e, h=h: e.tensor_copy(out=M_bf[:, h, :], in_=M_f32[:, h, :]),
                            inc="s_ve")
            if s == 1 and isinstance(debug, str) and debug == "cap3":
                off = 1160 + 140 * h
                emit("vector", lambda e, off=off: e.tensor_copy(out=t_scr1[:, off:off + 130],
                                                                in_=M_f32[:, 0, :]))

            for qc in range(4):
                nk = (qc + 1) * 128
                boff = (qc % 2) * 512
                w = []
                if h == 0 and qc == 0:
                    w = [("s_se", v_exk), ("s_ve", v_evac_k)]
                elif qc >= 2:
                    w = [("s_se", P.val("s_se"))]
                v_sc = emit("tensor", lambda e, h=h, qc=qc, nk=nk, boff=boff: e.matmul(
                    psB[:, boff:boff + nk], q_bf[:, h, qc * 128:(qc + 1) * 128],
                    k_bf[:, h, 0:nk], start=True, stop=True),
                    inc="s_mm", waits=w)
                v_nm = emit("vector", lambda e, qc=qc, nk=nk, boff=boff: e.tensor_reduce(
                    t_nm[:, qc:qc + 1], psB[:, boff:boff + nk], AX.X, OP.max, negate=True),
                    inc="s_ve", waits=[("s_mm", v_sc)])
                v_exp = emit("scalar", lambda e, qc=qc, nk=nk, boff=boff: e.activation(
                    p_bf[:, qc, 0:nk], psB[:, boff:boff + nk], AF.Exp,
                    bias=t_nm[:, qc:qc + 1]),
                    inc="s_se", waits=[("s_ve", v_nm)])
                v_msk = emit("vector", lambda e, qc=qc: e.tensor_tensor(
                    p_bf[:, qc, qc * 128:(qc + 1) * 128], p_bf[:, qc, qc * 128:(qc + 1) * 128],
                    tril_bf[:], OP.mult), inc="s_ve", waits=[("s_se", v_exp)])
                emit("sync", lambda e, qc=qc, nk=nk: e.dma_start(
                    out=pT_bf[:, qc, 0:qc + 1, :],
                    in_=p_bf[:, qc, 0:nk], transpose=True),
                    inc="s_tp", n=16, waits=[("s_ve", v_msk)])
                v_pT_qc[qc] = P.val("s_tp")

            for qc in range(4):
                def pv_group(e, h=h, qc=qc):
                    for kc in range(qc + 1):
                        inst = e.matmul(psB[:, 1024 + qc * 256:1024 + qc * 256 + 130],
                                        pT_bf[:, qc, kc, :],
                                        v_bf[:, kc, h, :], start=(kc == 0), stop=(kc == qc))
                    return inst
                emit("tensor", pv_group, inc="s_mm",
                     waits=[("s_tp", v_pT_qc[qc])])
            v_pv = P.val("s_mm")

            v_rd = emit("vector", lambda e: e.reciprocal(
                out=t_rd[:], in_=psB.ap().rearrange("p (c u) -> p c u", u=256)[:, 4:8, 128]),
                waits=[("s_mm", v_pv)], inc="s_ve")
            v_r2 = emit("vector", lambda e, h=h: e.tensor_scalar(t_r2g[:], t_rd[:], gm1_bc[:, h:h + 1], None, OP.mult),
                        inc="s_ve", waits=[("s_ve", max(v_gm1, v_rd))])
            for qc in range(4):
                emit("scalar", lambda e, qc=qc: e.activation(
                    t_scr2[:, qc * 128:(qc + 1) * 128], psA[:, qc * 256:qc * 256 + 128],
                    AF.Identity, scale=t_rqg[:, qc:qc + 1]),
                    waits=[("s_ve", v_r2)] if qc == 0 else (),
                    inc="s_se" if qc == 3 else None)
            v_am = P.val("s_se")
            v_am_prev = v_am
            for qc in range(4):
                emit("vector", lambda e, h=h, qc=qc, sp=sp: e.scalar_tensor_tensor(
                    a_stage[:, sp, qc, h * 128:(h + 1) * 128],
                    psB[:, 1024 + qc * 256:1024 + qc * 256 + 128],
                    t_r2g[:, qc:qc + 1], t_scr2[:, qc * 128:(qc + 1) * 128], OP.mult, OP.add),
                    waits=([("s_ve", v_r2)] if qc == 0 else []) +
                          [("s_se", v_am), ("s_ld", v_axp_hist[sp])],
                    inc="s_ve" if qc == 3 else None)
            v_comb_prev = P.val("s_ve")
            if s == NSEG - 1 and h == 0 and isinstance(debug, str) and debug == "cap6":
                emit("vector", lambda e, sp=sp: e.tensor_copy(
                    out=t_scr1[:, 1024:1536].rearrange("p (q f) -> p q f", q=4),
                    in_=a_stage[:, sp, :, 0:128]))
                emit("vector", lambda e: e.tensor_copy(out=t_scr1[:, 1536:1540], in_=t_r2g[:]))
                emit("vector", lambda e: e.tensor_copy(out=t_scr1[:, 1540:1544], in_=t_rqg[:]))
                emit("vector", lambda e: e.tensor_copy(out=t_scr1[:, 1544:1674],
                                                       in_=psB[:, 1024:1154]))
                emit("vector", lambda e: e.tensor_copy(out=t_scr1[:, 1674:1804],
                                                       in_=psB[:, 1792:1922]))
                emit("vector", lambda e: e.tensor_copy(out=t_scr1[:, 1804:1932],
                                                       in_=t_scr2[:, 384:512]))

            if debug and s == 0 and h == 0:
                emit("gpsimd", lambda e: e.dma_start(out=dbg["p"], in_=p_bf[:]), inc="s_st", n=16,
                     waits=[("s_ve", v_comb_prev)])

        v_head_vec_prev = P.val("s_ve")

        for qc in range(4):
            emit("sync", lambda e, s=s, sp=sp, qc=qc: e.dma_start(
                out=ax_priv[s, qc], in_=a_stage[:, sp, qc, :]),
                inc="s_ld", n=16, waits=[("s_ve", v_head_vec_prev)] if qc == 0 else ())
        v_axp = P.val("s_ld")
        v_axp_hist[sp] = v_axp

        def astore(e, s=s):
            pc = g_pcol(e)
            return e.dma_start(
                out=a_xchg[s * 512:(s + 1) * 512, ds(pc, 512)],
                in_=ax_priv[s].rearrange("qc p f -> (qc p) f"))
        emit("gpsimd", astore, inc="s_st", n=16, waits=[("s_ld", v_axp)])
        v_astore_hist[sp] = P.val("s_st")

        if debug and s == 0:
            emit("gpsimd", lambda e: e.dma_start(out=dbg["astage"], in_=a_stage[:, 0, :, :]),
                 inc="s_st", n=16, waits=[("s_ve", v_head_vec_prev)])
            v_astore_hist[sp] = P.val("s_st")

    if debug:
        emit("gpsimd", lambda e: e.dma_start(out=dbg["M"], in_=M_f32[:]), inc="s_st", n=16,
             waits=[("s_ve", max(v_Mbf))])

    v_all_store = P.val("s_st")
    v_barw = emit("gpsimd", lambda e: e.dma_start(out=bar_in.ap(), in_=sgc[0:1, 0:8]),
                  inc="s_ex", n=16, waits=[("s_st", v_all_store)])
    emit("gpsimd", lambda e: e.collective_compute(
        "AllReduce", OP.add, replica_groups=[list(range(8))],
        ins=[bar_in.ap().opt()], outs=[bar_out.ap().opt()]),
        inc="s_cc", waits=[("s_ex", v_barw)])
    V_BAR = P.val("s_cc")

    def a_copy(e):
        pid = e.partition_id()
        off = e.snap((pid % 2) * TPC, min_val=0, max_val=TPC)
        return e.dma_start(out=a_loc, in_=a_xchg[ds(off, TPC), :])
    V_ACP = emit("gpsimd", a_copy, inc="s_gld", n=16, waits=[("s_cc", V_BAR)])

    if ph_dump == "vbuf":
        tmp_f = nc.sbuf_tensor("dumpV_f", [128, 2080], F32)
        with tmp_f as tf:
            v_c = emit("vector", lambda e: e.tensor_copy(
                out=tf[:], in_=v_bf.ap().rearrange("p a b c -> p (a b c)")),
                inc="s_ve", waits=[("s_ve", P.val("s_ve")), ("s_se", P.val("s_se")),
                                   ("s_mm", P.val("s_mm"))])
            emit("gpsimd", lambda e: e.dma_start(out=y_out[0:128, 0:1024], in_=tf[:, 0:1024]),
                 inc="s_st", n=16, waits=[("s_ve", v_c)])
            emit("gpsimd", lambda e: e.dma_start(out=y_out[128:256, 0:1024], in_=tf[:, 1024:2048]),
                 inc="s_st", n=16)
            v_d3 = emit("gpsimd", lambda e: e.dma_start(out=y_out[256:384, 0:32], in_=tf[:, 2048:2080]),
                 inc="s_st", n=16)
            v_c2 = emit("vector", lambda e: e.tensor_copy(
                out=tf[:, 0:520], in_=vd_bf.ap().rearrange("p a b -> p (a b)")),
                inc="s_ve", waits=[("s_st", v_d3)])
            emit("gpsimd", lambda e: e.dma_start(out=y_out[384:512, 0:520], in_=tf[:, 0:520]),
                 inc="s_st", n=16, waits=[("s_ve", v_c2)])
        P.wait("gpsimd", "s_st", P.val("s_st"))
        return _finalize(nc, P, dbg)

    if ph_dump == "M":
        tmp_f = nc.sbuf_tensor("dumpM_f", [128, HPC, 130], F32)
        with tmp_f as tf:
            v_c = emit("vector", lambda e: e.tensor_copy(
                out=tf.ap().rearrange("p h c -> p (h c)"),
                in_=M_f32.ap().rearrange("p h c -> p (h c)")),
                inc="s_ve", waits=[("s_ve", max(v_Mbf))])
            v_d1 = emit("gpsimd", lambda e: e.dma_start(
                out=y_out[0:128, 0:HPC * 130], in_=tf.ap().rearrange("p h c -> p (h c)")),
                inc="s_st", n=16, waits=[("s_ve", v_c)])
            v_c2 = emit("vector", lambda e: e.tensor_copy(
                out=tf.ap().rearrange("p h c -> p (h c)")[:, 0:512],
                in_=a_stage[:, (NSEG - 1) % 2, 0, :]), inc="s_ve",
                waits=[("s_st", v_d1)])
            emit("gpsimd", lambda e: e.dma_start(
                out=y_out[128:256, 0:512], in_=tf.ap().rearrange("p h c -> p (h c)")[:, 0:512]),
                inc="s_st", n=16, waits=[("s_ve", v_c2)])
            if isinstance(debug, str) and debug == "cap":
                emit("gpsimd", lambda e: e.dma_start(out=y_out[256:384, 0:1024], in_=t_scr1[:, 868:1892]),
                     inc="s_st", n=16)
                emit("gpsimd", lambda e: e.dma_start(out=y_out[384:512, 0:138], in_=t_scr1[:, 1892:2030]),
                     inc="s_st", n=16)
            if isinstance(debug, str) and debug == "cap2":
                emit("gpsimd", lambda e: e.dma_start(out=y_out[256:384, 0:268], in_=t_scr1[:, 1024:1292]),
                     inc="s_st", n=16)
                emit("gpsimd", lambda e: e.dma_start(out=y_out[384:512, 0:650], in_=t_scr1[:, 1300:1950]),
                     inc="s_st", n=16)
            if isinstance(debug, str) and debug in ("cap3", "cap4"):
                emit("gpsimd", lambda e: e.dma_start(out=y_out[256:384, 0:696], in_=t_scr1[:, 1024:1720]),
                     inc="s_st", n=16)
            if isinstance(debug, str) and debug == "cap6":
                emit("gpsimd", lambda e: e.dma_start(out=y_out[256:384, 0:908], in_=t_scr1[:, 1024:1932]),
                     inc="s_st", n=16)
            if isinstance(debug, str) and debug == "cap5":
                v_c5 = emit("vector", lambda e: e.tensor_copy(
                    out=t_scr1[:], in_=a_stage[:, (NSEG - 1) % 2, :, :].rearrange("p q f -> p (q f)")),
                    inc="s_ve", waits=[("s_st", P.val("s_st"))])
                emit("gpsimd", lambda e: e.dma_start(out=y_out[256:384, 0:1024], in_=t_scr1[:, 0:1024]),
                     inc="s_st", n=16, waits=[("s_ve", v_c5)])
                emit("gpsimd", lambda e: e.dma_start(out=y_out[384:512, 0:1024], in_=t_scr1[:, 1024:2048]),
                     inc="s_st", n=16)
                v_c8 = emit("vector", lambda e: e.tensor_copy(out=t_scr1[:, 1024:1536], in_=t_scr2[:, 0:512]),
                            inc="s_ve", waits=[("s_st", P.val("s_st"))])
                emit("vector", lambda e: e.tensor_copy(out=t_scr1[:, 2040:2044], in_=t_nm[:]))
                emit("vector", lambda e: e.tensor_copy(out=t_scr1[:, 2044:2048], in_=t_rqg[:]))
                emit("vector", lambda e: e.tensor_copy(out=t_scr1[:, 2036:2040], in_=t_r2g[:]))
                v_c9 = emit("vector", lambda e: e.tensor_copy(
                    out=t_scr2[:, 0:1024], in_=psB[:, 1024:2048]), inc="s_ve")
                emit("gpsimd", lambda e: e.dma_start(out=y_out[640:768, 0:1024], in_=t_scr2[:, 0:1024]),
                     inc="s_st", n=16, waits=[("s_ve", v_c9)])
                emit("gpsimd", lambda e: e.dma_start(out=y_out[768:896, 0:512], in_=t_scr1[:, 1024:1536]),
                     inc="s_st", n=16)
                emit("gpsimd", lambda e: e.dma_start(out=y_out[896:1024, 0:12], in_=t_scr1[:, 2036:2048]),
                     inc="s_st", n=16)
                with nc.sbuf_tensor("dump_axp", [128, 2, 512], BF16) as axp_t:
                    v_c6 = emit("gpsimd", lambda e: e.dma_start(out=axp_t[:, 0, :], in_=ax_priv[NSEG - 1, 1]),
                                inc="s_gld", n=16)
                    emit("gpsimd", lambda e: e.dma_start(out=axp_t[:, 1, :], in_=ax_priv[NSEG - 1, 3]),
                         inc="s_gld", n=16)
                    v_c7 = emit("vector", lambda e: e.tensor_copy(
                        out=t_scr1[:, 0:1024], in_=axp_t.ap().rearrange("p a b -> p (a b)")),
                        inc="s_ve", waits=[("s_gld", P.val("s_gld")), ("s_st", P.val("s_st"))])
                    emit("gpsimd", lambda e: e.dma_start(out=y_out[512:640, 0:1024], in_=t_scr1[:, 0:1024]),
                         inc="s_st", n=16, waits=[("s_ve", v_c7)])
        P.wait("gpsimd", "s_st", P.val("s_st"))
        return _finalize(nc, P, dbg)

    if ph_dump in ("aloc", "ax"):
        dump_src = a_loc if ph_dump == "aloc" else a_xchg
        tmp_bf = nc.sbuf_tensor("dump_bf", [128, 1024], BF16)
        tmp_f = nc.sbuf_tensor("dump_f", [128, 2, 1024], F32)
        with tmp_bf as tbf, tmp_f as tf:
            v_pv = 0
            for t in range(NTILE):
                tb = t % 2
                v_l = emit("gpsimd", lambda e, t=t: e.dma_start(
                    out=tbf[:], in_=dump_src[t * 128:(t + 1) * 128, :]),
                    inc="s_gld", n=16, waits=[("s_gld", V_ACP), ("s_ve", v_pv)])
                v_c = emit("vector", lambda e, tb=tb: e.tensor_copy(out=tf[:, tb, :], in_=tbf[:]),
                           inc="s_ve", waits=[("s_gld", v_l), ("s_st", v_pv_st if t >= 2 else 0)])
                v_s = emit("gpsimd", lambda e, t=t, tb=tb: e.dma_start(
                    out=y_out[t * 128:(t + 1) * 128, :], in_=tf[:, tb, :]),
                    inc="s_st", n=16, waits=[("s_ve", v_c)])
                v_pv = v_c
                v_pv_st = P.val("s_st")
        P.wait("gpsimd", "s_st", P.val("s_st"))
        return _finalize(nc, P, dbg)

    es1.close()
    sb2p, sb2a = mk_sb(es2p), mk_sb(es2a)
    x1_sb = sb2p("x1_sb", [128, NTILE, 1024], BF16)
    x2_sb = sb2a("x2_sb", [128, 2, 1024])
    a2_sb = sb2a("a2_sb", [128, 2, 1024], BF16)
    r_sb = sb2a("r_sb", [128, 2, 1024])
    rsq_sb = sb2a("rsq_sb", [128, 1024])
    xq_bf = sb2a("xq_bf", [128, 2, 1024], BF16)
    xqT = sb2a("xqT", [128, 8, NTILE, 128], BF16)
    wgt_sb = sb2a("wgt_sb", [128, 2, 8, 128], BF16)
    wut_sb = sb2a("wut_sb", [128, 2, 8, 128], BF16)
    sg_sb = sb2a("sg_sb", [128, 512], BF16)
    su_sb = sb2a("su_sb", [128, 512], BF16)
    h_bf = sb2a("h_bf", [128, 4096], BF16)

    v_r_prev = [0, 0]
    v_x1_hist = [0] * NTILE
    v_xqT_tile = [0] * NTILE
    for t in range(NTILE if phases >= 3 else 0):
        tb = t % 2
        v_xl = emit("sync", lambda e, t=t, tb=tb: e.dma_start(
            out=x2_sb[:, tb, :],
            in_=x2_in[t * 128:(t + 1) * 128, :]),
            inc="s_ld", n=16, waits=[("s_cc", V_BAR), ("s_ve", v_r_prev[tb])])
        v_al = emit("gpsimd", lambda e, t=t, tb=tb: e.dma_start(
            out=a2_sb[:, tb, :],
            in_=a_loc[t * 128:(t + 1) * 128, :]),
            inc="s_gld", n=16, waits=[("s_gld", V_ACP), ("s_ve", v_r_prev[tb])])
        emit("vector", lambda e, tb=tb: e.scalar_tensor_tensor(
            r_sb[:, tb, :], x2_sb[:, tb, :], 0.0, a2_sb[:, tb, :], OP.add, OP.add,
            accum_out=st2[:, 0:1]),
            waits=[("s_ld", v_xl), ("s_gld", v_al)])
        v_acc = emit("vector", lambda e, tb=tb: e.scalar_tensor_tensor(
            rsq_sb[:], r_sb[:, tb, :], 0.0, r_sb[:, tb, :], OP.add, OP.mult,
            accum_out=st2[:, 1:2]), inc="s_ve")
        v1 = emit("vector", lambda e: e.tensor_scalar(st2[:, 0:1], st2[:, 0:1], float(1.0 / D), None, OP.mult),
                  waits=[("s_ve", v_acc)], inc="s_ve")
        v2 = emit("vector", lambda e: e.tensor_tensor(st2[:, 3:4], st2[:, 0:1], st2[:, 0:1], OP.mult),
                  waits=[("s_ve", v1)], inc="s_ve")
        v3 = emit("vector", lambda e: e.scalar_tensor_tensor(st2[:, 2:3], st2[:, 1:2], float(1.0 / D),
                                                             st2[:, 3:4], OP.mult, OP.subtract),
                  waits=[("s_ve", v2)], inc="s_ve")
        v4 = emit("vector", lambda e: e.tensor_scalar(st2[:, 2:3], st2[:, 2:3], EPS_LN, None, OP.add),
                  waits=[("s_ve", v3)], inc="s_ve")
        v_var = emit("vector", lambda e: e.reciprocal(out=st2[:, 2:3], in_=st2[:, 2:3]), inc="s_ve",
                     waits=[("s_ve", v4)])
        v_rst = emit("scalar", lambda e: e.activation(t_rstd[:, 0:1], st2[:, 2:3], AF.Sqrt),
                     inc="s_se", waits=[("s_ve", v_var)])
        v_bias = emit("vector", lambda e: e.tensor_scalar(t_rstd[:, 1:2], st2[:, 0:1], t_rstd[:, 0:1],
                                                          -1.0, OP.mult, OP.mult),
                      inc="s_ve", waits=[("s_se", v_rst)])
        v_x1t = emit("scalar", lambda e, t=t, tb=tb: e.activation(
            x1_sb[:, t, :], r_sb[:, tb, :], AF.Identity,
            bias=t_rstd[:, 1:2], scale=t_rstd[:, 0:1]),
            inc="s_se", waits=[("s_se", v_rst), ("s_ve", v_bias)])
        v_x1_hist[t] = v_x1t
        emit("vector", lambda e, t=t: e.tensor_scalar(
            rsq_sb[:], x1_sb[:, t, :], 1.0, None, OP.mult, OP.max,
            accum_out=xmx[:, 2 * t:2 * t + 1]), waits=[("s_se", v_x1t)])
        emit("vector", lambda e, t=t: e.tensor_scalar(
            rsq_sb[:], x1_sb[:, t, :], -1.0, None, OP.mult, OP.max,
            accum_out=xmx[:, 2 * t + 1:2 * t + 2]), inc="s_ve")
        v_r_prev[tb] = P.val("s_ve")
        if debug and t == 0:
            emit("gpsimd", lambda e: e.dma_start(out=dbg["r0"], in_=r_sb[:, 0, :]), inc="s_st", n=16,
                 waits=[("s_ve", v_r_prev[0])])

    if phases < 3:
        P.wait("gpsimd", "s_cc", P.val("s_cc"))
        P.wait("gpsimd", "s_gld", P.val("s_gld"))
        P.wait("gpsimd", "s_st", P.val("s_st"))
        return _finalize(nc, P, dbg)
    v_xm = emit("vector", lambda e: e.tensor_reduce(xmx[:, 0:1], xmx[:], AX.X, OP.max), inc="s_ve",
                waits=[("s_ve", P.val("s_ve"))])
    v_sw = emit("gpsimd", lambda e: e.dma_start(
        out=scr[0:1, 64:192].rearrange("o (p u) -> p (o u)", p=128), in_=xmx[:, 0:1]),
        inc="s_ex", n=16, waits=[("s_ve", v_xm)])
    v_sr = emit("gpsimd", lambda e: e.dma_start(out=row128[:], in_=scr[0:1, 64:192]),
                inc="s_gld", n=16, waits=[("s_ex", v_sw)])
    vl1 = emit("vector", lambda e: e.tensor_reduce(lmax[0:1, 0:1], row128[:], AX.X, OP.max),
               waits=[("s_gld", v_sr)], inc="s_ve")
    vl2 = emit("vector", lambda e: e.tensor_copy(out=lmax[0:1, 1:2], in_=lmax[0:1, 0:1]),
               waits=[("s_ve", vl1)], inc="s_ve")
    vl3 = emit("vector", lambda e: e.tensor_copy(out=lmax[0:1, 2:4], in_=lmax[0:1, 0:2]),
               waits=[("s_ve", vl2)], inc="s_ve")
    v_lm = emit("vector", lambda e: e.tensor_copy(out=lmax[0:1, 4:8], in_=lmax[0:1, 0:4]), inc="s_ve",
                waits=[("s_ve", vl3)])
    v_mxw = emit("gpsimd", lambda e: e.dma_start(out=mx_in.ap(), in_=lmax[:]),
                 inc="s_ex", n=16, waits=[("s_ve", v_lm)])
    emit("gpsimd", lambda e: e.collective_compute(
        "AllReduce", OP.max, replica_groups=[list(range(8))],
        ins=[mx_in.ap().opt()], outs=[mx_out.ap().opt()]),
        inc="s_cc", waits=[("s_ex", v_mxw)])
    V_MX = P.val("s_cc")
    v_gm = emit("gpsimd", lambda e: e.dma_start(out=lmax[0:1, 0:1], in_=mx_out.ap()[0:1, 0:1]),
                inc="s_gld", n=16, waits=[("s_cc", V_MX)])
    if debug:
        emit("gpsimd", lambda e: e.dma_start(out=dbg["gmax"], in_=lmax[:]), inc="s_st", n=16,
             waits=[("s_gld", v_gm)])
    vsx1 = emit("vector", lambda e: e.tensor_scalar(lmax[0:1, 1:2], lmax[0:1, 0:1], sgc[0:1, 6:7],
                                                    INV127, OP.mult, OP.mult),
                waits=[("s_gld", v_gm)], inc="s_ve")
    v_sx = emit("vector", lambda e: e.reciprocal(out=lmax[0:1, 2:3], in_=lmax[0:1, 1:2]), inc="s_ve",
                waits=[("s_ve", vsx1)])
    v_sxw = emit("gpsimd", lambda e: e.dma_start(out=scr[0:1, 8:10], in_=lmax[0:1, 1:3]),
                 inc="s_ex", n=16, waits=[("s_ve", v_sx)])
    emit("gpsimd", lambda e: e.dma_start(out=sx_bc[:], in_=scr[0:1, 8:9].broadcast_to([128, 1])),
         inc="s_gld", n=16, waits=[("s_ex", v_sxw)])
    v_sxb = emit("gpsimd", lambda e: e.dma_start(out=rx_bc[:], in_=scr[0:1, 9:10].broadcast_to([128, 1])),
                 inc="s_gld", n=16)

    v_tp_hist2 = [0, 0]
    for t in range(NTILE):
        tb = t % 2
        emit("vector", lambda e, t=t: e.tensor_scalar(rsq_sb[:], x1_sb[:, t, :], rx_bc[:], MAGIC, OP.mult, OP.add),
             waits=[("s_gld", v_sxb), ("s_tp", v_tp_hist2[tb]), ("s_se", v_x1_hist[t])])
        emit("vector", lambda e: e.tensor_scalar(rsq_sb[:], rsq_sb[:], MAGIC + 127.0, MAGIC - 128.0, OP.min, OP.max))
        v_q2 = emit("vector", lambda e, tb=tb: e.tensor_scalar(xq_bf[:, tb, :], rsq_sb[:], MAGIC, None, OP.subtract),
                    inc="s_ve")
        for dc in range(8):
            emit("sync", lambda e, t=t, tb=tb, dc=dc: e.dma_start(
                out=xqT[:, dc, t, :],
                in_=xq_bf[:, tb, dc * 128:(dc + 1) * 128], transpose=True),
                inc="s_tp", n=16, waits=[("s_ve", v_q2)])
        v_tp_hist2[tb] = P.val("s_tp")
        v_xqT_tile[t] = P.val("s_tp")
    v_xqT = P.val("s_tp")

    if ph_dump == "x1":
        tmp_f = nc.sbuf_tensor("dumpX_f", [128, 2048], F32)
        with tmp_f as tf:
            v_c = emit("vector", lambda e: e.tensor_copy(out=tf[:, 0:1024], in_=x1_sb[:, 0, :]),
                       inc="s_ve", waits=[("s_tp", v_xqT)])
            emit("vector", lambda e: e.tensor_copy(out=tf[:, 1024:2048], in_=x1_sb[:, 31, :]),
                 inc="s_ve")
            v_d1 = emit("gpsimd", lambda e: e.dma_start(out=y_out[0:128, 0:1024], in_=tf[:, 0:1024]),
                        inc="s_st", n=16, waits=[("s_ve", P.val("s_ve"))])
            emit("gpsimd", lambda e: e.dma_start(out=y_out[128:256, 0:1024], in_=tf[:, 1024:2048]),
                 inc="s_st", n=16)
            emit("gpsimd", lambda e: e.dma_start(out=y_out[256:257, 0:8], in_=lmax[0:1, :]),
                 inc="s_st", n=16)
            v_c2 = emit("vector", lambda e: e.tensor_copy(
                out=tf[:, 0:1024].rearrange("p (a bc) -> p a bc", a=2),
                in_=xqT[:, 0:2, 0:4, :].rearrange("p a b c -> p a (b c)")),
                inc="s_ve", waits=[("s_st", v_d1)])
            emit("gpsimd", lambda e: e.dma_start(out=y_out[384:512, 0:1024], in_=tf[:, 0:1024]),
                 inc="s_st", n=16, waits=[("s_ve", v_c2)])
            v_c3 = emit("vector", lambda e: e.tensor_copy(out=tf[:, 1024:2048], in_=r_sb[:, 0, :]),
                        inc="s_ve")
            emit("gpsimd", lambda e: e.dma_start(out=y_out[512:640, 0:1024], in_=tf[:, 1024:2048]),
                 inc="s_st", n=16, waits=[("s_ve", v_c3)])
            v_c4 = emit("vector", lambda e: e.tensor_copy(out=tf[:, 0:2], in_=t_rstd[:]), inc="s_ve",
                        waits=[("s_st", P.val("s_st"))])
            emit("vector", lambda e: e.tensor_copy(out=tf[:, 2:6], in_=st2[:]))
            v_c5 = emit("vector", lambda e: e.tensor_copy(out=tf[:, 6:70], in_=xmx[:]), inc="s_ve")
            emit("gpsimd", lambda e: e.dma_start(out=y_out[640:768, 0:70], in_=tf[:, 0:70]),
                 inc="s_st", n=16, waits=[("s_ve", v_c5)])
        P.wait("gpsimd", "s_st", P.val("s_st"))
        return _finalize(nc, P, dbg)

    chunk_slots = []
    for sslot in range(PREP_SLOTS):
        for r in range(8):
            ci = r + 8 * sslot
            if ci < NCH:
                chunk_slots.append((ci, r, sslot))
    chunk_slots.sort()

    if phases < 4:
        P.wait("gpsimd", "s_cc", P.val("s_cc"))
        P.wait("gpsimd", "s_tp", P.val("s_tp"))
        P.wait("gpsimd", "s_ve", P.val("s_ve"))
        return _finalize(nc, P, dbg)
    v_spill_prev = 0
    v_gse_slot = [0, 0]
    v_gu_hist = [0, 0]
    v_httail_prev = 0
    for idx, (ci, r_, s_) in enumerate(chunk_slots):
        cb = idx % 2
        grow = r_ * CCR + s_ * 1024
        urow = r_ * CCR + (PREP_SLOTS + s_) * 1024
        emit("sync", lambda e, grow=grow, cb=cb: e.dma_start(
            out=wgt_sb[:, cb, :, :],
            in_=cc_out.ap()[grow:grow + 1024, :].rearrange("(c p) i -> p c i", p=128)),
            inc="s_ld", n=16,
            waits=[("s_cc", max(V_AG, V_BAR)), ("s_mm", v_gu_hist[cb])])
        v_wu = emit("sync", lambda e, urow=urow, cb=cb: e.dma_start(
            out=wut_sb[:, cb, :, :],
            in_=cc_out.ap()[urow:urow + 1024, :].rearrange("(c p) i -> p c i", p=128)),
            inc="s_ld", n=16)
        for tt in range(8):
            goff = (tt % 2) * 1024

            def gu_group(e, cb, tt, goff, which):
                w = wgt_sb if which == 0 else wut_sb
                for hc in range(8):
                    inst = e.matmul(psB[:, goff + which * 512: goff + which * 512 + 512],
                                    w[:, cb, hc, :],
                                    xqT[:, hc, 4 * tt:4 * tt + 4, :].rearrange("p a b -> p (a b)"),
                                    start=(hc == 0), stop=(hc == 7))
                return inst
            emit("tensor", lambda e, cb=cb, tt=tt, goff=goff: gu_group(e, cb, tt, goff, 0),
                 inc="s_mm", waits=[("s_ld", v_wu), ("s_tp", v_xqT_tile[4 * tt + 3]),
                                    ("s_se", v_gse_slot[tt % 2])])
            v_gu = emit("tensor", lambda e, cb=cb, tt=tt, goff=goff: gu_group(e, cb, tt, goff, 1),
                        inc="s_mm")
            silu_w = [("s_mm", v_gu), ("s_gld", v_sxb), ("s_ve", v_httail_prev)]
            if debug and idx == 0 and tt == 1:
                silu_w.append(("s_st", P.val("s_st")))
            emit("scalar", lambda e, goff=goff: e.activation(sg_sb[:], psB[:, goff:goff + 512],
                                                             AF.Silu, scale=sx_bc[:]),
                 waits=silu_w)
            v_gse = emit("scalar", lambda e, goff=goff: e.activation(
                su_sb[:], psB[:, goff + 512:goff + 1024], AF.Identity, scale=sx_bc[:]),
                inc="s_se")
            v_gse_slot[tt % 2] = v_gse
            if debug and idx == 0 and tt == 0:
                v_cap = emit("vector", lambda e: e.tensor_copy(out=rsq_sb[:, 0:512], in_=psB[:, 0:512]),
                             inc="s_ve", waits=[("s_se", v_gse)])
                v_cap2 = emit("vector", lambda e: e.tensor_copy(out=x2_sb[:, 0, 0:512], in_=psB[:, 512:1024]),
                              inc="s_ve")
                emit("gpsimd", lambda e: e.dma_start(out=dbg["gps"], in_=rsq_sb[:, 0:512]),
                     inc="s_st", n=16, waits=[("s_ve", v_cap2)])
                emit("gpsimd", lambda e: e.dma_start(out=dbg["ups"], in_=x2_sb[:, 0, 0:512]),
                     inc="s_st", n=16)
                emit("gpsimd", lambda e: e.dma_start(out=dbg["sg0"], in_=sg_sb[:]),
                     inc="s_st", n=16, waits=[("s_se", v_gse)])
                emit("gpsimd", lambda e: e.dma_start(out=dbg["su0"], in_=su_sb[:]),
                     inc="s_st", n=16)
                emit("gpsimd", lambda e, cb=cb: e.dma_start(out=dbg["wgt0"], in_=wgt_sb[:, cb, :, :]),
                     inc="s_st", n=16)
                emit("gpsimd", lambda e: e.dma_start(out=dbg["xqT0"], in_=xqT[:, :, 0, :]),
                     inc="s_st", n=16)
            col = 2 * (idx * 8 + tt)
            emit("vector", lambda e, tt=tt: e.tensor_tensor(
                h_bf[:, tt * 512:(tt + 1) * 512], sg_sb[:], su_sb[:], OP.mult),
                waits=[("s_se", v_gse), ("s_st", v_spill_prev)])
            emit("vector", lambda e, tt=tt, col=col: e.tensor_scalar(
                rsq_sb[:, 0:512], h_bf[:, tt * 512:(tt + 1) * 512], 1.0, None, OP.mult, OP.max,
                accum_out=hmx[:, col:col + 1]))
            v_httail_prev = emit("vector", lambda e, tt=tt, col=col: e.tensor_scalar(
                rsq_sb[:, 0:512], h_bf[:, tt * 512:(tt + 1) * 512], -1.0, None, OP.mult, OP.max,
                accum_out=hmx[:, col + 1:col + 2]), inc="s_ve")
        v_gu_hist[cb] = P.val("s_mm")
        v_hprev = P.val("s_ve")
        emit("gpsimd", lambda e, ci=ci: e.dma_start(
            out=h_spill[ci, :, :], in_=h_bf[:]),
            inc="s_st", n=16, waits=[("s_ve", v_hprev)])
        v_spill_prev = P.val("s_st")
        if debug and idx == 0:
            emit("gpsimd", lambda e: e.dma_start(out=dbg["h0"], in_=h_bf[:, 0:512]),
                 inc="s_st", n=16, waits=[("s_ve", v_hprev)])
            v_spill_prev = P.val("s_st")
    v_hlast = P.val("s_ve")
    v_gu_end = P.val("s_mm")

    v_hm = emit("vector", lambda e: e.tensor_reduce(hmx[:, 0:1], hmx[:], AX.X, OP.max), inc="s_ve",
                waits=[("s_ve", v_hlast)])
    v_hw = emit("gpsimd", lambda e: e.dma_start(
        out=scr[0:1, 64:192].rearrange("o (p u) -> p (o u)", p=128), in_=hmx[:, 0:1]),
        inc="s_ex", n=16, waits=[("s_ve", v_hm)])
    v_hr = emit("gpsimd", lambda e: e.dma_start(out=row128[:], in_=scr[0:1, 64:192]),
                inc="s_gld", n=16, waits=[("s_ex", v_hw)])
    vh1 = emit("vector", lambda e: e.tensor_reduce(lmax[0:1, 0:1], row128[:], AX.X, OP.max),
               waits=[("s_gld", v_hr)], inc="s_ve")
    vh2 = emit("vector", lambda e: e.tensor_copy(out=lmax[0:1, 1:2], in_=lmax[0:1, 0:1]),
               waits=[("s_ve", vh1)], inc="s_ve")
    vh3 = emit("vector", lambda e: e.tensor_copy(out=lmax[0:1, 2:4], in_=lmax[0:1, 0:2]),
               waits=[("s_ve", vh2)], inc="s_ve")
    v_lh = emit("vector", lambda e: e.tensor_copy(out=lmax[0:1, 4:8], in_=lmax[0:1, 0:4]), inc="s_ve",
                waits=[("s_ve", vh3)])
    v_hmw = emit("gpsimd", lambda e: e.dma_start(out=hm_in.ap(), in_=lmax[:]),
                 inc="s_ex", n=16, waits=[("s_ve", v_lh)])
    emit("gpsimd", lambda e: e.collective_compute(
        "AllReduce", OP.max, replica_groups=[list(range(8))],
        ins=[hm_in.ap().opt()], outs=[hm_out.ap().opt()]),
        inc="s_cc", waits=[("s_ex", v_hmw)])
    V_HM = P.val("s_cc")
    v_ghm = emit("gpsimd", lambda e: e.dma_start(out=lmax[0:1, 0:1], in_=hm_out.ap()[0:1, 0:1]),
                 inc="s_gld", n=16, waits=[("s_cc", V_HM)])
    if debug:
        emit("gpsimd", lambda e: e.dma_start(out=dbg["hmax"], in_=lmax[:]), inc="s_st", n=16,
             waits=[("s_gld", v_ghm)])
    vsh1 = emit("vector", lambda e: e.tensor_scalar(lmax[0:1, 1:2], lmax[0:1, 0:1], sgc[0:1, 7:8],
                                                    INV127, OP.mult, OP.mult),
                waits=[("s_gld", v_ghm)], inc="s_ve")
    v_sh = emit("vector", lambda e: e.reciprocal(out=lmax[0:1, 2:3], in_=lmax[0:1, 1:2]), inc="s_ve",
                waits=[("s_ve", vsh1)])
    v_shw = emit("gpsimd", lambda e: e.dma_start(out=scr[0:1, 10:12], in_=lmax[0:1, 1:3]),
                 inc="s_ex", n=16, waits=[("s_ve", v_sh)])
    emit("gpsimd", lambda e: e.dma_start(out=sh_bc[:], in_=scr[0:1, 10:11].broadcast_to([128, 1])),
         inc="s_gld", n=16, waits=[("s_ex", v_shw)])
    v_shb = emit("gpsimd", lambda e: e.dma_start(out=rh_bc[:], in_=scr[0:1, 11:12].broadcast_to([128, 1])),
                 inc="s_gld", n=16)

    es2a.close()
    sb2b = mk_sb(es2b)
    wdT_sb = sb2b("wdT_sb", [128, NCH, 1024], BF16)
    h2_sb = sb2b("h2_sb", [128, NCH, 512], BF16)
    kh_bf = sb2b("kh_bf", [128, NCH, 512], BF16)
    z_sb = sb2b("z_sb", [128, 1024])
    zsq_sb = sb2b("zsq_sb", [128, 1024])
    y_sb = sb2b("y_sb", [128, 2, 1024])

    ccv = cc_out.ap().rearrange("(r q) j -> r q j", r=8)
    for ci, _, _ in chunk_slots:
        emit("gpsimd", lambda e, ci=ci: e.dma_start(
            out=wdT_sb[:, ci, :].rearrange("p (r j) -> p r j", r=8),
            in_=ccv[:, 6 * 1024 + ci * 128: 6 * 1024 + (ci + 1) * 128, :].transpose([1, 0, 2])),
            inc="s_gld", n=16,
            waits=[("s_cc", V_AG), ("s_mm", v_gu_end), ("s_ve", v_hlast),
                   ("s_se", P.val("s_se"))])
    v_wd = P.val("s_gld")

    v_all_spill = v_spill_prev
    v_yprev = [0, 0]
    v_dn_hist = 0
    v_evac_slot = [0, 0, 0, 0]
    v_yt_prev = 0
    v_quant_prev = 0
    for g in range(8):
        v_hl = emit("gpsimd", lambda e, g=g: e.dma_start(
            out=h2_sb[:],
            in_=h_spill[:, :, g * 512:(g + 1) * 512].transpose([1, 0, 2])),
            inc="s_gld", n=16,
            waits=[("s_st", v_all_spill), ("s_ve", v_quant_prev),
                   ("s_mm", v_gu_end)])
        P.wait("vector", "s_gld", max(v_hl, v_shb))
        P.wait("vector", "s_mm", v_dn_hist)
        for sub in range(11):
            emit("vector", lambda e, sub=sub: e.tensor_scalar(
                zsq_sb[:], h2_sb[:, 2 * sub:2 * sub + 2, :].rearrange("p c t -> p (c t)"),
                rh_bc[:], MAGIC, OP.mult, OP.add))
            emit("vector", lambda e: e.tensor_scalar(
                zsq_sb[:], zsq_sb[:], MAGIC + 127.0, MAGIC - 128.0, OP.min, OP.max))
            last = (sub == 10)
            emit("vector", lambda e, sub=sub: e.tensor_scalar(
                kh_bf[:, 2 * sub:2 * sub + 2, :].rearrange("p c t -> p (c t)"), zsq_sb[:],
                MAGIC, None, OP.subtract), inc="s_ve" if last else None)
        v_quant_prev = P.val("s_ve")
        v_kh = v_quant_prev
        if debug and g == 0:
            emit("gpsimd", lambda e: e.dma_start(out=dbg["kh"], in_=kh_bf[:]),
                 inc="s_st", n=16, waits=[("s_ve", v_kh)])
        for tc in range(4):
            t_glob = g * 4 + tc
            for hh in range(2):
                slot = (tc * 2 + hh) % 4

                def dn_group(e, tc=tc, hh=hh, slot=slot):
                    for ci in range(NCH):
                        inst = e.matmul(psA[:, slot * 512:(slot + 1) * 512],
                                        kh_bf[:, ci, tc * 128:(tc + 1) * 128],
                                        wdT_sb[:, ci, hh * 512:(hh + 1) * 512],
                                        start=(ci == 0), stop=(ci == NCH - 1))
                    return inst
                v_dn = emit("tensor", dn_group, inc="s_mm",
                            waits=[("s_ve", max(v_kh, v_evac_slot[slot])), ("s_gld", v_wd)])
                emit("vector", lambda e, t_glob=t_glob, hh=hh, slot=slot: e.scalar_tensor_tensor(
                    z_sb[:, hh * 512:(hh + 1) * 512], psA[:, slot * 512:(slot + 1) * 512],
                    sh_bc[:], x1_sb[:, t_glob, hh * 512:(hh + 1) * 512], OP.mult, OP.add,
                    accum_out=st2[:, hh:hh + 1]),
                    inc="s_ve",
                    waits=[("s_mm", v_dn), ("s_gld", v_shb), ("s_se", v_yt_prev)])
                v_evac_slot[slot] = P.val("s_ve")
            w1 = emit("vector", lambda e: e.tensor_tensor(st2[:, 0:1], st2[:, 0:1], st2[:, 1:2], OP.add),
                      waits=[("s_ve", P.val("s_ve"))], inc="s_ve")
            v_zacc = emit("vector", lambda e: e.scalar_tensor_tensor(
                zsq_sb[:], z_sb[:], 0.0, z_sb[:], OP.add, OP.mult, accum_out=st2[:, 1:2]),
                inc="s_ve")
            w2 = emit("vector", lambda e: e.tensor_scalar(st2[:, 0:1], st2[:, 0:1], float(1.0 / D), None, OP.mult),
                      waits=[("s_ve", v_zacc)], inc="s_ve")
            w3 = emit("vector", lambda e: e.tensor_tensor(st2[:, 3:4], st2[:, 0:1], st2[:, 0:1], OP.mult),
                      waits=[("s_ve", w2)], inc="s_ve")
            w4 = emit("vector", lambda e: e.scalar_tensor_tensor(st2[:, 2:3], st2[:, 1:2], float(1.0 / D),
                                                                 st2[:, 3:4], OP.mult, OP.subtract),
                      waits=[("s_ve", w3)], inc="s_ve")
            w5 = emit("vector", lambda e: e.tensor_scalar(st2[:, 2:3], st2[:, 2:3], EPS_LN, None, OP.add),
                      waits=[("s_ve", w4)], inc="s_ve")
            v_v2 = emit("vector", lambda e: e.reciprocal(out=st2[:, 2:3], in_=st2[:, 2:3]), inc="s_ve",
                        waits=[("s_ve", w5)])
            v_rs2 = emit("scalar", lambda e: e.activation(t_rstd[:, 0:1], st2[:, 2:3], AF.Sqrt),
                         inc="s_se", waits=[("s_ve", v_v2)])
            v_b2 = emit("vector", lambda e: e.tensor_scalar(t_rstd[:, 1:2], st2[:, 0:1], t_rstd[:, 0:1],
                                                            -1.0, OP.mult, OP.mult),
                        inc="s_ve", waits=[("s_se", v_rs2)])
            yb = t_glob % 2
            v_yt = emit("scalar", lambda e, yb=yb: e.activation(
                y_sb[:, yb, :], z_sb[:], AF.Identity, bias=t_rstd[:, 1:2], scale=t_rstd[:, 0:1]),
                inc="s_se", waits=[("s_se", v_rs2), ("s_ve", v_b2), ("s_st", v_yprev[yb])])
            v_yt_prev = v_yt
            if debug and t_glob == 0:
                emit("gpsimd", lambda e: e.dma_start(out=dbg["z0"], in_=z_sb[:]),
                     inc="s_st", n=16, waits=[("s_ve", v_b2)])
            emit("gpsimd", lambda e, t_glob=t_glob, yb=yb: e.dma_start(
                out=y_out[t_glob * 128:(t_glob + 1) * 128, :], in_=y_sb[:, yb, :]),
                inc="s_st", n=16, waits=[("s_se", v_yt)])
            v_yprev[yb] = P.val("s_st")
        v_dn_hist = P.val("s_mm")

    if debug in ("lite", "scr", "scr_nowrite"):
        lt_scr = nc.dram_tensor("lt_scr", [1, 256], F32, kind="ExternalOutput").ap()
        lt_aloc = (nc.dram_tensor("lt_aloc", [TPC, D], BF16, kind="ExternalOutput").ap()
                   if debug == "lite" else None)
        if debug == "lite":
            lt_x1 = nc.dram_tensor("lt_x1", [128, NTILE, 1024], BF16, kind="ExternalOutput").ap()
            lt_h = nc.dram_tensor("lt_h", [NCH, 128, TPC], BF16, kind="ExternalOutput").ap()
            lt_ax = nc.dram_tensor("lt_ax", [S, D], BF16, kind="ExternalOutput").ap()
        v_pre = P.val("s_st")
        if debug != "scr_nowrite":
            v_l1 = emit("gpsimd", lambda e: e.dma_start(out=row128[:], in_=scr[0:1, 0:128]),
                        inc="s_gld", n=16,
                        waits=[("s_gld", P.val("s_gld")), ("s_st", v_pre)])
            emit("gpsimd", lambda e: e.dma_start(out=lt_scr[0:1, 0:128], in_=row128[:]), inc="s_st", n=16,
                 waits=[("s_gld", v_l1)])
        if debug == "lite":
            emit("gpsimd", lambda e: e.dma_start(out=lt_aloc, in_=a_loc), inc="s_st", n=16)
            emit("gpsimd", lambda e: e.dma_start(out=lt_x1, in_=x1_sb[:]), inc="s_st", n=16,
                 waits=[("s_se", P.val("s_se"))])
            emit("gpsimd", lambda e: e.dma_start(out=lt_h, in_=h_spill), inc="s_st", n=16)
            emit("gpsimd", lambda e: e.dma_start(out=lt_ax, in_=a_xchg), inc="s_st", n=16)
    v_fin = P.val("s_st")
    P.wait("gpsimd", "s_st", v_fin)
    return _finalize(nc, P, dbg)


def _finalize(nc, P, dbg):
    with nc.Block() as block:
        @block.sync
        def _(e):
            for fn in P.ops["sync"]:
                fn(e)

        @block.gpsimd
        def _(e):
            for fn in P.ops["gpsimd"]:
                fn(e)

        @block.vector
        def _(e):
            for fn in P.ops["vector"]:
                fn(e)

        @block.scalar
        def _(e):
            for fn in P.ops["scalar"]:
                fn(e)

        @block.tensor
        def _(e):
            for fn in P.ops["tensor"]:
                fn(e)

    return nc, dbg


def make_in_maps(inputs):
    x = np.ascontiguousarray(inputs["L0_attn_x"], dtype=np.float32)
    Wq = np.asarray(inputs["L0_attn_Wq"], dtype=np.float32)
    Wk = np.asarray(inputs["L0_attn_Wk"], dtype=np.float32)
    Wv = np.asarray(inputs["L0_attn_Wv"], dtype=np.float32)
    bq = np.asarray(inputs["L0_attn_bq"], dtype=np.float32)
    bk = np.asarray(inputs["L0_attn_bk"], dtype=np.float32)
    bv = np.asarray(inputs["L0_attn_bv"], dtype=np.float32)
    beta = np.asarray(inputs["L0_attn_beta"], dtype=np.float32)
    clipx = np.asarray(inputs["L0_ffn_clip_x"], dtype=np.float32)
    clipd = np.asarray(inputs["L0_ffn_clip_down"], dtype=np.float32)
    wg = np.asarray(inputs["L0_ffn_w_gate"], dtype=np.float32)
    wu = np.asarray(inputs["L0_ffn_w_up"], dtype=np.float32)
    wd = np.asarray(inputs["L0_ffn_w_down"], dtype=np.float32)

    tril = np.tril(np.ones((128, 128))).astype(ml_dtypes.bfloat16)
    in_maps = []
    for c in range(8):
        b, p = c // 2, c % 2
        fs = slice(p * 512, (p + 1) * 512)

        def prep_rows(w, c=c):
            rows = []
            for sslot in range(PREP_SLOTS):
                ci = c + 8 * sslot
                if ci < NCH:
                    rows.append(w[ci * 128:(ci + 1) * 128, :])
                else:
                    rows.append(np.zeros((128, D), np.float32))
            return np.ascontiguousarray(np.concatenate(rows, axis=0))
        in_maps.append({
            "x": np.ascontiguousarray(x[b]),
            "x2": np.ascontiguousarray(x[b, p * TPC:(p + 1) * TPC, :]),
            "wq": np.ascontiguousarray(Wq[:, fs]),
            "wk": np.ascontiguousarray(Wk[:, fs]),
            "wv": np.ascontiguousarray(Wv[:, fs]),
            "bq": np.ascontiguousarray(bq[fs]).reshape(1, 512),
            "bk": np.ascontiguousarray(bk[fs]).reshape(1, 512),
            "bv": np.ascontiguousarray(bv[fs]).reshape(1, 512),
            "beta": np.ascontiguousarray(beta[p * 4:(p + 1) * 4]).reshape(1, 4),
            "clipx": np.ascontiguousarray(clipx).reshape(1, 1),
            "clipd": np.ascontiguousarray(clipd).reshape(1, 1),
            "tril": tril,
            "wg_prep": prep_rows(wg),
            "wu_prep": prep_rows(wu),
            "wd_prep": np.ascontiguousarray(wd[c * 128:(c + 1) * 128, :]),
        })
    return in_maps


def kernel(**inputs):
    in_maps = make_in_maps(inputs)
    nc, _ = build_program(debug=False)
    res = run_bass_kernel_spmd(nc, in_maps, core_ids=list(range(8)))
    out = np.empty((B, S, D), dtype=np.float32)
    for c in range(8):
        b, p = c // 2, c % 2
        out[b, p * TPC:(p + 1) * TPC, :] = res.results[c]["y"]
    return out


# version 05
# speedup vs baseline: 1.1501x; 1.0450x over previous
import numpy as np
import ml_dtypes
from contextlib import ExitStack

import concourse.bass as bass
import concourse.mybir as mybir
from concourse.bass_utils import run_bass_kernel_spmd

F32 = mybir.dt.float32
BF16 = mybir.dt.bfloat16
I8 = mybir.dt.int8
AF = mybir.ActivationFunctionType
OP = mybir.AluOpType
AX = mybir.AxisListType
ds = bass.ds

f32 = np.float32

B, S, D = 4, 8192, 1024
HPC = 4
NSEG = 16
TPC = 4096
NTILE = 32
II = 2816
NCH = 22
PREP_SLOTS = 3
EPS_ATT = f32(1e-6)
EPS_LN = 1e-5
MAGIC = 12582912.0
INV_SQRT_DK = f32(1.0 / np.sqrt(128.0))
INV127 = 1.0 / 127.0
CCR = PREP_SLOTS * 1024 * 2 + II

ENGS = ("sync", "gpsimd", "vector", "scalar", "tensor")


class Prog:
    def __init__(self):
        self.ops = {e: [] for e in ENGS}
        self.ctr = {}
        self.waited = {e: {} for e in ENGS}
        self.sems = {}

    def inc(self, sem, n=1):
        self.ctr[sem] = self.ctr.get(sem, 0) + n
        return self.ctr[sem]

    def val(self, sem):
        return self.ctr.get(sem, 0)

    def wait(self, eng, sem, v=None):
        if v is None:
            v = self.val(sem)
        if v <= 0:
            return
        if self.waited[eng].get(sem, -1) >= v:
            return
        self.waited[eng][sem] = v

        def fn(e, sem=sem, v=v):
            e.wait_ge(self.sems[sem], v)
        self.ops[eng].append(fn)


def build_program(debug=False, phases=4, seg_detail="full"):
    ph_dump = phases if isinstance(phases, str) else None
    if ph_dump is not None:
        phases = 3.5 if ph_dump == "x1" else 2.5
    nc = bass.Bass("TRN2", target_bir_lowering=False, detect_race_conditions=False)
    P = Prog()

    x_in = nc.dram_tensor("x", [S, D], F32, kind="ExternalInput").ap()
    wq_in = nc.dram_tensor("wq", [D, 512], F32, kind="ExternalInput").ap()
    wk_in = nc.dram_tensor("wk", [D, 512], F32, kind="ExternalInput").ap()
    wv_in = nc.dram_tensor("wv", [D, 512], F32, kind="ExternalInput").ap()
    bq_in = nc.dram_tensor("bq", [1, 512], F32, kind="ExternalInput").ap()
    bk_in = nc.dram_tensor("bk", [1, 512], F32, kind="ExternalInput").ap()
    bv_in = nc.dram_tensor("bv", [1, 512], F32, kind="ExternalInput").ap()
    beta_in = nc.dram_tensor("beta", [1, HPC], F32, kind="ExternalInput").ap()
    clipx_in = nc.dram_tensor("clipx", [1, 1], F32, kind="ExternalInput").ap()
    clipd_in = nc.dram_tensor("clipd", [1, 1], F32, kind="ExternalInput").ap()
    tril_in = nc.dram_tensor("tril", [128, 128], BF16, kind="ExternalInput").ap()
    wgp_in = nc.dram_tensor("wg_prep", [PREP_SLOTS * 128, D], F32, kind="ExternalInput").ap()
    wup_in = nc.dram_tensor("wu_prep", [PREP_SLOTS * 128, D], F32, kind="ExternalInput").ap()
    wdp_in = nc.dram_tensor("wd_prep", [128, II], F32, kind="ExternalInput").ap()

    x2_in = nc.dram_tensor("x2", [TPC, D], F32, kind="ExternalInput").ap()
    y_out = nc.dram_tensor("y", [TPC, D], F32, kind="ExternalOutput").ap()

    a_xchg = nc.dram_tensor("a_xchg", [S, D], BF16, kind="Internal", addr_space="Shared").ap()
    ax_priv = nc.dram_tensor("ax_priv", [NSEG, 4, 128, 512], BF16, kind="Internal").ap()
    a_loc = nc.dram_tensor("a_loc", [TPC, D], BF16, kind="Internal").ap()
    h_spill = nc.dram_tensor("h_spill", [NCH, 128, TPC], BF16, kind="Internal").ap()
    scr = nc.dram_tensor("scr", [1, 256], F32, kind="Internal").ap()
    cc_in = nc.dram_tensor("cc_in", [CCR, 128], BF16, kind="Internal")
    cc_out = nc.dram_tensor("cc_out", [8 * CCR, 128], BF16, kind="Internal")
    bar_in = nc.dram_tensor("bar_in", [1, 8], F32, kind="Internal")
    bar_out = nc.dram_tensor("bar_out", [1, 8], F32, kind="Internal")
    mx_in = nc.dram_tensor("mx_in", [1, 8], F32, kind="Internal")
    mx_out = nc.dram_tensor("mx_out", [1, 8], F32, kind="Internal")
    hm_in = nc.dram_tensor("hm_in", [1, 8], F32, kind="Internal")
    hm_out = nc.dram_tensor("hm_out", [1, 8], F32, kind="Internal")

    dbg = {}
    if debug:
        def dbg_t(name, shape, dt=F32):
            dbg[name] = nc.dram_tensor("dbg_" + name, shape, dt, kind="ExternalOutput").ap()
        dbg_t("qT", [128, HPC, 512])
        dbg_t("sqT", [128, HPC, 512])
        dbg_t("skT", [128, HPC, 512])
        dbg_t("v", [128, 4, HPC, 130])
        dbg_t("p", [128, 4, 512])
        dbg_t("M", [128, HPC, 130])
        dbg_t("astage", [128, 4, 512])
        dbg_t("r0", [128, 1024])
        dbg_t("gmax", [1, 8])
        dbg_t("h0", [128, 512])
        dbg_t("sg0", [128, 512])
        dbg_t("su0", [128, 512])
        dbg_t("gps", [128, 512])
        dbg_t("ups", [128, 512])
        dbg_t("xqT0", [128, 8, 128])
        dbg_t("wgt0", [128, 8, 128])
        dbg_t("hmax", [1, 8])
        dbg_t("kh", [128, NCH, 512])
        dbg_t("z0", [128, 1024])

    es0 = ExitStack()
    es1 = ExitStack()
    es2p = ExitStack()
    es2a = ExitStack()
    es2b = ExitStack()

    def mk_sb(stack):
        def sbf(name, shape, dt=F32):
            return stack.enter_context(nc.sbuf_tensor(name, shape, dt))
        return sbf

    sb0, sb1 = mk_sb(es0), mk_sb(es1)

    sgc = sb0("sgc", [1, 8])
    g_bc = sb0("g_bc", [128, HPC])
    gm1_bc = sb0("gm1_bc", [128, HPC])
    ones_r = sb0("ones_r", [1, 512], BF16)
    tril_bf = sb0("tril_bf", [128, 128], BF16)
    lmax = sb0("lmax", [1, 8])
    row128 = sb0("row128", [1, 128])
    sx_bc = sb0("sx_bc", [128, 1])
    rx_bc = sb0("rx_bc", [128, 1])
    sh_bc = sb0("sh_bc", [128, 1])
    rh_bc = sb0("rh_bc", [128, 1])
    st2 = sb0("st2", [128, 4])
    t_rstd = sb0("t_rstd", [128, 2])
    xmx = sb0("xmx", [128, 2 * NTILE])
    hmx = sb0("hmx", [128, 2 * 8 * NCH])

    psA = es0.enter_context(nc.psum_tensor("psA", [128, 2048], F32))
    psB = es0.enter_context(nc.psum_tensor("psB", [128, 2048], F32))

    for sn in ["s_ld", "s_gld", "s_tp", "s_mm", "s_ve", "s_se", "s_st", "s_cc", "s_ex"]:
        P.sems[sn] = es0.enter_context(nc.semaphore(sn))
    SEM = P.sems

    wq_bf = sb1("wq_bf", [128, 8, 512], BF16)
    wk_bf = sb1("wk_bf", [128, 8, 512], BF16)
    wv_bf = sb1("wv_bf", [128, 8, 512], BF16)
    bq_bf = sb1("bq_bf", [1, 512], BF16)
    bk_bf = sb1("bk_bf", [1, 512], BF16)
    bv_bf = sb1("bv_bf", [1, 512], BF16)
    x_tm = sb1("x_tm", [128, 2, 4, 1024], BF16)
    x_fm = sb1("x_fm", [128, 2, 4, 8, 128], BF16)
    q_bf = sb1("q_bf", [128, HPC, 512], BF16)
    k_bf = sb1("k_bf", [128, HPC, 512], BF16)
    sq_bf = sb1("sq_bf", [128, HPC, 512], BF16)
    sk_bf = sb1("sk_bf", [128, HPC, 512], BF16)
    v_bf = sb1("v_bf", [128, 4, HPC, 130], BF16)
    sk_tm = sb1("sk_tm", [128, HPC, 4, 128], BF16)
    p_bf = sb1("p_bf", [128, 4, 512], BF16)
    pT_bf = sb1("pT_bf", [128, 4, 4, 128], BF16)
    vd_bf = sb1("vd_bf", [128, 4, 130], BF16)
    M_f32 = sb1("M_f32", [128, HPC, 130])
    M_bf = sb1("M_bf", [128, HPC, 130], BF16)
    t_scr1 = sb1("t_scr1", [128, 2048])
    t_scr2 = sb1("t_scr2", [128, 2048])
    t_den = sb1("t_den", [128, 8])
    t_rq = sb1("t_rq", [128, 8])
    t_rqg = sb1("t_rqg", [128, 4])
    t_rkn = sb1("t_rkn", [128, 4])
    t_rd = sb1("t_rd", [128, 4])
    t_r2g = sb1("t_r2g", [128, 4])
    t_nm = sb1("t_nm", [128, 4])
    a_stage = sb1("a_stage", [128, 2, 4, 512], BF16)
    wp_sb = sb1("wp_sb", [128, PREP_SLOTS, 1024])
    wup_sb = sb1("wup_sb", [128, PREP_SLOTS, 1024])
    wdp_sb = sb1("wdp_sb", [128, II])
    wqz = sb1("wqz", [128, 1024], BF16)
    wdz = sb1("wdz", [128, II], BF16)
    wT_sb = sb1("wT_sb", [128, 8, 128], BF16)
    wdT_s = sb1("wdT_s", [128, NCH, 128], BF16)
    t_rm = sb1("t_rm", [128, 4])

    def emit(eng_name, fn, inc=None, n=1, waits=()):
        for (sm, v) in waits:
            P.wait(eng_name, sm, v)
        tgt = P.inc(inc, n) if inc else None

        def run(e, fn=fn, inc=inc, n=n):
            inst = fn(e)
            if inc is not None and inst is not None:
                inst.then_inc(SEM[inc], n)
        P.ops[eng_name].append(run)
        return tgt

    v_wgp = emit("sync", lambda e: e.dma_start(out=wp_sb[:], in_=wgp_in.rearrange("(c p) d -> p c d", p=128)), inc="s_ld", n=16)
    v_wup = emit("sync", lambda e: e.dma_start(out=wup_sb[:], in_=wup_in.rearrange("(c p) d -> p c d", p=128)), inc="s_ld", n=16)
    v_wdp = emit("sync", lambda e: e.dma_start(out=wdp_sb[:], in_=wdp_in), inc="s_ld", n=16)
    for (wi, wb) in ((wq_in, wq_bf), (wk_in, wk_bf), (wv_in, wv_bf)):
        emit("gpsimd", lambda e, wi=wi, wb=wb: e.dma_start(
            out=wb[:], in_=wi.rearrange("(c p) m -> p c m", p=128)), inc="s_gld", n=16)
    for (bi, bb) in ((bq_in, bq_bf), (bk_in, bk_bf), (bv_in, bv_bf)):
        emit("gpsimd", lambda e, bi=bi, bb=bb: e.dma_start(out=bb[:], in_=bi), inc="s_gld", n=16)
    v_wcast = P.val("s_gld")
    emit("sync", lambda e: e.dma_start(out=sgc[0:1, 0:4], in_=beta_in[0:1, :]), inc="s_ld", n=16)
    emit("sync", lambda e: e.dma_start(out=sgc[0:1, 4:5], in_=clipx_in[0:1, :]), inc="s_ld", n=16)
    emit("sync", lambda e: e.dma_start(out=sgc[0:1, 5:6], in_=clipd_in[0:1, :]), inc="s_ld", n=16)
    emit("sync", lambda e: e.dma_start(out=tril_bf[:], in_=tril_in), inc="s_ld", n=16)
    v_small = P.val("s_ld")
    emit("vector", lambda e: e.memset(ones_r[:], 1.0))
    emit("vector", lambda e: e.memset(p_bf.ap().rearrange("p a b -> p (a b)"), 0.0))
    emit("vector", lambda e: e.memset(v_bf.ap().rearrange("p a b c -> p (a b c)"), 1.0))
    emit("vector", lambda e: e.memset(vd_bf.ap().rearrange("p a b -> p (a b)"), 1.0))
    emit("vector", lambda e: e.memset(M_f32.ap().rearrange("p h c -> p (h c)"), 0.0))
    emit("vector", lambda e: e.memset(M_bf.ap().rearrange("p h c -> p (h c)"), 0.0), inc="s_ve")
    v_const = P.val("s_ve")
    emit("scalar", lambda e: e.activation(sgc[0:1, 6:7], sgc[0:1, 4:5], AF.Sigmoid),
         waits=[("s_ld", v_small)])
    emit("scalar", lambda e: e.activation(sgc[0:1, 7:8], sgc[0:1, 5:6], AF.Sigmoid))
    v_sig = emit("scalar", lambda e: e.activation(sgc[0:1, 0:4], sgc[0:1, 0:4], AF.Sigmoid), inc="s_se")
    v_gw = emit("gpsimd", lambda e: e.dma_start(out=scr[0:1, 0:8], in_=sgc[0:1, 0:8]),
                inc="s_ex", n=16, waits=[("s_se", v_sig)])
    v_gbc = emit("gpsimd", lambda e: e.dma_start(out=g_bc[:], in_=scr[0:1, 0:4].broadcast_to([128, 4])),
                 inc="s_ex", n=16, waits=[("s_ex", v_gw)])
    v_gm1 = emit("vector", lambda e: e.tensor_scalar(gm1_bc[:], g_bc[:], -1.0, 1.0, OP.mult, OP.add),
                 inc="s_ve", waits=[("s_ex", v_gbc)])

    def prep_one(kind, slot):
        if kind == "g":
            src_ap, row_base, wait_ld = wp_sb[:, slot, :], slot * 1024, v_wgp
        elif kind == "u":
            src_ap, row_base, wait_ld = wup_sb[:, slot, :], (PREP_SLOTS + slot) * 1024, v_wup
        else:
            src_ap, row_base, wait_ld = wdp_sb[:], 2 * PREP_SLOTS * 1024, v_wdp
        p1 = emit("vector", lambda e: e.tensor_reduce(t_rm[:, 0:1], src_ap, AX.X, OP.max,
                                                      apply_absolute_value=True),
                  waits=[("s_ld", wait_ld), ("s_tp", P.val("s_tp"))], inc="s_ve")
        p2 = emit("vector", lambda e: e.tensor_scalar(t_rm[:, 1:2], t_rm[:, 0:1], 1e-30, None, OP.max),
                  waits=[("s_ve", p1)], inc="s_ve")
        p3 = emit("vector", lambda e: e.reciprocal(out=t_rm[:, 2:3], in_=t_rm[:, 1:2]),
                  waits=[("s_ve", p2)], inc="s_ve")
        p4 = emit("vector", lambda e: e.tensor_scalar(t_rm[:, 2:3], t_rm[:, 2:3], 127.0, None, OP.mult),
                  waits=[("s_ve", p3)], inc="s_ve")
        p5 = emit("vector", lambda e: e.tensor_scalar(t_rm[:, 3:4], t_rm[:, 1:2], INV127, None, OP.mult),
                  waits=[("s_ve", p4)], inc="s_ve")
        if kind != "d":
            emit("vector", lambda e: e.tensor_scalar(t_scr1[:, 0:1024], src_ap, t_rm[:, 2:3], MAGIC, OP.mult, OP.add),
                 waits=[("s_ve", p5)])
            v_q = emit("vector", lambda e: e.tensor_scalar(wqz[:], t_scr1[:, 0:1024], MAGIC, t_rm[:, 3:4], OP.subtract, OP.mult), inc="s_ve")
            for half in range(2):
                emit("sync", lambda e, half=half: e.dma_start(
                    out=wT_sb[:, half * 4:half * 4 + 4, :],
                    in_=wqz[:, half * 512:(half + 1) * 512], transpose=True),
                    inc="s_tp", n=16, waits=[("s_ve", v_q)])
            v_t = P.val("s_tp")
            emit("gpsimd", lambda e, row_base=row_base: e.dma_start(
                out=cc_in.ap()[row_base:row_base + 1024, :].rearrange("(c p) i -> p c i", p=128),
                in_=wT_sb[:]), inc="s_ex", n=16, waits=[("s_tp", v_t)])
        else:
            for part, (o0, o1) in enumerate(((0, 1024), (1024, 2048), (2048, 2816))):
                w = o1 - o0
                emit("vector", lambda e, o0=o0, w=w: e.tensor_scalar(
                    t_scr1[:, 0:w], src_ap[:, o0:o0 + w], t_rm[:, 2:3], MAGIC, OP.mult, OP.add),
                    waits=[("s_ve", p5)] if o0 == 0 else ())
                emit("vector", lambda e, o0=o0, w=w: e.tensor_scalar(
                    wdz[:, o0:o0 + w], t_scr1[:, 0:w], MAGIC, t_rm[:, 3:4], OP.subtract, OP.mult),
                    inc="s_ve" if o1 == 2816 else None)
            v_q = P.val("s_ve")
            off = 0
            while off < II:
                wch = min(512, II - off)
                emit("sync", lambda e, off=off, wch=wch: e.dma_start(
                    out=wdT_s[:, off // 128: (off + wch) // 128, :],
                    in_=wdz[:, off:off + wch], transpose=True),
                    inc="s_tp", n=16, waits=[("s_ve", v_q)])
                off += wch
            v_t = P.val("s_tp")
            emit("gpsimd", lambda e, row_base=row_base: e.dma_start(
                out=cc_in.ap()[row_base:row_base + II, :].rearrange("(c p) i -> p c i", p=128),
                in_=wdT_s[:]), inc="s_ex", n=16, waits=[("s_tp", v_t)])

    for slot in range(PREP_SLOTS):
        prep_one("g", slot)
    for slot in range(PREP_SLOTS):
        prep_one("u", slot)
    prep_one("d", 0)
    v_prep = P.val("s_ex")
    emit("gpsimd", lambda e: e.collective_compute(
        "AllGather", OP.bypass, replica_groups=[list(range(8))],
        ins=[cc_in.ap().opt()], outs=[cc_out.ap().opt()]),
        inc="s_cc", waits=[("s_ex", v_prep)])
    V_AG = P.val("s_cc")

    pcol_holder = {}

    def g_pcol(e):
        if "v" not in pcol_holder:
            pid = e.partition_id()
            pcol_holder["v"] = e.snap((pid % 2) * 512, min_val=0, max_val=512)
        return pcol_holder["v"]

    v_Mbf = [v_const] * HPC
    v_pT_qc = [0, 0, 0, 0]
    v_head_vec_prev = v_const
    v_am_prev = 0
    v_comb_prev = v_const
    v_xfm_hist = [0, 0]
    v_mmv_hist = [0, 0]
    v_astore_hist = [0, 0]
    v_axp_hist = [0, 0]

    for s in range(NSEG if phases >= 2 else 0):
        sp = s % 2
        v_xld = emit("gpsimd", lambda e, s=s, sp=sp: e.dma_start(
            out=x_tm[:, sp, :, :],
            in_=x_in[s * 512:(s + 1) * 512, :].rearrange("(tc p) d -> p tc d", p=128)),
            inc="s_gld", n=16, waits=[("s_tp", v_xfm_hist[sp])])
        for tc in range(4):
            for half in range(2):
                emit("sync", lambda e, sp=sp, tc=tc, half=half: e.dma_start(
                    out=x_fm[:, sp, tc, half * 4:half * 4 + 4, :],
                    in_=x_tm[:, sp, tc, half * 512:(half + 1) * 512], transpose=True),
                    inc="s_tp", n=16,
                    waits=[("s_gld", v_xld), ("s_mm", v_mmv_hist[sp])])
        v_xfm = P.val("s_tp")
        v_xfm_hist[sp] = v_xfm
        if seg_detail == "xonly":
            P.wait("gpsimd", "s_tp", v_xfm)
            continue

        def proj_group(e, w_sb_, b_sb_, f, dst, sp):
            for d in range(8):
                e.matmul(dst, w_sb_[:, d, f * 128:(f + 1) * 128],
                         x_fm[:, sp, :, d, :],
                         start=(d == 0), stop=False)
            return e.matmul(dst, b_sb_[0:1, f * 128:(f + 1) * 128], ones_r[:],
                            start=False, stop=True)

        _gates = {"qmm": 1, "qevac1": 1.2, "qevac2": 1.4, "qevac3": 1.6, "qevac": 2, "vmm": 3, "kevac": 4, "vevac": 5, "sk": 6, "proj": 7, "full": 99}
        _lvl = _gates.get(seg_detail, 99)
        q_waits = [("s_tp", v_xfm), ("s_gld", v_wcast),
                   ("s_ve", max(v_const, v_head_vec_prev)), ("s_se", v_am_prev)]
        for f in range(HPC):
            emit("tensor", lambda e, f=f, sp=sp: proj_group(e, wq_bf, bq_bf, f, psA[:, f * 512:(f + 1) * 512], sp),
                 inc="s_mm", waits=q_waits if f == 0 else ())
        v_mm_q = P.val("s_mm")
        for f in range(HPC):
            emit("tensor", lambda e, f=f, sp=sp: proj_group(e, wk_bf, bk_bf, f, psB[:, f * 512:(f + 1) * 512], sp),
                 inc="s_mm")
        v_mm_k = P.val("s_mm")

        if _lvl < 1.2:
            P.wait("gpsimd", "s_mm", P.val("s_mm"))
            continue
        v_qcopy = emit("scalar", lambda e: e.activation(q_bf.ap().rearrange("p h t -> p (h t)"), psA[:],
                                                        AF.Copy, scale=INV_SQRT_DK),
                       inc="s_se", waits=[("s_mm", v_mm_q)])
        if _lvl < 1.4:
            P.wait("gpsimd", "s_se", P.val("s_se"))
            continue
        v_minq = emit("vector", lambda e: e.tensor_scalar(t_scr1[:], psA[:], 0.0, None, OP.min),
                      inc="s_ve", waits=[("s_mm", v_mm_q), ("s_se", v_qcopy)])
        if _lvl < 1.6:
            P.wait("gpsimd", "s_se", P.val("s_se"))
            P.wait("gpsimd", "s_ve", P.val("s_ve"))
            continue
        v_exq = emit("scalar", lambda e: e.activation(t_scr2[:], t_scr1[:], AF.Exp),
                     inc="s_se", waits=[("s_ve", v_minq)])
        if _lvl < 2:
            P.wait("gpsimd", "s_se", P.val("s_se"))
            P.wait("gpsimd", "s_ve", P.val("s_ve"))
            continue
        v_evac_q = emit("vector", lambda e: e.scalar_tensor_tensor(
            sq_bf.ap().rearrange("p h t -> p (h t)"), psA[:], 0.0, t_scr2[:], OP.max, OP.add),
            inc="s_ve", waits=[("s_se", v_exq)])

        if _lvl < 3:
            P.wait("gpsimd", "s_ve", P.val("s_ve"))
            P.wait("gpsimd", "s_se", P.val("s_se"))
            continue
        for tc in range(4):
            def vgroup(e, tc=tc, sp=sp):
                for d in range(8):
                    e.matmul(psA[:, tc * 512:(tc + 1) * 512],
                             x_fm[:, sp, tc, d, :],
                             wv_bf[:, d, :], start=(d == 0), stop=False)
                return e.matmul(psA[:, tc * 512:(tc + 1) * 512], ones_r[0:1, 0:128],
                                bv_bf[:], start=False, stop=True)
            emit("tensor", vgroup, inc="s_mm",
                 waits=[("s_ve", v_evac_q), ("s_se", v_exq)] if tc == 0 else ())
        v_mm_v = P.val("s_mm")
        v_mmv_hist[sp] = v_mm_v

        if _lvl < 4:
            P.wait("gpsimd", "s_mm", P.val("s_mm"))
            continue
        v_kcopy = emit("scalar", lambda e: e.activation(k_bf.ap().rearrange("p h t -> p (h t)"), psB[:],
                                                        AF.Copy, scale=1.0),
                       inc="s_se", waits=[("s_mm", v_mm_k)])
        v_mink = emit("vector", lambda e: e.tensor_scalar(t_scr1[:], psB[:], 0.0, None, OP.min),
                      inc="s_ve", waits=[("s_mm", v_mm_k), ("s_se", v_kcopy)])
        v_exk = emit("scalar", lambda e: e.activation(t_scr2[:], t_scr1[:], AF.Exp),
                     inc="s_se", waits=[("s_ve", v_mink)])
        v_evac_k = emit("vector", lambda e: e.scalar_tensor_tensor(
            sk_bf.ap().rearrange("p h t -> p (h t)"), psB[:], 0.0, t_scr2[:], OP.max, OP.add),
            inc="s_ve", waits=[("s_se", v_exk)])

        if _lvl < 5:
            P.wait("gpsimd", "s_ve", P.val("s_ve"))
            P.wait("gpsimd", "s_se", P.val("s_se"))
            continue
        for tc in range(4):
            emit("vector", lambda e, tc=tc: e.tensor_copy(
                out=v_bf[:, tc, :, 0:128],
                in_=psA.ap().rearrange("p (tc h f) -> p tc h f", tc=4, h=HPC)[:, tc, :, :]),
                inc="s_ve" if tc == 3 else None,
                waits=[("s_mm", v_mm_v)] if tc == 0 else ())
        v_evac_v = P.val("s_ve")

        if _lvl < 6:
            P.wait("gpsimd", "s_ve", P.val("s_ve"))
            continue
        for h in range(HPC):
            emit("sync", lambda e, h=h: e.dma_start(
                out=sk_tm[:, h, :, :], in_=sk_bf[:, h, :], transpose=True),
                inc="s_tp", n=16, waits=[("s_ve", v_evac_k)])
        v_sktm = P.val("s_tp")

        if debug and s == 0:
            emit("gpsimd", lambda e: e.dma_start(out=dbg["qT"], in_=q_bf[:]), inc="s_st", n=16,
                 waits=[("s_se", v_exq)])
            emit("gpsimd", lambda e: e.dma_start(out=dbg["sqT"], in_=sq_bf[:]), inc="s_st", n=16,
                 waits=[("s_ve", v_evac_q)])
            emit("gpsimd", lambda e: e.dma_start(out=dbg["skT"], in_=sk_bf[:]), inc="s_st", n=16,
                 waits=[("s_ve", v_evac_k)])
            emit("gpsimd", lambda e: e.dma_start(out=dbg["v"], in_=v_bf[:]), inc="s_st", n=16,
                 waits=[("s_ve", v_evac_v)])

        if seg_detail == "proj":
            P.wait("gpsimd", "s_ve", P.val("s_ve"))
            P.wait("gpsimd", "s_tp", P.val("s_tp"))
            continue
        for h in range(HPC):
            def numq_group(e, h=h):
                for qc in range(4):
                    inst = e.matmul(psA[:, qc * 256:qc * 256 + 130],
                                    sq_bf[:, h, qc * 128:(qc + 1) * 128],
                                    M_bf[:, h, :], start=True, stop=True)
                return inst
            emit("tensor", numq_group, inc="s_mm",
                 waits=[("s_ve", max(v_evac_q, v_evac_v, v_Mbf[h])),
                        ("s_se", v_am_prev)])

            def numk_group(e, h=h):
                for kc in range(4):
                    inst = e.matmul(psA[:, 1024 + kc * 256:1024 + kc * 256 + 130],
                                    sk_bf[:, h, kc * 128:(kc + 1) * 128],
                                    M_bf[:, h, :], start=True, stop=True)
                return inst
            v_numk = emit("tensor", numk_group, inc="s_mm", waits=[("s_ve", v_evac_k)])

            v_dense = emit("scalar", lambda e: e.add(
                t_den[:], psA.ap().rearrange("p (c u) -> p c u", u=256)[:, :, 128], EPS_ATT),
                inc="s_se", waits=[("s_mm", v_numk), ("s_ve", max(v_Mbf))])
            if s == 1 and h == 0 and isinstance(debug, str) and debug == "cap2":
                emit("vector", lambda e: e.tensor_copy(out=t_scr1[:, 1024:1032], in_=t_den[:]),
                     inc="s_ve", waits=[("s_se", v_dense)])
                emit("vector", lambda e: e.tensor_copy(out=t_scr1[:, 1032:1162], in_=M_bf[:, 0, :]))
                emit("vector", lambda e: e.tensor_copy(out=t_scr1[:, 1162:1292], in_=psA[:, 1024:1154]))
                emit("vector", lambda e: e.tensor_copy(out=t_scr1[:, 1300:1820],
                                                       in_=M_bf.ap().rearrange("p h c -> p (h c)")))
                emit("vector", lambda e: e.tensor_copy(out=t_scr1[:, 1820:1950],
                                                       in_=M_f32[:, 0, :]))
            v_rq = emit("vector", lambda e: e.reciprocal(out=t_rq[:], in_=t_den[:]),
                        waits=[("s_se", v_dense)], inc="s_ve")
            emit("vector", lambda e, h=h: e.tensor_scalar(t_rqg[:], t_rq[:, 0:4], g_bc[:, h:h + 1], None, OP.mult),
                 waits=[("s_ex", v_gbc), ("s_ve", v_rq)])
            v_rkn = emit("vector", lambda e: e.tensor_scalar(t_rkn[:], t_rq[:, 4:8], -1.0, None, OP.mult),
                         inc="s_ve")
            if s == 1 and h == 0 and isinstance(debug, str) and debug == "cap4":
                emit("vector", lambda e: e.tensor_copy(out=t_scr1[:, 1024:1032], in_=t_den[:]))
                emit("vector", lambda e: e.tensor_copy(out=t_scr1[:, 1032:1040], in_=t_rq[:]))
                emit("vector", lambda e: e.tensor_copy(out=t_scr1[:, 1040:1044], in_=t_rkn[:]))
                emit("vector", lambda e: e.tensor_copy(out=t_scr1[:, 1044:1048], in_=t_rqg[:]))
            for kc in range(4):
                emit("vector", lambda e, kc=kc, h=h: e.scalar_tensor_tensor(
                    vd_bf[:, kc, 0:128], psA[:, 1024 + kc * 256:1024 + kc * 256 + 128],
                    t_rkn[:, kc:kc + 1], v_bf[:, kc, h, 0:128], OP.mult, OP.add),
                    inc="s_ve" if kc == 3 else None,
                    waits=[("s_ve", v_rkn)] if kc == 0 else ())
            v_vd = P.val("s_ve")

            if s == 1 and h == 0 and isinstance(debug, str) and debug == "cap4":
                emit("vector", lambda e: e.tensor_copy(out=t_scr1[:, 1048:1178], in_=vd_bf[:, 0, :]))
                emit("vector", lambda e: e.tensor_copy(out=t_scr1[:, 1178:1308], in_=vd_bf[:, 3, :]))

            def mup_group(e, h=h):
                for kc in range(4):
                    inst = e.matmul(psA[:, 1024:1154], sk_tm[:, h, kc, :],
                                    vd_bf[:, kc, :], start=(kc == 0), stop=(kc == 3))
                return inst
            v_mup = emit("tensor", mup_group, inc="s_mm",
                         waits=[("s_ve", v_vd), ("s_tp", v_sktm)])
            if s == 0 and h == 0 and isinstance(debug, str) and debug == "cap":
                emit("vector", lambda e: e.tensor_copy(out=t_scr1[:, 1900:2030], in_=psA[:, 1024:1154]),
                     inc="s_ve", waits=[("s_mm", v_mup)])
                emit("vector", lambda e: e.tensor_copy(out=t_scr1[:, 1380:1900],
                                                       in_=vd_bf.ap().rearrange("p a b -> p (a b)")))
                emit("vector", lambda e: e.tensor_copy(out=t_scr1[:, 868:1380],
                                                       in_=sk_tm[:, 0, :, :].rearrange("p a b -> p (a b)")))
            if s == 1 and h == 0 and isinstance(debug, str) and debug == "cap3":
                emit("vector", lambda e: e.tensor_copy(out=t_scr1[:, 1024:1154], in_=psA[:, 1024:1154]),
                     waits=[("s_mm", v_mup)])
            emit("vector", lambda e, h=h: e.tensor_tensor(M_f32[:, h, :], M_f32[:, h, :],
                                                          psA[:, 1024:1154], OP.add),
                 waits=[("s_mm", v_mup)])
            v_Mbf[h] = emit("vector", lambda e, h=h: e.tensor_copy(out=M_bf[:, h, :], in_=M_f32[:, h, :]),
                            inc="s_ve")
            if s == 1 and isinstance(debug, str) and debug == "cap3":
                off = 1160 + 140 * h
                emit("vector", lambda e, off=off: e.tensor_copy(out=t_scr1[:, off:off + 130],
                                                                in_=M_f32[:, 0, :]))

            for qc in range(4):
                nk = (qc + 1) * 128
                boff = (qc % 2) * 512
                w = []
                if h == 0 and qc == 0:
                    w = [("s_se", v_exk), ("s_ve", v_evac_k)]
                elif qc >= 2:
                    w = [("s_se", P.val("s_se"))]
                v_sc = emit("tensor", lambda e, h=h, qc=qc, nk=nk, boff=boff: e.matmul(
                    psB[:, boff:boff + nk], q_bf[:, h, qc * 128:(qc + 1) * 128],
                    k_bf[:, h, 0:nk], start=True, stop=True),
                    inc="s_mm", waits=w)
                v_nm = emit("vector", lambda e, qc=qc, nk=nk, boff=boff: e.tensor_reduce(
                    t_nm[:, qc:qc + 1], psB[:, boff:boff + nk], AX.X, OP.max, negate=True),
                    inc="s_ve", waits=[("s_mm", v_sc)])
                v_exp = emit("scalar", lambda e, qc=qc, nk=nk, boff=boff: e.activation(
                    p_bf[:, qc, 0:nk], psB[:, boff:boff + nk], AF.Exp,
                    bias=t_nm[:, qc:qc + 1]),
                    inc="s_se", waits=[("s_ve", v_nm)])
                v_msk = emit("vector", lambda e, qc=qc: e.tensor_tensor(
                    p_bf[:, qc, qc * 128:(qc + 1) * 128], p_bf[:, qc, qc * 128:(qc + 1) * 128],
                    tril_bf[:], OP.mult), inc="s_ve", waits=[("s_se", v_exp)])
                emit("sync", lambda e, qc=qc, nk=nk: e.dma_start(
                    out=pT_bf[:, qc, 0:qc + 1, :],
                    in_=p_bf[:, qc, 0:nk], transpose=True),
                    inc="s_tp", n=16, waits=[("s_ve", v_msk)])
                v_pT_qc[qc] = P.val("s_tp")

            for qc in range(4):
                def pv_group(e, h=h, qc=qc):
                    for kc in range(qc + 1):
                        inst = e.matmul(psB[:, 1024 + qc * 256:1024 + qc * 256 + 130],
                                        pT_bf[:, qc, kc, :],
                                        v_bf[:, kc, h, :], start=(kc == 0), stop=(kc == qc))
                    return inst
                emit("tensor", pv_group, inc="s_mm",
                     waits=[("s_tp", v_pT_qc[qc])])
            v_pv = P.val("s_mm")

            v_rd = emit("vector", lambda e: e.reciprocal(
                out=t_rd[:], in_=psB.ap().rearrange("p (c u) -> p c u", u=256)[:, 4:8, 128]),
                waits=[("s_mm", v_pv)], inc="s_ve")
            v_r2 = emit("vector", lambda e, h=h: e.tensor_scalar(t_r2g[:], t_rd[:], gm1_bc[:, h:h + 1], None, OP.mult),
                        inc="s_ve", waits=[("s_ve", max(v_gm1, v_rd))])
            for qc in range(4):
                emit("scalar", lambda e, qc=qc: e.activation(
                    t_scr2[:, qc * 128:(qc + 1) * 128], psA[:, qc * 256:qc * 256 + 128],
                    AF.Identity, scale=t_rqg[:, qc:qc + 1]),
                    waits=[("s_ve", v_r2)] if qc == 0 else (),
                    inc="s_se" if qc == 3 else None)
            v_am = P.val("s_se")
            v_am_prev = v_am
            for qc in range(4):
                emit("vector", lambda e, h=h, qc=qc, sp=sp: e.scalar_tensor_tensor(
                    a_stage[:, sp, qc, h * 128:(h + 1) * 128],
                    psB[:, 1024 + qc * 256:1024 + qc * 256 + 128],
                    t_r2g[:, qc:qc + 1], t_scr2[:, qc * 128:(qc + 1) * 128], OP.mult, OP.add),
                    waits=([("s_ve", v_r2)] if qc == 0 else []) +
                          [("s_se", v_am), ("s_ld", v_axp_hist[sp])],
                    inc="s_ve" if qc == 3 else None)
            v_comb_prev = P.val("s_ve")
            if s == NSEG - 1 and h == 0 and isinstance(debug, str) and debug == "cap6":
                emit("vector", lambda e, sp=sp: e.tensor_copy(
                    out=t_scr1[:, 1024:1536].rearrange("p (q f) -> p q f", q=4),
                    in_=a_stage[:, sp, :, 0:128]))
                emit("vector", lambda e: e.tensor_copy(out=t_scr1[:, 1536:1540], in_=t_r2g[:]))
                emit("vector", lambda e: e.tensor_copy(out=t_scr1[:, 1540:1544], in_=t_rqg[:]))
                emit("vector", lambda e: e.tensor_copy(out=t_scr1[:, 1544:1674],
                                                       in_=psB[:, 1024:1154]))
                emit("vector", lambda e: e.tensor_copy(out=t_scr1[:, 1674:1804],
                                                       in_=psB[:, 1792:1922]))
                emit("vector", lambda e: e.tensor_copy(out=t_scr1[:, 1804:1932],
                                                       in_=t_scr2[:, 384:512]))

            if debug and s == 0 and h == 0:
                emit("gpsimd", lambda e: e.dma_start(out=dbg["p"], in_=p_bf[:]), inc="s_st", n=16,
                     waits=[("s_ve", v_comb_prev)])

        v_head_vec_prev = P.val("s_ve")

        for qc in range(4):
            emit("sync", lambda e, s=s, sp=sp, qc=qc: e.dma_start(
                out=ax_priv[s, qc], in_=a_stage[:, sp, qc, :]),
                inc="s_ld", n=16, waits=[("s_ve", v_head_vec_prev)] if qc == 0 else ())
        v_axp = P.val("s_ld")
        v_axp_hist[sp] = v_axp

        def astore(e, s=s):
            pc = g_pcol(e)
            return e.dma_start(
                out=a_xchg[s * 512:(s + 1) * 512, ds(pc, 512)],
                in_=ax_priv[s].rearrange("qc p f -> (qc p) f"))
        emit("gpsimd", astore, inc="s_st", n=16, waits=[("s_ld", v_axp)])
        v_astore_hist[sp] = P.val("s_st")

        if debug and s == 0:
            emit("gpsimd", lambda e: e.dma_start(out=dbg["astage"], in_=a_stage[:, 0, :, :]),
                 inc="s_st", n=16, waits=[("s_ve", v_head_vec_prev)])
            v_astore_hist[sp] = P.val("s_st")

    if debug:
        emit("gpsimd", lambda e: e.dma_start(out=dbg["M"], in_=M_f32[:]), inc="s_st", n=16,
             waits=[("s_ve", max(v_Mbf))])

    v_all_store = P.val("s_st")
    v_barw = emit("gpsimd", lambda e: e.dma_start(out=bar_in.ap(), in_=sgc[0:1, 0:8]),
                  inc="s_ex", n=16, waits=[("s_st", v_all_store)])
    emit("gpsimd", lambda e: e.collective_compute(
        "AllReduce", OP.add, replica_groups=[list(range(8))],
        ins=[bar_in.ap().opt()], outs=[bar_out.ap().opt()]),
        inc="s_cc", waits=[("s_ex", v_barw)])
    V_BAR = P.val("s_cc")

    def a_copy(e):
        pid = e.partition_id()
        off = e.snap((pid % 2) * TPC, min_val=0, max_val=TPC)
        return e.dma_start(out=a_loc, in_=a_xchg[ds(off, TPC), :])
    V_ACP = emit("gpsimd", a_copy, inc="s_gld", n=16, waits=[("s_cc", V_BAR)])

    if ph_dump == "vbuf":
        tmp_f = nc.sbuf_tensor("dumpV_f", [128, 2080], F32)
        with tmp_f as tf:
            v_c = emit("vector", lambda e: e.tensor_copy(
                out=tf[:], in_=v_bf.ap().rearrange("p a b c -> p (a b c)")),
                inc="s_ve", waits=[("s_ve", P.val("s_ve")), ("s_se", P.val("s_se")),
                                   ("s_mm", P.val("s_mm"))])
            emit("gpsimd", lambda e: e.dma_start(out=y_out[0:128, 0:1024], in_=tf[:, 0:1024]),
                 inc="s_st", n=16, waits=[("s_ve", v_c)])
            emit("gpsimd", lambda e: e.dma_start(out=y_out[128:256, 0:1024], in_=tf[:, 1024:2048]),
                 inc="s_st", n=16)
            v_d3 = emit("gpsimd", lambda e: e.dma_start(out=y_out[256:384, 0:32], in_=tf[:, 2048:2080]),
                 inc="s_st", n=16)
            v_c2 = emit("vector", lambda e: e.tensor_copy(
                out=tf[:, 0:520], in_=vd_bf.ap().rearrange("p a b -> p (a b)")),
                inc="s_ve", waits=[("s_st", v_d3)])
            emit("gpsimd", lambda e: e.dma_start(out=y_out[384:512, 0:520], in_=tf[:, 0:520]),
                 inc="s_st", n=16, waits=[("s_ve", v_c2)])
        P.wait("gpsimd", "s_st", P.val("s_st"))
        return _finalize(nc, P, dbg)

    if ph_dump == "M":
        tmp_f = nc.sbuf_tensor("dumpM_f", [128, HPC, 130], F32)
        with tmp_f as tf:
            v_c = emit("vector", lambda e: e.tensor_copy(
                out=tf.ap().rearrange("p h c -> p (h c)"),
                in_=M_f32.ap().rearrange("p h c -> p (h c)")),
                inc="s_ve", waits=[("s_ve", max(v_Mbf))])
            v_d1 = emit("gpsimd", lambda e: e.dma_start(
                out=y_out[0:128, 0:HPC * 130], in_=tf.ap().rearrange("p h c -> p (h c)")),
                inc="s_st", n=16, waits=[("s_ve", v_c)])
            v_c2 = emit("vector", lambda e: e.tensor_copy(
                out=tf.ap().rearrange("p h c -> p (h c)")[:, 0:512],
                in_=a_stage[:, (NSEG - 1) % 2, 0, :]), inc="s_ve",
                waits=[("s_st", v_d1)])
            emit("gpsimd", lambda e: e.dma_start(
                out=y_out[128:256, 0:512], in_=tf.ap().rearrange("p h c -> p (h c)")[:, 0:512]),
                inc="s_st", n=16, waits=[("s_ve", v_c2)])
            if isinstance(debug, str) and debug == "cap":
                emit("gpsimd", lambda e: e.dma_start(out=y_out[256:384, 0:1024], in_=t_scr1[:, 868:1892]),
                     inc="s_st", n=16)
                emit("gpsimd", lambda e: e.dma_start(out=y_out[384:512, 0:138], in_=t_scr1[:, 1892:2030]),
                     inc="s_st", n=16)
            if isinstance(debug, str) and debug == "cap2":
                emit("gpsimd", lambda e: e.dma_start(out=y_out[256:384, 0:268], in_=t_scr1[:, 1024:1292]),
                     inc="s_st", n=16)
                emit("gpsimd", lambda e: e.dma_start(out=y_out[384:512, 0:650], in_=t_scr1[:, 1300:1950]),
                     inc="s_st", n=16)
            if isinstance(debug, str) and debug in ("cap3", "cap4"):
                emit("gpsimd", lambda e: e.dma_start(out=y_out[256:384, 0:696], in_=t_scr1[:, 1024:1720]),
                     inc="s_st", n=16)
            if isinstance(debug, str) and debug == "cap6":
                emit("gpsimd", lambda e: e.dma_start(out=y_out[256:384, 0:908], in_=t_scr1[:, 1024:1932]),
                     inc="s_st", n=16)
            if isinstance(debug, str) and debug == "cap5":
                v_c5 = emit("vector", lambda e: e.tensor_copy(
                    out=t_scr1[:], in_=a_stage[:, (NSEG - 1) % 2, :, :].rearrange("p q f -> p (q f)")),
                    inc="s_ve", waits=[("s_st", P.val("s_st"))])
                emit("gpsimd", lambda e: e.dma_start(out=y_out[256:384, 0:1024], in_=t_scr1[:, 0:1024]),
                     inc="s_st", n=16, waits=[("s_ve", v_c5)])
                emit("gpsimd", lambda e: e.dma_start(out=y_out[384:512, 0:1024], in_=t_scr1[:, 1024:2048]),
                     inc="s_st", n=16)
                v_c8 = emit("vector", lambda e: e.tensor_copy(out=t_scr1[:, 1024:1536], in_=t_scr2[:, 0:512]),
                            inc="s_ve", waits=[("s_st", P.val("s_st"))])
                emit("vector", lambda e: e.tensor_copy(out=t_scr1[:, 2040:2044], in_=t_nm[:]))
                emit("vector", lambda e: e.tensor_copy(out=t_scr1[:, 2044:2048], in_=t_rqg[:]))
                emit("vector", lambda e: e.tensor_copy(out=t_scr1[:, 2036:2040], in_=t_r2g[:]))
                v_c9 = emit("vector", lambda e: e.tensor_copy(
                    out=t_scr2[:, 0:1024], in_=psB[:, 1024:2048]), inc="s_ve")
                emit("gpsimd", lambda e: e.dma_start(out=y_out[640:768, 0:1024], in_=t_scr2[:, 0:1024]),
                     inc="s_st", n=16, waits=[("s_ve", v_c9)])
                emit("gpsimd", lambda e: e.dma_start(out=y_out[768:896, 0:512], in_=t_scr1[:, 1024:1536]),
                     inc="s_st", n=16)
                emit("gpsimd", lambda e: e.dma_start(out=y_out[896:1024, 0:12], in_=t_scr1[:, 2036:2048]),
                     inc="s_st", n=16)
                with nc.sbuf_tensor("dump_axp", [128, 2, 512], BF16) as axp_t:
                    v_c6 = emit("gpsimd", lambda e: e.dma_start(out=axp_t[:, 0, :], in_=ax_priv[NSEG - 1, 1]),
                                inc="s_gld", n=16)
                    emit("gpsimd", lambda e: e.dma_start(out=axp_t[:, 1, :], in_=ax_priv[NSEG - 1, 3]),
                         inc="s_gld", n=16)
                    v_c7 = emit("vector", lambda e: e.tensor_copy(
                        out=t_scr1[:, 0:1024], in_=axp_t.ap().rearrange("p a b -> p (a b)")),
                        inc="s_ve", waits=[("s_gld", P.val("s_gld")), ("s_st", P.val("s_st"))])
                    emit("gpsimd", lambda e: e.dma_start(out=y_out[512:640, 0:1024], in_=t_scr1[:, 0:1024]),
                         inc="s_st", n=16, waits=[("s_ve", v_c7)])
        P.wait("gpsimd", "s_st", P.val("s_st"))
        return _finalize(nc, P, dbg)

    if ph_dump in ("aloc", "ax"):
        dump_src = a_loc if ph_dump == "aloc" else a_xchg
        tmp_bf = nc.sbuf_tensor("dump_bf", [128, 1024], BF16)
        tmp_f = nc.sbuf_tensor("dump_f", [128, 2, 1024], F32)
        with tmp_bf as tbf, tmp_f as tf:
            v_pv = 0
            for t in range(NTILE):
                tb = t % 2
                v_l = emit("gpsimd", lambda e, t=t: e.dma_start(
                    out=tbf[:], in_=dump_src[t * 128:(t + 1) * 128, :]),
                    inc="s_gld", n=16, waits=[("s_gld", V_ACP), ("s_ve", v_pv)])
                v_c = emit("vector", lambda e, tb=tb: e.tensor_copy(out=tf[:, tb, :], in_=tbf[:]),
                           inc="s_ve", waits=[("s_gld", v_l), ("s_st", v_pv_st if t >= 2 else 0)])
                v_s = emit("gpsimd", lambda e, t=t, tb=tb: e.dma_start(
                    out=y_out[t * 128:(t + 1) * 128, :], in_=tf[:, tb, :]),
                    inc="s_st", n=16, waits=[("s_ve", v_c)])
                v_pv = v_c
                v_pv_st = P.val("s_st")
        P.wait("gpsimd", "s_st", P.val("s_st"))
        return _finalize(nc, P, dbg)

    es1.close()
    sb2p, sb2a = mk_sb(es2p), mk_sb(es2a)
    x1_sb = sb2p("x1_sb", [128, NTILE, 1024], BF16)
    x2_sb = sb2a("x2_sb", [128, 2, 1024])
    a2_sb = sb2a("a2_sb", [128, 2, 1024], BF16)
    r_sb = sb2a("r_sb", [128, 2, 1024])
    rsq_sb = sb2a("rsq_sb", [128, 1024])
    xq_bf = sb2a("xq_bf", [128, 2, 1024], BF16)
    xqT = sb2a("xqT", [128, NTILE, 8, 128], BF16)
    wgt_sb = sb2a("wgt_sb", [128, 2, 8, 128], BF16)
    wut_sb = sb2a("wut_sb", [128, 2, 8, 128], BF16)
    sg_sb = sb2a("sg_sb", [128, 512], BF16)
    su_sb = sb2a("su_sb", [128, 512], BF16)
    h_bf = sb2a("h_bf", [128, 4096], BF16)

    v_r_prev = [0, 0]
    v_x1_hist = [0] * NTILE
    v_xqT_tile = [0] * NTILE
    for t in range(NTILE if phases >= 3 else 0):
        tb = t % 2
        v_xl = emit("sync", lambda e, t=t, tb=tb: e.dma_start(
            out=x2_sb[:, tb, :],
            in_=x2_in[t * 128:(t + 1) * 128, :]),
            inc="s_ld", n=16, waits=[("s_cc", V_BAR), ("s_ve", v_r_prev[tb])])
        v_al = emit("gpsimd", lambda e, t=t, tb=tb: e.dma_start(
            out=a2_sb[:, tb, :],
            in_=a_loc[t * 128:(t + 1) * 128, :]),
            inc="s_gld", n=16, waits=[("s_gld", V_ACP), ("s_ve", v_r_prev[tb])])
        emit("vector", lambda e, tb=tb: e.scalar_tensor_tensor(
            r_sb[:, tb, :], x2_sb[:, tb, :], 0.0, a2_sb[:, tb, :], OP.add, OP.add,
            accum_out=st2[:, 0:1]),
            waits=[("s_ld", v_xl), ("s_gld", v_al)])
        v_acc = emit("vector", lambda e, tb=tb: e.scalar_tensor_tensor(
            rsq_sb[:], r_sb[:, tb, :], 0.0, r_sb[:, tb, :], OP.add, OP.mult,
            accum_out=st2[:, 1:2]), inc="s_ve")
        v1 = emit("vector", lambda e: e.tensor_scalar(st2[:, 0:1], st2[:, 0:1], float(1.0 / D), None, OP.mult),
                  waits=[("s_ve", v_acc)], inc="s_ve")
        v2 = emit("vector", lambda e: e.tensor_tensor(st2[:, 3:4], st2[:, 0:1], st2[:, 0:1], OP.mult),
                  waits=[("s_ve", v1)], inc="s_ve")
        v3 = emit("vector", lambda e: e.scalar_tensor_tensor(st2[:, 2:3], st2[:, 1:2], float(1.0 / D),
                                                             st2[:, 3:4], OP.mult, OP.subtract),
                  waits=[("s_ve", v2)], inc="s_ve")
        v4 = emit("vector", lambda e: e.tensor_scalar(st2[:, 2:3], st2[:, 2:3], EPS_LN, None, OP.add),
                  waits=[("s_ve", v3)], inc="s_ve")
        v_var = emit("vector", lambda e: e.reciprocal(out=st2[:, 2:3], in_=st2[:, 2:3]), inc="s_ve",
                     waits=[("s_ve", v4)])
        v_rst = emit("scalar", lambda e: e.activation(t_rstd[:, 0:1], st2[:, 2:3], AF.Sqrt),
                     inc="s_se", waits=[("s_ve", v_var)])
        v_bias = emit("vector", lambda e: e.tensor_scalar(t_rstd[:, 1:2], st2[:, 0:1], t_rstd[:, 0:1],
                                                          -1.0, OP.mult, OP.mult),
                      inc="s_ve", waits=[("s_se", v_rst)])
        v_x1t = emit("scalar", lambda e, t=t, tb=tb: e.activation(
            x1_sb[:, t, :], r_sb[:, tb, :], AF.Identity,
            bias=t_rstd[:, 1:2], scale=t_rstd[:, 0:1]),
            inc="s_se", waits=[("s_se", v_rst), ("s_ve", v_bias)])
        v_x1_hist[t] = v_x1t
        emit("vector", lambda e, t=t: e.tensor_scalar(
            rsq_sb[:], x1_sb[:, t, :], 1.0, None, OP.mult, OP.max,
            accum_out=xmx[:, 2 * t:2 * t + 1]), waits=[("s_se", v_x1t)])
        emit("vector", lambda e, t=t: e.tensor_scalar(
            rsq_sb[:], x1_sb[:, t, :], -1.0, None, OP.mult, OP.max,
            accum_out=xmx[:, 2 * t + 1:2 * t + 2]), inc="s_ve")
        v_r_prev[tb] = P.val("s_ve")
        if debug and t == 0:
            emit("gpsimd", lambda e: e.dma_start(out=dbg["r0"], in_=r_sb[:, 0, :]), inc="s_st", n=16,
                 waits=[("s_ve", v_r_prev[0])])

    if phases < 3:
        P.wait("gpsimd", "s_cc", P.val("s_cc"))
        P.wait("gpsimd", "s_gld", P.val("s_gld"))
        P.wait("gpsimd", "s_st", P.val("s_st"))
        return _finalize(nc, P, dbg)
    v_xm = emit("vector", lambda e: e.tensor_reduce(xmx[:, 0:1], xmx[:], AX.X, OP.max), inc="s_ve",
                waits=[("s_ve", P.val("s_ve"))])
    v_sw = emit("gpsimd", lambda e: e.dma_start(
        out=scr[0:1, 64:192].rearrange("o (p u) -> p (o u)", p=128), in_=xmx[:, 0:1]),
        inc="s_ex", n=16, waits=[("s_ve", v_xm)])
    v_sr = emit("gpsimd", lambda e: e.dma_start(out=row128[:], in_=scr[0:1, 64:192]),
                inc="s_gld", n=16, waits=[("s_ex", v_sw)])
    vl1 = emit("vector", lambda e: e.tensor_reduce(lmax[0:1, 0:1], row128[:], AX.X, OP.max),
               waits=[("s_gld", v_sr)], inc="s_ve")
    vl2 = emit("vector", lambda e: e.tensor_copy(out=lmax[0:1, 1:2], in_=lmax[0:1, 0:1]),
               waits=[("s_ve", vl1)], inc="s_ve")
    vl3 = emit("vector", lambda e: e.tensor_copy(out=lmax[0:1, 2:4], in_=lmax[0:1, 0:2]),
               waits=[("s_ve", vl2)], inc="s_ve")
    v_lm = emit("vector", lambda e: e.tensor_copy(out=lmax[0:1, 4:8], in_=lmax[0:1, 0:4]), inc="s_ve",
                waits=[("s_ve", vl3)])
    v_mxw = emit("gpsimd", lambda e: e.dma_start(out=mx_in.ap(), in_=lmax[:]),
                 inc="s_ex", n=16, waits=[("s_ve", v_lm)])
    emit("gpsimd", lambda e: e.collective_compute(
        "AllReduce", OP.max, replica_groups=[list(range(8))],
        ins=[mx_in.ap().opt()], outs=[mx_out.ap().opt()]),
        inc="s_cc", waits=[("s_ex", v_mxw)])
    V_MX = P.val("s_cc")
    v_gm = emit("gpsimd", lambda e: e.dma_start(out=lmax[0:1, 0:1], in_=mx_out.ap()[0:1, 0:1]),
                inc="s_gld", n=16, waits=[("s_cc", V_MX)])
    if debug:
        emit("gpsimd", lambda e: e.dma_start(out=dbg["gmax"], in_=lmax[:]), inc="s_st", n=16,
             waits=[("s_gld", v_gm)])
    vsx1 = emit("vector", lambda e: e.tensor_scalar(lmax[0:1, 1:2], lmax[0:1, 0:1], sgc[0:1, 6:7],
                                                    INV127, OP.mult, OP.mult),
                waits=[("s_gld", v_gm)], inc="s_ve")
    v_sx = emit("vector", lambda e: e.reciprocal(out=lmax[0:1, 2:3], in_=lmax[0:1, 1:2]), inc="s_ve",
                waits=[("s_ve", vsx1)])
    v_sxw = emit("gpsimd", lambda e: e.dma_start(out=scr[0:1, 8:10], in_=lmax[0:1, 1:3]),
                 inc="s_ex", n=16, waits=[("s_ve", v_sx)])
    emit("gpsimd", lambda e: e.dma_start(out=sx_bc[:], in_=scr[0:1, 8:9].broadcast_to([128, 1])),
         inc="s_gld", n=16, waits=[("s_ex", v_sxw)])
    v_sxb = emit("gpsimd", lambda e: e.dma_start(out=rx_bc[:], in_=scr[0:1, 9:10].broadcast_to([128, 1])),
                 inc="s_gld", n=16)

    v_tp_hist2 = [0, 0]
    for t in range(NTILE):
        tb = t % 2
        emit("vector", lambda e, t=t: e.tensor_scalar(rsq_sb[:], x1_sb[:, t, :], rx_bc[:], MAGIC, OP.mult, OP.add),
             waits=[("s_gld", v_sxb), ("s_tp", v_tp_hist2[tb]), ("s_se", v_x1_hist[t])])
        emit("vector", lambda e: e.tensor_scalar(rsq_sb[:], rsq_sb[:], MAGIC + 127.0, MAGIC - 128.0, OP.min, OP.max))
        v_q2 = emit("vector", lambda e, tb=tb: e.tensor_scalar(xq_bf[:, tb, :], rsq_sb[:], MAGIC, None, OP.subtract),
                    inc="s_ve")
        for half in range(2):
            emit("sync", lambda e, t=t, tb=tb, half=half: e.dma_start(
                out=xqT[:, t, half * 4:half * 4 + 4, :],
                in_=xq_bf[:, tb, half * 512:(half + 1) * 512], transpose=True),
                inc="s_tp", n=16, waits=[("s_ve", v_q2)])
        v_tp_hist2[tb] = P.val("s_tp")
        v_xqT_tile[t] = P.val("s_tp")
    v_xqT = P.val("s_tp")

    if ph_dump == "x1":
        tmp_f = nc.sbuf_tensor("dumpX_f", [128, 2048], F32)
        with tmp_f as tf:
            v_c = emit("vector", lambda e: e.tensor_copy(out=tf[:, 0:1024], in_=x1_sb[:, 0, :]),
                       inc="s_ve", waits=[("s_tp", v_xqT)])
            emit("vector", lambda e: e.tensor_copy(out=tf[:, 1024:2048], in_=x1_sb[:, 31, :]),
                 inc="s_ve")
            v_d1 = emit("gpsimd", lambda e: e.dma_start(out=y_out[0:128, 0:1024], in_=tf[:, 0:1024]),
                        inc="s_st", n=16, waits=[("s_ve", P.val("s_ve"))])
            emit("gpsimd", lambda e: e.dma_start(out=y_out[128:256, 0:1024], in_=tf[:, 1024:2048]),
                 inc="s_st", n=16)
            emit("gpsimd", lambda e: e.dma_start(out=y_out[256:257, 0:8], in_=lmax[0:1, :]),
                 inc="s_st", n=16)
            v_c2 = emit("vector", lambda e: e.tensor_copy(
                out=tf[:, 0:1024].rearrange("p (a bc) -> p a bc", a=2),
                in_=xqT[:, 0:2, 0:4, :].rearrange("p a b c -> p a (b c)")),
                inc="s_ve", waits=[("s_st", v_d1)])
            emit("gpsimd", lambda e: e.dma_start(out=y_out[384:512, 0:1024], in_=tf[:, 0:1024]),
                 inc="s_st", n=16, waits=[("s_ve", v_c2)])
            v_c3 = emit("vector", lambda e: e.tensor_copy(out=tf[:, 1024:2048], in_=r_sb[:, 0, :]),
                        inc="s_ve")
            emit("gpsimd", lambda e: e.dma_start(out=y_out[512:640, 0:1024], in_=tf[:, 1024:2048]),
                 inc="s_st", n=16, waits=[("s_ve", v_c3)])
            v_c4 = emit("vector", lambda e: e.tensor_copy(out=tf[:, 0:2], in_=t_rstd[:]), inc="s_ve",
                        waits=[("s_st", P.val("s_st"))])
            emit("vector", lambda e: e.tensor_copy(out=tf[:, 2:6], in_=st2[:]))
            v_c5 = emit("vector", lambda e: e.tensor_copy(out=tf[:, 6:70], in_=xmx[:]), inc="s_ve")
            emit("gpsimd", lambda e: e.dma_start(out=y_out[640:768, 0:70], in_=tf[:, 0:70]),
                 inc="s_st", n=16, waits=[("s_ve", v_c5)])
        P.wait("gpsimd", "s_st", P.val("s_st"))
        return _finalize(nc, P, dbg)

    chunk_slots = []
    for sslot in range(PREP_SLOTS):
        for r in range(8):
            ci = r + 8 * sslot
            if ci < NCH:
                chunk_slots.append((ci, r, sslot))
    chunk_slots.sort()

    if phases < 4:
        P.wait("gpsimd", "s_cc", P.val("s_cc"))
        P.wait("gpsimd", "s_tp", P.val("s_tp"))
        P.wait("gpsimd", "s_ve", P.val("s_ve"))
        return _finalize(nc, P, dbg)
    v_spill_prev = 0
    v_gse_slot = [0, 0]
    v_gu_hist = [0, 0]
    v_httail_prev = 0
    for idx, (ci, r_, s_) in enumerate(chunk_slots):
        cb = idx % 2
        grow = r_ * CCR + s_ * 1024
        urow = r_ * CCR + (PREP_SLOTS + s_) * 1024
        emit("sync", lambda e, grow=grow, cb=cb: e.dma_start(
            out=wgt_sb[:, cb, :, :],
            in_=cc_out.ap()[grow:grow + 1024, :].rearrange("(c p) i -> p c i", p=128)),
            inc="s_ld", n=16,
            waits=[("s_cc", max(V_AG, V_BAR)), ("s_mm", v_gu_hist[cb])])
        v_wu = emit("sync", lambda e, urow=urow, cb=cb: e.dma_start(
            out=wut_sb[:, cb, :, :],
            in_=cc_out.ap()[urow:urow + 1024, :].rearrange("(c p) i -> p c i", p=128)),
            inc="s_ld", n=16)
        for tt in range(8):
            goff = (tt % 2) * 1024

            def gu_group(e, cb, tt, goff, which):
                w = wgt_sb if which == 0 else wut_sb
                for hc in range(8):
                    inst = e.matmul(psB[:, goff + which * 512: goff + which * 512 + 512],
                                    w[:, cb, hc, :],
                                    xqT[:, 4 * tt:4 * tt + 4, hc, :],
                                    start=(hc == 0), stop=(hc == 7))
                return inst
            emit("tensor", lambda e, cb=cb, tt=tt, goff=goff: gu_group(e, cb, tt, goff, 0),
                 inc="s_mm", waits=[("s_ld", v_wu), ("s_tp", v_xqT_tile[4 * tt + 3]),
                                    ("s_se", v_gse_slot[tt % 2])])
            v_gu = emit("tensor", lambda e, cb=cb, tt=tt, goff=goff: gu_group(e, cb, tt, goff, 1),
                        inc="s_mm")
            silu_w = [("s_mm", v_gu), ("s_gld", v_sxb), ("s_ve", v_httail_prev)]
            if debug and idx == 0 and tt == 1:
                silu_w.append(("s_st", P.val("s_st")))
            emit("scalar", lambda e, goff=goff: e.activation(sg_sb[:], psB[:, goff:goff + 512],
                                                             AF.Silu, scale=sx_bc[:]),
                 waits=silu_w)
            v_gse = emit("scalar", lambda e, goff=goff: e.activation(
                su_sb[:], psB[:, goff + 512:goff + 1024], AF.Identity, scale=sx_bc[:]),
                inc="s_se")
            v_gse_slot[tt % 2] = v_gse
            if debug and idx == 0 and tt == 0:
                v_cap = emit("vector", lambda e: e.tensor_copy(out=rsq_sb[:, 0:512], in_=psB[:, 0:512]),
                             inc="s_ve", waits=[("s_se", v_gse)])
                v_cap2 = emit("vector", lambda e: e.tensor_copy(out=x2_sb[:, 0, 0:512], in_=psB[:, 512:1024]),
                              inc="s_ve")
                emit("gpsimd", lambda e: e.dma_start(out=dbg["gps"], in_=rsq_sb[:, 0:512]),
                     inc="s_st", n=16, waits=[("s_ve", v_cap2)])
                emit("gpsimd", lambda e: e.dma_start(out=dbg["ups"], in_=x2_sb[:, 0, 0:512]),
                     inc="s_st", n=16)
                emit("gpsimd", lambda e: e.dma_start(out=dbg["sg0"], in_=sg_sb[:]),
                     inc="s_st", n=16, waits=[("s_se", v_gse)])
                emit("gpsimd", lambda e: e.dma_start(out=dbg["su0"], in_=su_sb[:]),
                     inc="s_st", n=16)
                emit("gpsimd", lambda e, cb=cb: e.dma_start(out=dbg["wgt0"], in_=wgt_sb[:, cb, :, :]),
                     inc="s_st", n=16)
                emit("gpsimd", lambda e: e.dma_start(out=dbg["xqT0"], in_=xqT[:, :, 0, :]),
                     inc="s_st", n=16)
            col = 2 * (idx * 8 + tt)
            emit("vector", lambda e, tt=tt: e.tensor_tensor(
                h_bf[:, tt * 512:(tt + 1) * 512], sg_sb[:], su_sb[:], OP.mult),
                waits=[("s_se", v_gse), ("s_st", v_spill_prev)])
            emit("vector", lambda e, tt=tt, col=col: e.tensor_scalar(
                rsq_sb[:, 0:512], h_bf[:, tt * 512:(tt + 1) * 512], 1.0, None, OP.mult, OP.max,
                accum_out=hmx[:, col:col + 1]))
            v_httail_prev = emit("vector", lambda e, tt=tt, col=col: e.tensor_scalar(
                rsq_sb[:, 0:512], h_bf[:, tt * 512:(tt + 1) * 512], -1.0, None, OP.mult, OP.max,
                accum_out=hmx[:, col + 1:col + 2]), inc="s_ve")
        v_gu_hist[cb] = P.val("s_mm")
        v_hprev = P.val("s_ve")
        emit("gpsimd", lambda e, ci=ci: e.dma_start(
            out=h_spill[ci, :, :], in_=h_bf[:]),
            inc="s_st", n=16, waits=[("s_ve", v_hprev)])
        v_spill_prev = P.val("s_st")
        if debug and idx == 0:
            emit("gpsimd", lambda e: e.dma_start(out=dbg["h0"], in_=h_bf[:, 0:512]),
                 inc="s_st", n=16, waits=[("s_ve", v_hprev)])
            v_spill_prev = P.val("s_st")
    v_hlast = P.val("s_ve")
    v_gu_end = P.val("s_mm")

    v_hm = emit("vector", lambda e: e.tensor_reduce(hmx[:, 0:1], hmx[:], AX.X, OP.max), inc="s_ve",
                waits=[("s_ve", v_hlast)])
    v_hw = emit("gpsimd", lambda e: e.dma_start(
        out=scr[0:1, 64:192].rearrange("o (p u) -> p (o u)", p=128), in_=hmx[:, 0:1]),
        inc="s_ex", n=16, waits=[("s_ve", v_hm)])
    v_hr = emit("gpsimd", lambda e: e.dma_start(out=row128[:], in_=scr[0:1, 64:192]),
                inc="s_gld", n=16, waits=[("s_ex", v_hw)])
    vh1 = emit("vector", lambda e: e.tensor_reduce(lmax[0:1, 0:1], row128[:], AX.X, OP.max),
               waits=[("s_gld", v_hr)], inc="s_ve")
    vh2 = emit("vector", lambda e: e.tensor_copy(out=lmax[0:1, 1:2], in_=lmax[0:1, 0:1]),
               waits=[("s_ve", vh1)], inc="s_ve")
    vh3 = emit("vector", lambda e: e.tensor_copy(out=lmax[0:1, 2:4], in_=lmax[0:1, 0:2]),
               waits=[("s_ve", vh2)], inc="s_ve")
    v_lh = emit("vector", lambda e: e.tensor_copy(out=lmax[0:1, 4:8], in_=lmax[0:1, 0:4]), inc="s_ve",
                waits=[("s_ve", vh3)])
    v_hmw = emit("gpsimd", lambda e: e.dma_start(out=hm_in.ap(), in_=lmax[:]),
                 inc="s_ex", n=16, waits=[("s_ve", v_lh)])
    emit("gpsimd", lambda e: e.collective_compute(
        "AllReduce", OP.max, replica_groups=[list(range(8))],
        ins=[hm_in.ap().opt()], outs=[hm_out.ap().opt()]),
        inc="s_cc", waits=[("s_ex", v_hmw)])
    V_HM = P.val("s_cc")
    v_ghm = emit("gpsimd", lambda e: e.dma_start(out=lmax[0:1, 0:1], in_=hm_out.ap()[0:1, 0:1]),
                 inc="s_gld", n=16, waits=[("s_cc", V_HM)])
    if debug:
        emit("gpsimd", lambda e: e.dma_start(out=dbg["hmax"], in_=lmax[:]), inc="s_st", n=16,
             waits=[("s_gld", v_ghm)])
    vsh1 = emit("vector", lambda e: e.tensor_scalar(lmax[0:1, 1:2], lmax[0:1, 0:1], sgc[0:1, 7:8],
                                                    INV127, OP.mult, OP.mult),
                waits=[("s_gld", v_ghm)], inc="s_ve")
    v_sh = emit("vector", lambda e: e.reciprocal(out=lmax[0:1, 2:3], in_=lmax[0:1, 1:2]), inc="s_ve",
                waits=[("s_ve", vsh1)])
    v_shw = emit("gpsimd", lambda e: e.dma_start(out=scr[0:1, 10:12], in_=lmax[0:1, 1:3]),
                 inc="s_ex", n=16, waits=[("s_ve", v_sh)])
    emit("gpsimd", lambda e: e.dma_start(out=sh_bc[:], in_=scr[0:1, 10:11].broadcast_to([128, 1])),
         inc="s_gld", n=16, waits=[("s_ex", v_shw)])
    v_shb = emit("gpsimd", lambda e: e.dma_start(out=rh_bc[:], in_=scr[0:1, 11:12].broadcast_to([128, 1])),
                 inc="s_gld", n=16)

    es2a.close()
    sb2b = mk_sb(es2b)
    wdT_sb = sb2b("wdT_sb", [128, NCH, 1024], BF16)
    h2_sb = sb2b("h2_sb", [128, NCH, 512], BF16)
    kh_bf = sb2b("kh_bf", [128, NCH, 512], BF16)
    z_sb = sb2b("z_sb", [128, 1024])
    zsq_sb = sb2b("zsq_sb", [128, 1024])
    y_sb = sb2b("y_sb", [128, 2, 1024])

    ccv = cc_out.ap().rearrange("(r q) j -> r q j", r=8)
    for ci, _, _ in chunk_slots:
        emit("gpsimd", lambda e, ci=ci: e.dma_start(
            out=wdT_sb[:, ci, :].rearrange("p (r j) -> p r j", r=8),
            in_=ccv[:, 6 * 1024 + ci * 128: 6 * 1024 + (ci + 1) * 128, :].transpose([1, 0, 2])),
            inc="s_gld", n=16,
            waits=[("s_cc", V_AG), ("s_mm", v_gu_end), ("s_ve", v_hlast),
                   ("s_se", P.val("s_se"))])
    v_wd = P.val("s_gld")

    v_all_spill = v_spill_prev
    v_yprev = [0, 0]
    v_dn_hist = 0
    v_evac_slot = [0, 0, 0, 0]
    v_yt_prev = 0
    v_quant_prev = 0
    for g in range(8):
        v_hl = emit("gpsimd", lambda e, g=g: e.dma_start(
            out=h2_sb[:],
            in_=h_spill[:, :, g * 512:(g + 1) * 512].transpose([1, 0, 2])),
            inc="s_gld", n=16,
            waits=[("s_st", v_all_spill), ("s_ve", v_quant_prev),
                   ("s_mm", v_gu_end)])
        P.wait("vector", "s_gld", max(v_hl, v_shb))
        P.wait("vector", "s_mm", v_dn_hist)
        for sub in range(11):
            emit("vector", lambda e, sub=sub: e.tensor_scalar(
                zsq_sb[:], h2_sb[:, 2 * sub:2 * sub + 2, :].rearrange("p c t -> p (c t)"),
                rh_bc[:], MAGIC, OP.mult, OP.add))
            emit("vector", lambda e: e.tensor_scalar(
                zsq_sb[:], zsq_sb[:], MAGIC + 127.0, MAGIC - 128.0, OP.min, OP.max))
            last = (sub == 10)
            emit("vector", lambda e, sub=sub: e.tensor_scalar(
                kh_bf[:, 2 * sub:2 * sub + 2, :].rearrange("p c t -> p (c t)"), zsq_sb[:],
                MAGIC, None, OP.subtract), inc="s_ve" if last else None)
        v_quant_prev = P.val("s_ve")
        v_kh = v_quant_prev
        if debug and g == 0:
            emit("gpsimd", lambda e: e.dma_start(out=dbg["kh"], in_=kh_bf[:]),
                 inc="s_st", n=16, waits=[("s_ve", v_kh)])
        for tc in range(4):
            t_glob = g * 4 + tc
            for hh in range(2):
                slot = (tc * 2 + hh) % 4

                def dn_group(e, tc=tc, hh=hh, slot=slot):
                    for ci in range(NCH):
                        inst = e.matmul(psA[:, slot * 512:(slot + 1) * 512],
                                        kh_bf[:, ci, tc * 128:(tc + 1) * 128],
                                        wdT_sb[:, ci, hh * 512:(hh + 1) * 512],
                                        start=(ci == 0), stop=(ci == NCH - 1))
                    return inst
                v_dn = emit("tensor", dn_group, inc="s_mm",
                            waits=[("s_ve", max(v_kh, v_evac_slot[slot])), ("s_gld", v_wd)])
                emit("vector", lambda e, t_glob=t_glob, hh=hh, slot=slot: e.scalar_tensor_tensor(
                    z_sb[:, hh * 512:(hh + 1) * 512], psA[:, slot * 512:(slot + 1) * 512],
                    sh_bc[:], x1_sb[:, t_glob, hh * 512:(hh + 1) * 512], OP.mult, OP.add,
                    accum_out=st2[:, hh:hh + 1]),
                    inc="s_ve",
                    waits=[("s_mm", v_dn), ("s_gld", v_shb), ("s_se", v_yt_prev)])
                v_evac_slot[slot] = P.val("s_ve")
            w1 = emit("vector", lambda e: e.tensor_tensor(st2[:, 0:1], st2[:, 0:1], st2[:, 1:2], OP.add),
                      waits=[("s_ve", P.val("s_ve"))], inc="s_ve")
            v_zacc = emit("vector", lambda e: e.scalar_tensor_tensor(
                zsq_sb[:], z_sb[:], 0.0, z_sb[:], OP.add, OP.mult, accum_out=st2[:, 1:2]),
                inc="s_ve")
            w2 = emit("vector", lambda e: e.tensor_scalar(st2[:, 0:1], st2[:, 0:1], float(1.0 / D), None, OP.mult),
                      waits=[("s_ve", v_zacc)], inc="s_ve")
            w3 = emit("vector", lambda e: e.tensor_tensor(st2[:, 3:4], st2[:, 0:1], st2[:, 0:1], OP.mult),
                      waits=[("s_ve", w2)], inc="s_ve")
            w4 = emit("vector", lambda e: e.scalar_tensor_tensor(st2[:, 2:3], st2[:, 1:2], float(1.0 / D),
                                                                 st2[:, 3:4], OP.mult, OP.subtract),
                      waits=[("s_ve", w3)], inc="s_ve")
            w5 = emit("vector", lambda e: e.tensor_scalar(st2[:, 2:3], st2[:, 2:3], EPS_LN, None, OP.add),
                      waits=[("s_ve", w4)], inc="s_ve")
            v_v2 = emit("vector", lambda e: e.reciprocal(out=st2[:, 2:3], in_=st2[:, 2:3]), inc="s_ve",
                        waits=[("s_ve", w5)])
            v_rs2 = emit("scalar", lambda e: e.activation(t_rstd[:, 0:1], st2[:, 2:3], AF.Sqrt),
                         inc="s_se", waits=[("s_ve", v_v2)])
            v_b2 = emit("vector", lambda e: e.tensor_scalar(t_rstd[:, 1:2], st2[:, 0:1], t_rstd[:, 0:1],
                                                            -1.0, OP.mult, OP.mult),
                        inc="s_ve", waits=[("s_se", v_rs2)])
            yb = t_glob % 2
            v_yt = emit("scalar", lambda e, yb=yb: e.activation(
                y_sb[:, yb, :], z_sb[:], AF.Identity, bias=t_rstd[:, 1:2], scale=t_rstd[:, 0:1]),
                inc="s_se", waits=[("s_se", v_rs2), ("s_ve", v_b2), ("s_st", v_yprev[yb])])
            v_yt_prev = v_yt
            if debug and t_glob == 0:
                emit("gpsimd", lambda e: e.dma_start(out=dbg["z0"], in_=z_sb[:]),
                     inc="s_st", n=16, waits=[("s_ve", v_b2)])
            emit("gpsimd", lambda e, t_glob=t_glob, yb=yb: e.dma_start(
                out=y_out[t_glob * 128:(t_glob + 1) * 128, :], in_=y_sb[:, yb, :]),
                inc="s_st", n=16, waits=[("s_se", v_yt)])
            v_yprev[yb] = P.val("s_st")
        v_dn_hist = P.val("s_mm")

    if debug in ("lite", "scr", "scr_nowrite"):
        lt_scr = nc.dram_tensor("lt_scr", [1, 256], F32, kind="ExternalOutput").ap()
        lt_aloc = (nc.dram_tensor("lt_aloc", [TPC, D], BF16, kind="ExternalOutput").ap()
                   if debug == "lite" else None)
        if debug == "lite":
            lt_x1 = nc.dram_tensor("lt_x1", [128, NTILE, 1024], BF16, kind="ExternalOutput").ap()
            lt_h = nc.dram_tensor("lt_h", [NCH, 128, TPC], BF16, kind="ExternalOutput").ap()
            lt_ax = nc.dram_tensor("lt_ax", [S, D], BF16, kind="ExternalOutput").ap()
        v_pre = P.val("s_st")
        if debug != "scr_nowrite":
            v_l1 = emit("gpsimd", lambda e: e.dma_start(out=row128[:], in_=scr[0:1, 0:128]),
                        inc="s_gld", n=16,
                        waits=[("s_gld", P.val("s_gld")), ("s_st", v_pre)])
            emit("gpsimd", lambda e: e.dma_start(out=lt_scr[0:1, 0:128], in_=row128[:]), inc="s_st", n=16,
                 waits=[("s_gld", v_l1)])
        if debug == "lite":
            emit("gpsimd", lambda e: e.dma_start(out=lt_aloc, in_=a_loc), inc="s_st", n=16)
            emit("gpsimd", lambda e: e.dma_start(out=lt_x1, in_=x1_sb[:]), inc="s_st", n=16,
                 waits=[("s_se", P.val("s_se"))])
            emit("gpsimd", lambda e: e.dma_start(out=lt_h, in_=h_spill), inc="s_st", n=16)
            emit("gpsimd", lambda e: e.dma_start(out=lt_ax, in_=a_xchg), inc="s_st", n=16)
    v_fin = P.val("s_st")
    P.wait("gpsimd", "s_st", v_fin)
    return _finalize(nc, P, dbg)


def _finalize(nc, P, dbg):
    with nc.Block() as block:
        @block.sync
        def _(e):
            for fn in P.ops["sync"]:
                fn(e)

        @block.gpsimd
        def _(e):
            for fn in P.ops["gpsimd"]:
                fn(e)

        @block.vector
        def _(e):
            for fn in P.ops["vector"]:
                fn(e)

        @block.scalar
        def _(e):
            for fn in P.ops["scalar"]:
                fn(e)

        @block.tensor
        def _(e):
            for fn in P.ops["tensor"]:
                fn(e)

    return nc, dbg


def make_in_maps(inputs):
    x = np.ascontiguousarray(inputs["L0_attn_x"], dtype=np.float32)
    Wq = np.asarray(inputs["L0_attn_Wq"], dtype=np.float32)
    Wk = np.asarray(inputs["L0_attn_Wk"], dtype=np.float32)
    Wv = np.asarray(inputs["L0_attn_Wv"], dtype=np.float32)
    bq = np.asarray(inputs["L0_attn_bq"], dtype=np.float32)
    bk = np.asarray(inputs["L0_attn_bk"], dtype=np.float32)
    bv = np.asarray(inputs["L0_attn_bv"], dtype=np.float32)
    beta = np.asarray(inputs["L0_attn_beta"], dtype=np.float32)
    clipx = np.asarray(inputs["L0_ffn_clip_x"], dtype=np.float32)
    clipd = np.asarray(inputs["L0_ffn_clip_down"], dtype=np.float32)
    wg = np.asarray(inputs["L0_ffn_w_gate"], dtype=np.float32)
    wu = np.asarray(inputs["L0_ffn_w_up"], dtype=np.float32)
    wd = np.asarray(inputs["L0_ffn_w_down"], dtype=np.float32)

    tril = np.tril(np.ones((128, 128))).astype(ml_dtypes.bfloat16)
    in_maps = []
    for c in range(8):
        b, p = c // 2, c % 2
        fs = slice(p * 512, (p + 1) * 512)

        def prep_rows(w, c=c):
            rows = []
            for sslot in range(PREP_SLOTS):
                ci = c + 8 * sslot
                if ci < NCH:
                    rows.append(w[ci * 128:(ci + 1) * 128, :])
                else:
                    rows.append(np.zeros((128, D), np.float32))
            return np.ascontiguousarray(np.concatenate(rows, axis=0))
        in_maps.append({
            "x": np.ascontiguousarray(x[b]),
            "x2": np.ascontiguousarray(x[b, p * TPC:(p + 1) * TPC, :]),
            "wq": np.ascontiguousarray(Wq[:, fs]),
            "wk": np.ascontiguousarray(Wk[:, fs]),
            "wv": np.ascontiguousarray(Wv[:, fs]),
            "bq": np.ascontiguousarray(bq[fs]).reshape(1, 512),
            "bk": np.ascontiguousarray(bk[fs]).reshape(1, 512),
            "bv": np.ascontiguousarray(bv[fs]).reshape(1, 512),
            "beta": np.ascontiguousarray(beta[p * 4:(p + 1) * 4]).reshape(1, 4),
            "clipx": np.ascontiguousarray(clipx).reshape(1, 1),
            "clipd": np.ascontiguousarray(clipd).reshape(1, 1),
            "tril": tril,
            "wg_prep": prep_rows(wg),
            "wu_prep": prep_rows(wu),
            "wd_prep": np.ascontiguousarray(wd[c * 128:(c + 1) * 128, :]),
        })
    return in_maps


def kernel(**inputs):
    in_maps = make_in_maps(inputs)
    nc, _ = build_program(debug=False)
    res = run_bass_kernel_spmd(nc, in_maps, core_ids=list(range(8)))
    out = np.empty((B, S, D), dtype=np.float32)
    for c in range(8):
        b, p = c // 2, c % 2
        out[b, p * TPC:(p + 1) * TPC, :] = res.results[c]["y"]
    return out


# version 06
# speedup vs baseline: 1.1992x; 1.0428x over previous
import numpy as np
import ml_dtypes
from contextlib import ExitStack

import concourse.bass as bass
import concourse.mybir as mybir
from concourse.bass_utils import run_bass_kernel_spmd

F32 = mybir.dt.float32
BF16 = mybir.dt.bfloat16
I8 = mybir.dt.int8
AF = mybir.ActivationFunctionType
OP = mybir.AluOpType
AX = mybir.AxisListType
ds = bass.ds

f32 = np.float32

B, S, D = 4, 8192, 1024
HPC = 4
NSEG = 16
TPC = 4096
NTILE = 32
II = 2816
NCH = 22
PREP_SLOTS = 3
EPS_ATT = f32(1e-6)
EPS_LN = 1e-5
MAGIC = 12582912.0
INV_SQRT_DK = f32(1.0 / np.sqrt(128.0))
INV127 = 1.0 / 127.0
CCR = PREP_SLOTS * 1024 * 2 + II

ENGS = ("sync", "gpsimd", "vector", "scalar", "tensor")


class Prog:
    def __init__(self):
        self.ops = {e: [] for e in ENGS}
        self.ctr = {}
        self.waited = {e: {} for e in ENGS}
        self.sems = {}

    def inc(self, sem, n=1):
        self.ctr[sem] = self.ctr.get(sem, 0) + n
        return self.ctr[sem]

    def val(self, sem):
        return self.ctr.get(sem, 0)

    def wait(self, eng, sem, v=None):
        if v is None:
            v = self.val(sem)
        if v <= 0:
            return
        if self.waited[eng].get(sem, -1) >= v:
            return
        self.waited[eng][sem] = v

        def fn(e, sem=sem, v=v):
            e.wait_ge(self.sems[sem], v)
        self.ops[eng].append(fn)


def build_program(debug=False, phases=4, seg_detail="full"):
    ph_dump = phases if isinstance(phases, str) else None
    if ph_dump is not None:
        phases = 3.5 if ph_dump == "x1" else 2.5
    nc = bass.Bass("TRN2", target_bir_lowering=False, detect_race_conditions=False)
    P = Prog()

    x_in = nc.dram_tensor("x", [S, D], F32, kind="ExternalInput").ap()
    wq_in = nc.dram_tensor("wq", [D, 512], F32, kind="ExternalInput").ap()
    wk_in = nc.dram_tensor("wk", [D, 512], F32, kind="ExternalInput").ap()
    wv_in = nc.dram_tensor("wv", [D, 512], F32, kind="ExternalInput").ap()
    bq_in = nc.dram_tensor("bq", [1, 512], F32, kind="ExternalInput").ap()
    bk_in = nc.dram_tensor("bk", [1, 512], F32, kind="ExternalInput").ap()
    bv_in = nc.dram_tensor("bv", [1, 512], F32, kind="ExternalInput").ap()
    beta_in = nc.dram_tensor("beta", [1, HPC], F32, kind="ExternalInput").ap()
    clipx_in = nc.dram_tensor("clipx", [1, 1], F32, kind="ExternalInput").ap()
    clipd_in = nc.dram_tensor("clipd", [1, 1], F32, kind="ExternalInput").ap()
    tril_in = nc.dram_tensor("tril", [128, 128], BF16, kind="ExternalInput").ap()
    wgp_in = nc.dram_tensor("wg_prep", [PREP_SLOTS * 128, D], F32, kind="ExternalInput").ap()
    wup_in = nc.dram_tensor("wu_prep", [PREP_SLOTS * 128, D], F32, kind="ExternalInput").ap()
    wdp_in = nc.dram_tensor("wd_prep", [128, II], F32, kind="ExternalInput").ap()

    x2_in = nc.dram_tensor("x2", [TPC, D], F32, kind="ExternalInput").ap()
    y_out = nc.dram_tensor("y", [TPC, D], F32, kind="ExternalOutput").ap()

    a_xchg = nc.dram_tensor("a_xchg", [S, D], BF16, kind="Internal", addr_space="Shared").ap()
    ax_priv = nc.dram_tensor("ax_priv", [NSEG, 4, 128, 512], BF16, kind="Internal").ap()
    a_loc = nc.dram_tensor("a_loc", [TPC, D], BF16, kind="Internal").ap()
    h_spill = nc.dram_tensor("h_spill", [NCH, 128, TPC], BF16, kind="Internal").ap()
    scr = nc.dram_tensor("scr", [1, 256], F32, kind="Internal").ap()
    cc_in = nc.dram_tensor("cc_in", [CCR, 128], BF16, kind="Internal")
    cc_out = nc.dram_tensor("cc_out", [8 * CCR, 128], BF16, kind="Internal")
    bar_in = nc.dram_tensor("bar_in", [1, 8], F32, kind="Internal")
    bar_out = nc.dram_tensor("bar_out", [1, 8], F32, kind="Internal")
    mx_in = nc.dram_tensor("mx_in", [1, 8], F32, kind="Internal")
    mx_out = nc.dram_tensor("mx_out", [1, 8], F32, kind="Internal")
    hm_in = nc.dram_tensor("hm_in", [1, 8], F32, kind="Internal")
    hm_out = nc.dram_tensor("hm_out", [1, 8], F32, kind="Internal")

    dbg = {}
    if debug:
        def dbg_t(name, shape, dt=F32):
            dbg[name] = nc.dram_tensor("dbg_" + name, shape, dt, kind="ExternalOutput").ap()
        dbg_t("qT", [128, HPC, 512])
        dbg_t("sqT", [128, HPC, 512])
        dbg_t("skT", [128, HPC, 512])
        dbg_t("v", [128, 4, HPC, 130])
        dbg_t("p", [128, 4, 512])
        dbg_t("M", [128, HPC, 130])
        dbg_t("astage", [128, 4, 512])
        dbg_t("r0", [128, 1024])
        dbg_t("gmax", [1, 8])
        dbg_t("h0", [128, 512])
        dbg_t("sg0", [128, 512])
        dbg_t("su0", [128, 512])
        dbg_t("gps", [128, 512])
        dbg_t("ups", [128, 512])
        dbg_t("xqT0", [128, 8, 128])
        dbg_t("wgt0", [128, 8, 128])
        dbg_t("hmax", [1, 8])
        dbg_t("kh", [128, NCH, 512])
        dbg_t("z0", [128, 1024])

    es0 = ExitStack()
    es1 = ExitStack()
    es2p = ExitStack()
    es2a = ExitStack()
    es2b = ExitStack()

    def mk_sb(stack):
        def sbf(name, shape, dt=F32):
            return stack.enter_context(nc.sbuf_tensor(name, shape, dt))
        return sbf

    sb0, sb1 = mk_sb(es0), mk_sb(es1)

    sgc = sb0("sgc", [1, 8])
    g_bc = sb0("g_bc", [128, HPC])
    gm1_bc = sb0("gm1_bc", [128, HPC])
    ones_r = sb0("ones_r", [1, 512], BF16)
    tril_bf = sb0("tril_bf", [128, 128], BF16)
    lmax = sb0("lmax", [1, 8])
    row128 = sb0("row128", [1, 128])
    sx_bc = sb0("sx_bc", [128, 1])
    rx_bc = sb0("rx_bc", [128, 1])
    sh_bc = sb0("sh_bc", [128, 1])
    rh_bc = sb0("rh_bc", [128, 1])
    st2 = sb0("st2", [128, 4])
    t_rstd = sb0("t_rstd", [128, 2])
    xmx = sb0("xmx", [128, 2 * NTILE])
    hmx = sb0("hmx", [128, 2 * 8 * NCH])

    psA = es0.enter_context(nc.psum_tensor("psA", [128, 2048], F32))
    psB = es0.enter_context(nc.psum_tensor("psB", [128, 2048], F32))

    for sn in ["s_ld", "s_gld", "s_tp", "s_mm", "s_ve", "s_se", "s_st", "s_cc", "s_ex"]:
        P.sems[sn] = es0.enter_context(nc.semaphore(sn))
    SEM = P.sems

    wq_bf = sb1("wq_bf", [128, 8, 512], BF16)
    wk_bf = sb1("wk_bf", [128, 8, 512], BF16)
    wv_bf = sb1("wv_bf", [128, 8, 512], BF16)
    bq_bf = sb1("bq_bf", [1, 512], BF16)
    bk_bf = sb1("bk_bf", [1, 512], BF16)
    bv_bf = sb1("bv_bf", [1, 512], BF16)
    x_tm = sb1("x_tm", [128, 2, 4, 1024], BF16)
    x_fm = sb1("x_fm", [128, 2, 4, 8, 128], BF16)
    q_bf = sb1("q_bf", [128, HPC, 512], BF16)
    k_bf = sb1("k_bf", [128, HPC, 512], BF16)
    sq_bf = sb1("sq_bf", [128, HPC, 512], BF16)
    sk_bf = sb1("sk_bf", [128, HPC, 512], BF16)
    v_bf = sb1("v_bf", [128, 4, HPC, 130], BF16)
    sk_tm = sb1("sk_tm", [128, HPC, 4, 128], BF16)
    p_bf = sb1("p_bf", [128, 4, 512], BF16)
    pT_bf = sb1("pT_bf", [128, 4, 4, 128], BF16)
    vd_bf = sb1("vd_bf", [128, 4, 130], BF16)
    M_f32 = sb1("M_f32", [128, HPC, 130])
    M_bf = sb1("M_bf", [128, HPC, 130], BF16)
    t_scr1 = sb1("t_scr1", [128, 2048])
    t_scr2 = sb1("t_scr2", [128, 2048])
    t_den = sb1("t_den", [128, 8])
    t_rq = sb1("t_rq", [128, 8])
    t_rqg = sb1("t_rqg", [128, 4])
    t_rkn = sb1("t_rkn", [128, 4])
    t_rd = sb1("t_rd", [128, 4])
    t_r2g = sb1("t_r2g", [128, 4])
    t_nm = sb1("t_nm", [128, 4])
    a_stage = sb1("a_stage", [128, 2, 4, 512], BF16)
    wp_sb = sb1("wp_sb", [128, PREP_SLOTS, 1024])
    wup_sb = sb1("wup_sb", [128, PREP_SLOTS, 1024])
    wdp_sb = sb1("wdp_sb", [128, II])
    wqz = sb1("wqz", [128, 1024], BF16)
    wdz = sb1("wdz", [128, II], BF16)
    wT_sb = sb1("wT_sb", [128, 8, 128], BF16)
    wdT_s = sb1("wdT_s", [128, NCH, 128], BF16)
    t_rm = sb1("t_rm", [128, 4])

    def emit(eng_name, fn, inc=None, n=1, waits=()):
        for (sm, v) in waits:
            P.wait(eng_name, sm, v)
        tgt = P.inc(inc, n) if inc else None

        def run(e, fn=fn, inc=inc, n=n):
            inst = fn(e)
            if inc is not None and inst is not None:
                inst.then_inc(SEM[inc], n)
        P.ops[eng_name].append(run)
        return tgt

    v_wgp = emit("sync", lambda e: e.dma_start(out=wp_sb[:], in_=wgp_in.rearrange("(c p) d -> p c d", p=128)), inc="s_ld", n=16)
    v_wup = emit("sync", lambda e: e.dma_start(out=wup_sb[:], in_=wup_in.rearrange("(c p) d -> p c d", p=128)), inc="s_ld", n=16)
    v_wdp = emit("sync", lambda e: e.dma_start(out=wdp_sb[:], in_=wdp_in), inc="s_ld", n=16)
    for (wi, wb) in ((wq_in, wq_bf), (wk_in, wk_bf), (wv_in, wv_bf)):
        emit("gpsimd", lambda e, wi=wi, wb=wb: e.dma_start(
            out=wb[:], in_=wi.rearrange("(c p) m -> p c m", p=128)), inc="s_gld", n=16)
    for (bi, bb) in ((bq_in, bq_bf), (bk_in, bk_bf), (bv_in, bv_bf)):
        emit("gpsimd", lambda e, bi=bi, bb=bb: e.dma_start(out=bb[:], in_=bi), inc="s_gld", n=16)
    v_wcast = P.val("s_gld")
    emit("sync", lambda e: e.dma_start(out=sgc[0:1, 0:4], in_=beta_in[0:1, :]), inc="s_ld", n=16)
    emit("sync", lambda e: e.dma_start(out=sgc[0:1, 4:5], in_=clipx_in[0:1, :]), inc="s_ld", n=16)
    emit("sync", lambda e: e.dma_start(out=sgc[0:1, 5:6], in_=clipd_in[0:1, :]), inc="s_ld", n=16)
    emit("sync", lambda e: e.dma_start(out=tril_bf[:], in_=tril_in), inc="s_ld", n=16)
    v_small = P.val("s_ld")
    emit("vector", lambda e: e.memset(ones_r[:], 1.0))
    emit("vector", lambda e: e.memset(p_bf.ap().rearrange("p a b -> p (a b)"), 0.0))
    emit("vector", lambda e: e.memset(v_bf.ap().rearrange("p a b c -> p (a b c)"), 1.0))
    emit("vector", lambda e: e.memset(vd_bf.ap().rearrange("p a b -> p (a b)"), 1.0))
    emit("vector", lambda e: e.memset(M_f32.ap().rearrange("p h c -> p (h c)"), 0.0))
    emit("vector", lambda e: e.memset(M_bf.ap().rearrange("p h c -> p (h c)"), 0.0), inc="s_ve")
    v_const = P.val("s_ve")
    emit("scalar", lambda e: e.activation(sgc[0:1, 6:7], sgc[0:1, 4:5], AF.Sigmoid),
         waits=[("s_ld", v_small)])
    emit("scalar", lambda e: e.activation(sgc[0:1, 7:8], sgc[0:1, 5:6], AF.Sigmoid))
    v_sig = emit("scalar", lambda e: e.activation(sgc[0:1, 0:4], sgc[0:1, 0:4], AF.Sigmoid), inc="s_se")
    v_gw = emit("gpsimd", lambda e: e.dma_start(out=scr[0:1, 0:8], in_=sgc[0:1, 0:8]),
                inc="s_ex", n=16, waits=[("s_se", v_sig)])
    v_gbc = emit("gpsimd", lambda e: e.dma_start(out=g_bc[:], in_=scr[0:1, 0:4].broadcast_to([128, 4])),
                 inc="s_ex", n=16, waits=[("s_ex", v_gw)])
    v_gm1 = emit("vector", lambda e: e.tensor_scalar(gm1_bc[:], g_bc[:], -1.0, 1.0, OP.mult, OP.add),
                 inc="s_ve", waits=[("s_ex", v_gbc)])

    def prep_one(kind, slot):
        if kind == "g":
            src_ap, row_base, wait_ld = wp_sb[:, slot, :], slot * 1024, v_wgp
        elif kind == "u":
            src_ap, row_base, wait_ld = wup_sb[:, slot, :], (PREP_SLOTS + slot) * 1024, v_wup
        else:
            src_ap, row_base, wait_ld = wdp_sb[:], 2 * PREP_SLOTS * 1024, v_wdp
        p1 = emit("vector", lambda e: e.tensor_reduce(t_rm[:, 0:1], src_ap, AX.X, OP.max,
                                                      apply_absolute_value=True),
                  waits=[("s_ld", wait_ld), ("s_tp", P.val("s_tp"))], inc="s_ve")
        p2 = emit("vector", lambda e: e.tensor_scalar(t_rm[:, 1:2], t_rm[:, 0:1], 1e-30, None, OP.max),
                  waits=[("s_ve", p1)], inc="s_ve")
        p3 = emit("vector", lambda e: e.reciprocal(out=t_rm[:, 2:3], in_=t_rm[:, 1:2]),
                  waits=[("s_ve", p2)], inc="s_ve")
        p4 = emit("vector", lambda e: e.tensor_scalar(t_rm[:, 2:3], t_rm[:, 2:3], 127.0, None, OP.mult),
                  waits=[("s_ve", p3)], inc="s_ve")
        p5 = emit("vector", lambda e: e.tensor_scalar(t_rm[:, 3:4], t_rm[:, 1:2], INV127, None, OP.mult),
                  waits=[("s_ve", p4)], inc="s_ve")
        if kind != "d":
            emit("vector", lambda e: e.tensor_scalar(t_scr1[:, 0:1024], src_ap, t_rm[:, 2:3], MAGIC, OP.mult, OP.add),
                 waits=[("s_ve", p5)])
            v_q = emit("vector", lambda e: e.tensor_scalar(wqz[:], t_scr1[:, 0:1024], MAGIC, t_rm[:, 3:4], OP.subtract, OP.mult), inc="s_ve")
            for half in range(2):
                emit("sync", lambda e, half=half: e.dma_start(
                    out=wT_sb[:, half * 4:half * 4 + 4, :],
                    in_=wqz[:, half * 512:(half + 1) * 512], transpose=True),
                    inc="s_tp", n=16, waits=[("s_ve", v_q)])
            v_t = P.val("s_tp")
            emit("gpsimd", lambda e, row_base=row_base: e.dma_start(
                out=cc_in.ap()[row_base:row_base + 1024, :].rearrange("(c p) i -> p c i", p=128),
                in_=wT_sb[:]), inc="s_ex", n=16, waits=[("s_tp", v_t)])
        else:
            for part, (o0, o1) in enumerate(((0, 1024), (1024, 2048), (2048, 2816))):
                w = o1 - o0
                emit("vector", lambda e, o0=o0, w=w: e.tensor_scalar(
                    t_scr1[:, 0:w], src_ap[:, o0:o0 + w], t_rm[:, 2:3], MAGIC, OP.mult, OP.add),
                    waits=[("s_ve", p5)] if o0 == 0 else ())
                emit("vector", lambda e, o0=o0, w=w: e.tensor_scalar(
                    wdz[:, o0:o0 + w], t_scr1[:, 0:w], MAGIC, t_rm[:, 3:4], OP.subtract, OP.mult),
                    inc="s_ve" if o1 == 2816 else None)
            v_q = P.val("s_ve")
            off = 0
            while off < II:
                wch = min(512, II - off)
                emit("sync", lambda e, off=off, wch=wch: e.dma_start(
                    out=wdT_s[:, off // 128: (off + wch) // 128, :],
                    in_=wdz[:, off:off + wch], transpose=True),
                    inc="s_tp", n=16, waits=[("s_ve", v_q)])
                off += wch
            v_t = P.val("s_tp")
            emit("gpsimd", lambda e, row_base=row_base: e.dma_start(
                out=cc_in.ap()[row_base:row_base + II, :].rearrange("(c p) i -> p c i", p=128),
                in_=wdT_s[:]), inc="s_ex", n=16, waits=[("s_tp", v_t)])

    for slot in range(PREP_SLOTS):
        prep_one("g", slot)
    for slot in range(PREP_SLOTS):
        prep_one("u", slot)
    prep_one("d", 0)
    v_prep = P.val("s_ex")
    emit("gpsimd", lambda e: e.collective_compute(
        "AllGather", OP.bypass, replica_groups=[list(range(8))],
        ins=[cc_in.ap().opt()], outs=[cc_out.ap().opt()]),
        inc="s_cc", waits=[("s_ex", v_prep)])
    V_AG = P.val("s_cc")

    pcol_holder = {}

    def g_pcol(e):
        if "v" not in pcol_holder:
            pid = e.partition_id()
            pcol_holder["v"] = e.snap((pid % 2) * 512, min_val=0, max_val=512)
        return pcol_holder["v"]

    v_Mbf = [v_const] * HPC
    v_pT_qc = [0, 0, 0, 0]
    v_head_vec_prev = v_const
    v_am_prev = 0
    v_comb_prev = v_const
    v_xfm_hist = [0, 0]
    v_mmv_hist = [0, 0]
    v_astore_hist = [0, 0]
    v_axp_hist = [0, 0]

    for s in range(NSEG if phases >= 2 else 0):
        sp = s % 2
        v_xld = emit("gpsimd", lambda e, s=s, sp=sp: e.dma_start(
            out=x_tm[:, sp, :, :],
            in_=x_in[s * 512:(s + 1) * 512, :].rearrange("(tc p) d -> p tc d", p=128)),
            inc="s_gld", n=16, waits=[("s_tp", v_xfm_hist[sp])])
        for tc in range(4):
            for half in range(2):
                emit("sync", lambda e, sp=sp, tc=tc, half=half: e.dma_start(
                    out=x_fm[:, sp, tc, half * 4:half * 4 + 4, :],
                    in_=x_tm[:, sp, tc, half * 512:(half + 1) * 512], transpose=True),
                    inc="s_tp", n=16,
                    waits=[("s_gld", v_xld), ("s_mm", v_mmv_hist[sp])])
        v_xfm = P.val("s_tp")
        v_xfm_hist[sp] = v_xfm
        if seg_detail == "xonly":
            P.wait("gpsimd", "s_tp", v_xfm)
            continue

        def proj_group(e, w_sb_, b_sb_, f, dst, sp):
            for d in range(8):
                e.matmul(dst, w_sb_[:, d, f * 128:(f + 1) * 128],
                         x_fm[:, sp, :, d, :],
                         start=(d == 0), stop=False)
            return e.matmul(dst, b_sb_[0:1, f * 128:(f + 1) * 128], ones_r[:],
                            start=False, stop=True)

        _gates = {"qmm": 1, "qevac1": 1.2, "qevac2": 1.4, "qevac3": 1.6, "qevac": 2, "vmm": 3, "kevac": 4, "vevac": 5, "sk": 6, "proj": 7, "full": 99}
        _lvl = _gates.get(seg_detail, 99)
        q_waits = [("s_tp", v_xfm), ("s_gld", v_wcast),
                   ("s_ve", max(v_const, v_head_vec_prev)), ("s_se", v_am_prev)]
        for f in range(HPC):
            emit("tensor", lambda e, f=f, sp=sp: proj_group(e, wq_bf, bq_bf, f, psA[:, f * 512:(f + 1) * 512], sp),
                 inc="s_mm", waits=q_waits if f == 0 else ())
        v_mm_q = P.val("s_mm")
        for f in range(HPC):
            emit("tensor", lambda e, f=f, sp=sp: proj_group(e, wk_bf, bk_bf, f, psB[:, f * 512:(f + 1) * 512], sp),
                 inc="s_mm")
        v_mm_k = P.val("s_mm")

        if _lvl < 1.2:
            P.wait("gpsimd", "s_mm", P.val("s_mm"))
            continue
        v_qcopy = emit("scalar", lambda e: e.activation(q_bf.ap().rearrange("p h t -> p (h t)"), psA[:],
                                                        AF.Copy, scale=INV_SQRT_DK),
                       inc="s_se", waits=[("s_mm", v_mm_q)])
        if _lvl < 1.4:
            P.wait("gpsimd", "s_se", P.val("s_se"))
            continue
        v_minq = emit("vector", lambda e: e.tensor_scalar(t_scr1[:], psA[:], 0.0, None, OP.min),
                      inc="s_ve", waits=[("s_mm", v_mm_q), ("s_se", v_qcopy)])
        if _lvl < 1.6:
            P.wait("gpsimd", "s_se", P.val("s_se"))
            P.wait("gpsimd", "s_ve", P.val("s_ve"))
            continue
        v_exq = emit("scalar", lambda e: e.activation(t_scr2[:], t_scr1[:], AF.Exp),
                     inc="s_se", waits=[("s_ve", v_minq)])
        if _lvl < 2:
            P.wait("gpsimd", "s_se", P.val("s_se"))
            P.wait("gpsimd", "s_ve", P.val("s_ve"))
            continue
        v_evac_q = emit("vector", lambda e: e.scalar_tensor_tensor(
            sq_bf.ap().rearrange("p h t -> p (h t)"), psA[:], 0.0, t_scr2[:], OP.max, OP.add),
            inc="s_ve", waits=[("s_se", v_exq)])

        if _lvl < 3:
            P.wait("gpsimd", "s_ve", P.val("s_ve"))
            P.wait("gpsimd", "s_se", P.val("s_se"))
            continue
        for tc in range(4):
            def vgroup(e, tc=tc, sp=sp):
                for d in range(8):
                    e.matmul(psA[:, tc * 512:(tc + 1) * 512],
                             x_fm[:, sp, tc, d, :],
                             wv_bf[:, d, :], start=(d == 0), stop=False)
                return e.matmul(psA[:, tc * 512:(tc + 1) * 512], ones_r[0:1, 0:128],
                                bv_bf[:], start=False, stop=True)
            emit("tensor", vgroup, inc="s_mm",
                 waits=[("s_ve", v_evac_q), ("s_se", v_exq)] if tc == 0 else ())
        v_mm_v = P.val("s_mm")
        v_mmv_hist[sp] = v_mm_v

        if _lvl < 4:
            P.wait("gpsimd", "s_mm", P.val("s_mm"))
            continue
        v_kcopy = emit("scalar", lambda e: e.activation(k_bf.ap().rearrange("p h t -> p (h t)"), psB[:],
                                                        AF.Copy, scale=1.0),
                       inc="s_se", waits=[("s_mm", v_mm_k)])
        v_mink = emit("vector", lambda e: e.tensor_scalar(t_scr1[:], psB[:], 0.0, None, OP.min),
                      inc="s_ve", waits=[("s_mm", v_mm_k), ("s_se", v_kcopy)])
        v_exk = emit("scalar", lambda e: e.activation(t_scr2[:], t_scr1[:], AF.Exp),
                     inc="s_se", waits=[("s_ve", v_mink)])
        v_evac_k = emit("vector", lambda e: e.scalar_tensor_tensor(
            sk_bf.ap().rearrange("p h t -> p (h t)"), psB[:], 0.0, t_scr2[:], OP.max, OP.add),
            inc="s_ve", waits=[("s_se", v_exk)])

        if _lvl < 5:
            P.wait("gpsimd", "s_ve", P.val("s_ve"))
            P.wait("gpsimd", "s_se", P.val("s_se"))
            continue
        for tc in range(4):
            emit("vector", lambda e, tc=tc: e.tensor_copy(
                out=v_bf[:, tc, :, 0:128],
                in_=psA.ap().rearrange("p (tc h f) -> p tc h f", tc=4, h=HPC)[:, tc, :, :]),
                inc="s_ve" if tc == 3 else None,
                waits=[("s_mm", v_mm_v)] if tc == 0 else ())
        v_evac_v = P.val("s_ve")

        if _lvl < 6:
            P.wait("gpsimd", "s_ve", P.val("s_ve"))
            continue
        for h in range(HPC):
            emit("sync", lambda e, h=h: e.dma_start(
                out=sk_tm[:, h, :, :], in_=sk_bf[:, h, :], transpose=True),
                inc="s_tp", n=16, waits=[("s_ve", v_evac_k)])
        v_sktm = P.val("s_tp")

        if debug and s == 0:
            emit("gpsimd", lambda e: e.dma_start(out=dbg["qT"], in_=q_bf[:]), inc="s_st", n=16,
                 waits=[("s_se", v_exq)])
            emit("gpsimd", lambda e: e.dma_start(out=dbg["sqT"], in_=sq_bf[:]), inc="s_st", n=16,
                 waits=[("s_ve", v_evac_q)])
            emit("gpsimd", lambda e: e.dma_start(out=dbg["skT"], in_=sk_bf[:]), inc="s_st", n=16,
                 waits=[("s_ve", v_evac_k)])
            emit("gpsimd", lambda e: e.dma_start(out=dbg["v"], in_=v_bf[:]), inc="s_st", n=16,
                 waits=[("s_ve", v_evac_v)])

        if seg_detail == "proj":
            P.wait("gpsimd", "s_ve", P.val("s_ve"))
            P.wait("gpsimd", "s_tp", P.val("s_tp"))
            continue
        for h in range(HPC):
            def numq_group(e, h=h):
                for qc in range(4):
                    inst = e.matmul(psA[:, qc * 256:qc * 256 + 130],
                                    sq_bf[:, h, qc * 128:(qc + 1) * 128],
                                    M_bf[:, h, :], start=True, stop=True)
                return inst
            emit("tensor", numq_group, inc="s_mm",
                 waits=[("s_ve", max(v_evac_q, v_evac_v, v_Mbf[h])),
                        ("s_se", v_am_prev)])

            def numk_group(e, h=h):
                for kc in range(4):
                    inst = e.matmul(psA[:, 1024 + kc * 256:1024 + kc * 256 + 130],
                                    sk_bf[:, h, kc * 128:(kc + 1) * 128],
                                    M_bf[:, h, :], start=True, stop=True)
                return inst
            v_numk = emit("tensor", numk_group, inc="s_mm", waits=[("s_ve", v_evac_k)])

            v_dense = emit("scalar", lambda e: e.add(
                t_den[:], psA.ap().rearrange("p (c u) -> p c u", u=256)[:, :, 128], EPS_ATT),
                inc="s_se", waits=[("s_mm", v_numk), ("s_ve", max(v_Mbf))])
            if s == 1 and h == 0 and isinstance(debug, str) and debug == "cap2":
                emit("vector", lambda e: e.tensor_copy(out=t_scr1[:, 1024:1032], in_=t_den[:]),
                     inc="s_ve", waits=[("s_se", v_dense)])
                emit("vector", lambda e: e.tensor_copy(out=t_scr1[:, 1032:1162], in_=M_bf[:, 0, :]))
                emit("vector", lambda e: e.tensor_copy(out=t_scr1[:, 1162:1292], in_=psA[:, 1024:1154]))
                emit("vector", lambda e: e.tensor_copy(out=t_scr1[:, 1300:1820],
                                                       in_=M_bf.ap().rearrange("p h c -> p (h c)")))
                emit("vector", lambda e: e.tensor_copy(out=t_scr1[:, 1820:1950],
                                                       in_=M_f32[:, 0, :]))
            v_rq = emit("vector", lambda e: e.reciprocal(out=t_rq[:], in_=t_den[:]),
                        waits=[("s_se", v_dense)], inc="s_ve")
            emit("vector", lambda e, h=h: e.tensor_scalar(t_rqg[:], t_rq[:, 0:4], g_bc[:, h:h + 1], None, OP.mult),
                 waits=[("s_ex", v_gbc), ("s_ve", v_rq)])
            v_rkn = emit("vector", lambda e: e.tensor_scalar(t_rkn[:], t_rq[:, 4:8], -1.0, None, OP.mult),
                         inc="s_ve")
            if s == 1 and h == 0 and isinstance(debug, str) and debug == "cap4":
                emit("vector", lambda e: e.tensor_copy(out=t_scr1[:, 1024:1032], in_=t_den[:]))
                emit("vector", lambda e: e.tensor_copy(out=t_scr1[:, 1032:1040], in_=t_rq[:]))
                emit("vector", lambda e: e.tensor_copy(out=t_scr1[:, 1040:1044], in_=t_rkn[:]))
                emit("vector", lambda e: e.tensor_copy(out=t_scr1[:, 1044:1048], in_=t_rqg[:]))
            for kc in range(4):
                emit("vector", lambda e, kc=kc, h=h: e.scalar_tensor_tensor(
                    vd_bf[:, kc, 0:128], psA[:, 1024 + kc * 256:1024 + kc * 256 + 128],
                    t_rkn[:, kc:kc + 1], v_bf[:, kc, h, 0:128], OP.mult, OP.add),
                    inc="s_ve" if kc == 3 else None,
                    waits=[("s_ve", v_rkn)] if kc == 0 else ())
            v_vd = P.val("s_ve")

            if s == 1 and h == 0 and isinstance(debug, str) and debug == "cap4":
                emit("vector", lambda e: e.tensor_copy(out=t_scr1[:, 1048:1178], in_=vd_bf[:, 0, :]))
                emit("vector", lambda e: e.tensor_copy(out=t_scr1[:, 1178:1308], in_=vd_bf[:, 3, :]))

            def mup_group(e, h=h):
                for kc in range(4):
                    inst = e.matmul(psA[:, 1024:1154], sk_tm[:, h, kc, :],
                                    vd_bf[:, kc, :], start=(kc == 0), stop=(kc == 3))
                return inst
            v_mup = emit("tensor", mup_group, inc="s_mm",
                         waits=[("s_ve", v_vd), ("s_tp", v_sktm)])
            if s == 0 and h == 0 and isinstance(debug, str) and debug == "cap":
                emit("vector", lambda e: e.tensor_copy(out=t_scr1[:, 1900:2030], in_=psA[:, 1024:1154]),
                     inc="s_ve", waits=[("s_mm", v_mup)])
                emit("vector", lambda e: e.tensor_copy(out=t_scr1[:, 1380:1900],
                                                       in_=vd_bf.ap().rearrange("p a b -> p (a b)")))
                emit("vector", lambda e: e.tensor_copy(out=t_scr1[:, 868:1380],
                                                       in_=sk_tm[:, 0, :, :].rearrange("p a b -> p (a b)")))
            if s == 1 and h == 0 and isinstance(debug, str) and debug == "cap3":
                emit("vector", lambda e: e.tensor_copy(out=t_scr1[:, 1024:1154], in_=psA[:, 1024:1154]),
                     waits=[("s_mm", v_mup)])
            emit("vector", lambda e, h=h: e.tensor_tensor(M_f32[:, h, :], M_f32[:, h, :],
                                                          psA[:, 1024:1154], OP.add),
                 waits=[("s_mm", v_mup)])
            v_Mbf[h] = emit("vector", lambda e, h=h: e.tensor_copy(out=M_bf[:, h, :], in_=M_f32[:, h, :]),
                            inc="s_ve")
            if s == 1 and isinstance(debug, str) and debug == "cap3":
                off = 1160 + 140 * h
                emit("vector", lambda e, off=off: e.tensor_copy(out=t_scr1[:, off:off + 130],
                                                                in_=M_f32[:, 0, :]))

            for qc in range(4):
                nk = (qc + 1) * 128
                boff = (qc % 2) * 512
                w = []
                if h == 0 and qc == 0:
                    w = [("s_se", v_exk), ("s_ve", v_evac_k)]
                elif qc >= 2:
                    w = [("s_se", P.val("s_se"))]
                v_sc = emit("tensor", lambda e, h=h, qc=qc, nk=nk, boff=boff: e.matmul(
                    psB[:, boff:boff + nk], q_bf[:, h, qc * 128:(qc + 1) * 128],
                    k_bf[:, h, 0:nk], start=True, stop=True),
                    inc="s_mm", waits=w)
                v_nm = emit("vector", lambda e, qc=qc, nk=nk, boff=boff: e.tensor_reduce(
                    t_nm[:, qc:qc + 1], psB[:, boff:boff + nk], AX.X, OP.max, negate=True),
                    inc="s_ve", waits=[("s_mm", v_sc)])
                v_exp = emit("scalar", lambda e, qc=qc, nk=nk, boff=boff: e.activation(
                    p_bf[:, qc, 0:nk], psB[:, boff:boff + nk], AF.Exp,
                    bias=t_nm[:, qc:qc + 1]),
                    inc="s_se", waits=[("s_ve", v_nm)])
                v_msk = emit("vector", lambda e, qc=qc: e.tensor_tensor(
                    p_bf[:, qc, qc * 128:(qc + 1) * 128], p_bf[:, qc, qc * 128:(qc + 1) * 128],
                    tril_bf[:], OP.mult), inc="s_ve", waits=[("s_se", v_exp)])
                emit("sync", lambda e, qc=qc, nk=nk: e.dma_start(
                    out=pT_bf[:, qc, 0:qc + 1, :],
                    in_=p_bf[:, qc, 0:nk], transpose=True),
                    inc="s_tp", n=16, waits=[("s_ve", v_msk)])
                v_pT_qc[qc] = P.val("s_tp")

            for qc in range(4):
                def pv_group(e, h=h, qc=qc):
                    for kc in range(qc + 1):
                        inst = e.matmul(psB[:, 1024 + qc * 256:1024 + qc * 256 + 130],
                                        pT_bf[:, qc, kc, :],
                                        v_bf[:, kc, h, :], start=(kc == 0), stop=(kc == qc))
                    return inst
                emit("tensor", pv_group, inc="s_mm",
                     waits=[("s_tp", v_pT_qc[qc])])
            v_pv = P.val("s_mm")

            v_rd = emit("vector", lambda e: e.reciprocal(
                out=t_rd[:], in_=psB.ap().rearrange("p (c u) -> p c u", u=256)[:, 4:8, 128]),
                waits=[("s_mm", v_pv)], inc="s_ve")
            v_r2 = emit("vector", lambda e, h=h: e.tensor_scalar(t_r2g[:], t_rd[:], gm1_bc[:, h:h + 1], None, OP.mult),
                        inc="s_ve", waits=[("s_ve", max(v_gm1, v_rd))])
            for qc in range(4):
                emit("scalar", lambda e, qc=qc: e.activation(
                    t_scr2[:, qc * 128:(qc + 1) * 128], psA[:, qc * 256:qc * 256 + 128],
                    AF.Identity, scale=t_rqg[:, qc:qc + 1]),
                    waits=[("s_ve", v_r2)] if qc == 0 else (),
                    inc="s_se" if qc == 3 else None)
            v_am = P.val("s_se")
            v_am_prev = v_am
            for qc in range(4):
                emit("vector", lambda e, h=h, qc=qc, sp=sp: e.scalar_tensor_tensor(
                    a_stage[:, sp, qc, h * 128:(h + 1) * 128],
                    psB[:, 1024 + qc * 256:1024 + qc * 256 + 128],
                    t_r2g[:, qc:qc + 1], t_scr2[:, qc * 128:(qc + 1) * 128], OP.mult, OP.add),
                    waits=([("s_ve", v_r2)] if qc == 0 else []) +
                          [("s_se", v_am), ("s_ld", v_axp_hist[sp])],
                    inc="s_ve" if qc == 3 else None)
            v_comb_prev = P.val("s_ve")
            if s == NSEG - 1 and h == 0 and isinstance(debug, str) and debug == "cap6":
                emit("vector", lambda e, sp=sp: e.tensor_copy(
                    out=t_scr1[:, 1024:1536].rearrange("p (q f) -> p q f", q=4),
                    in_=a_stage[:, sp, :, 0:128]))
                emit("vector", lambda e: e.tensor_copy(out=t_scr1[:, 1536:1540], in_=t_r2g[:]))
                emit("vector", lambda e: e.tensor_copy(out=t_scr1[:, 1540:1544], in_=t_rqg[:]))
                emit("vector", lambda e: e.tensor_copy(out=t_scr1[:, 1544:1674],
                                                       in_=psB[:, 1024:1154]))
                emit("vector", lambda e: e.tensor_copy(out=t_scr1[:, 1674:1804],
                                                       in_=psB[:, 1792:1922]))
                emit("vector", lambda e: e.tensor_copy(out=t_scr1[:, 1804:1932],
                                                       in_=t_scr2[:, 384:512]))

            if debug and s == 0 and h == 0:
                emit("gpsimd", lambda e: e.dma_start(out=dbg["p"], in_=p_bf[:]), inc="s_st", n=16,
                     waits=[("s_ve", v_comb_prev)])

        v_head_vec_prev = P.val("s_ve")

        for qc in range(4):
            emit("sync", lambda e, s=s, sp=sp, qc=qc: e.dma_start(
                out=ax_priv[s, qc], in_=a_stage[:, sp, qc, :]),
                inc="s_ld", n=16, waits=[("s_ve", v_head_vec_prev)] if qc == 0 else ())
        v_axp = P.val("s_ld")
        v_axp_hist[sp] = v_axp

        def astore(e, s=s):
            pc = g_pcol(e)
            return e.dma_start(
                out=a_xchg[s * 512:(s + 1) * 512, ds(pc, 512)],
                in_=ax_priv[s].rearrange("qc p f -> (qc p) f"))
        emit("gpsimd", astore, inc="s_st", n=16, waits=[("s_ld", v_axp)])
        v_astore_hist[sp] = P.val("s_st")

        if debug and s == 0:
            emit("gpsimd", lambda e: e.dma_start(out=dbg["astage"], in_=a_stage[:, 0, :, :]),
                 inc="s_st", n=16, waits=[("s_ve", v_head_vec_prev)])
            v_astore_hist[sp] = P.val("s_st")

    if debug:
        emit("gpsimd", lambda e: e.dma_start(out=dbg["M"], in_=M_f32[:]), inc="s_st", n=16,
             waits=[("s_ve", max(v_Mbf))])

    v_all_store = P.val("s_st")
    v_barw = emit("gpsimd", lambda e: e.dma_start(out=bar_in.ap(), in_=sgc[0:1, 0:8]),
                  inc="s_ex", n=16, waits=[("s_st", v_all_store)])
    emit("gpsimd", lambda e: e.collective_compute(
        "AllReduce", OP.add, replica_groups=[list(range(8))],
        ins=[bar_in.ap().opt()], outs=[bar_out.ap().opt()]),
        inc="s_cc", waits=[("s_ex", v_barw)])
    V_BAR = P.val("s_cc")

    def a_copy(e):
        pid = e.partition_id()
        off = e.snap((pid % 2) * TPC, min_val=0, max_val=TPC)
        return e.dma_start(out=a_loc, in_=a_xchg[ds(off, TPC), :])
    V_ACP = emit("gpsimd", a_copy, inc="s_gld", n=16, waits=[("s_cc", V_BAR)])

    if ph_dump == "vbuf":
        tmp_f = nc.sbuf_tensor("dumpV_f", [128, 2080], F32)
        with tmp_f as tf:
            v_c = emit("vector", lambda e: e.tensor_copy(
                out=tf[:], in_=v_bf.ap().rearrange("p a b c -> p (a b c)")),
                inc="s_ve", waits=[("s_ve", P.val("s_ve")), ("s_se", P.val("s_se")),
                                   ("s_mm", P.val("s_mm"))])
            emit("gpsimd", lambda e: e.dma_start(out=y_out[0:128, 0:1024], in_=tf[:, 0:1024]),
                 inc="s_st", n=16, waits=[("s_ve", v_c)])
            emit("gpsimd", lambda e: e.dma_start(out=y_out[128:256, 0:1024], in_=tf[:, 1024:2048]),
                 inc="s_st", n=16)
            v_d3 = emit("gpsimd", lambda e: e.dma_start(out=y_out[256:384, 0:32], in_=tf[:, 2048:2080]),
                 inc="s_st", n=16)
            v_c2 = emit("vector", lambda e: e.tensor_copy(
                out=tf[:, 0:520], in_=vd_bf.ap().rearrange("p a b -> p (a b)")),
                inc="s_ve", waits=[("s_st", v_d3)])
            emit("gpsimd", lambda e: e.dma_start(out=y_out[384:512, 0:520], in_=tf[:, 0:520]),
                 inc="s_st", n=16, waits=[("s_ve", v_c2)])
        P.wait("gpsimd", "s_st", P.val("s_st"))
        return _finalize(nc, P, dbg)

    if ph_dump == "M":
        tmp_f = nc.sbuf_tensor("dumpM_f", [128, HPC, 130], F32)
        with tmp_f as tf:
            v_c = emit("vector", lambda e: e.tensor_copy(
                out=tf.ap().rearrange("p h c -> p (h c)"),
                in_=M_f32.ap().rearrange("p h c -> p (h c)")),
                inc="s_ve", waits=[("s_ve", max(v_Mbf))])
            v_d1 = emit("gpsimd", lambda e: e.dma_start(
                out=y_out[0:128, 0:HPC * 130], in_=tf.ap().rearrange("p h c -> p (h c)")),
                inc="s_st", n=16, waits=[("s_ve", v_c)])
            v_c2 = emit("vector", lambda e: e.tensor_copy(
                out=tf.ap().rearrange("p h c -> p (h c)")[:, 0:512],
                in_=a_stage[:, (NSEG - 1) % 2, 0, :]), inc="s_ve",
                waits=[("s_st", v_d1)])
            emit("gpsimd", lambda e: e.dma_start(
                out=y_out[128:256, 0:512], in_=tf.ap().rearrange("p h c -> p (h c)")[:, 0:512]),
                inc="s_st", n=16, waits=[("s_ve", v_c2)])
            if isinstance(debug, str) and debug == "cap":
                emit("gpsimd", lambda e: e.dma_start(out=y_out[256:384, 0:1024], in_=t_scr1[:, 868:1892]),
                     inc="s_st", n=16)
                emit("gpsimd", lambda e: e.dma_start(out=y_out[384:512, 0:138], in_=t_scr1[:, 1892:2030]),
                     inc="s_st", n=16)
            if isinstance(debug, str) and debug == "cap2":
                emit("gpsimd", lambda e: e.dma_start(out=y_out[256:384, 0:268], in_=t_scr1[:, 1024:1292]),
                     inc="s_st", n=16)
                emit("gpsimd", lambda e: e.dma_start(out=y_out[384:512, 0:650], in_=t_scr1[:, 1300:1950]),
                     inc="s_st", n=16)
            if isinstance(debug, str) and debug in ("cap3", "cap4"):
                emit("gpsimd", lambda e: e.dma_start(out=y_out[256:384, 0:696], in_=t_scr1[:, 1024:1720]),
                     inc="s_st", n=16)
            if isinstance(debug, str) and debug == "cap6":
                emit("gpsimd", lambda e: e.dma_start(out=y_out[256:384, 0:908], in_=t_scr1[:, 1024:1932]),
                     inc="s_st", n=16)
            if isinstance(debug, str) and debug == "cap5":
                v_c5 = emit("vector", lambda e: e.tensor_copy(
                    out=t_scr1[:], in_=a_stage[:, (NSEG - 1) % 2, :, :].rearrange("p q f -> p (q f)")),
                    inc="s_ve", waits=[("s_st", P.val("s_st"))])
                emit("gpsimd", lambda e: e.dma_start(out=y_out[256:384, 0:1024], in_=t_scr1[:, 0:1024]),
                     inc="s_st", n=16, waits=[("s_ve", v_c5)])
                emit("gpsimd", lambda e: e.dma_start(out=y_out[384:512, 0:1024], in_=t_scr1[:, 1024:2048]),
                     inc="s_st", n=16)
                v_c8 = emit("vector", lambda e: e.tensor_copy(out=t_scr1[:, 1024:1536], in_=t_scr2[:, 0:512]),
                            inc="s_ve", waits=[("s_st", P.val("s_st"))])
                emit("vector", lambda e: e.tensor_copy(out=t_scr1[:, 2040:2044], in_=t_nm[:]))
                emit("vector", lambda e: e.tensor_copy(out=t_scr1[:, 2044:2048], in_=t_rqg[:]))
                emit("vector", lambda e: e.tensor_copy(out=t_scr1[:, 2036:2040], in_=t_r2g[:]))
                v_c9 = emit("vector", lambda e: e.tensor_copy(
                    out=t_scr2[:, 0:1024], in_=psB[:, 1024:2048]), inc="s_ve")
                emit("gpsimd", lambda e: e.dma_start(out=y_out[640:768, 0:1024], in_=t_scr2[:, 0:1024]),
                     inc="s_st", n=16, waits=[("s_ve", v_c9)])
                emit("gpsimd", lambda e: e.dma_start(out=y_out[768:896, 0:512], in_=t_scr1[:, 1024:1536]),
                     inc="s_st", n=16)
                emit("gpsimd", lambda e: e.dma_start(out=y_out[896:1024, 0:12], in_=t_scr1[:, 2036:2048]),
                     inc="s_st", n=16)
                with nc.sbuf_tensor("dump_axp", [128, 2, 512], BF16) as axp_t:
                    v_c6 = emit("gpsimd", lambda e: e.dma_start(out=axp_t[:, 0, :], in_=ax_priv[NSEG - 1, 1]),
                                inc="s_gld", n=16)
                    emit("gpsimd", lambda e: e.dma_start(out=axp_t[:, 1, :], in_=ax_priv[NSEG - 1, 3]),
                         inc="s_gld", n=16)
                    v_c7 = emit("vector", lambda e: e.tensor_copy(
                        out=t_scr1[:, 0:1024], in_=axp_t.ap().rearrange("p a b -> p (a b)")),
                        inc="s_ve", waits=[("s_gld", P.val("s_gld")), ("s_st", P.val("s_st"))])
                    emit("gpsimd", lambda e: e.dma_start(out=y_out[512:640, 0:1024], in_=t_scr1[:, 0:1024]),
                         inc="s_st", n=16, waits=[("s_ve", v_c7)])
        P.wait("gpsimd", "s_st", P.val("s_st"))
        return _finalize(nc, P, dbg)

    if ph_dump in ("aloc", "ax"):
        dump_src = a_loc if ph_dump == "aloc" else a_xchg
        tmp_bf = nc.sbuf_tensor("dump_bf", [128, 1024], BF16)
        tmp_f = nc.sbuf_tensor("dump_f", [128, 2, 1024], F32)
        with tmp_bf as tbf, tmp_f as tf:
            v_pv = 0
            for t in range(NTILE):
                tb = t % 2
                v_l = emit("gpsimd", lambda e, t=t: e.dma_start(
                    out=tbf[:], in_=dump_src[t * 128:(t + 1) * 128, :]),
                    inc="s_gld", n=16, waits=[("s_gld", V_ACP), ("s_ve", v_pv)])
                v_c = emit("vector", lambda e, tb=tb: e.tensor_copy(out=tf[:, tb, :], in_=tbf[:]),
                           inc="s_ve", waits=[("s_gld", v_l), ("s_st", v_pv_st if t >= 2 else 0)])
                v_s = emit("gpsimd", lambda e, t=t, tb=tb: e.dma_start(
                    out=y_out[t * 128:(t + 1) * 128, :], in_=tf[:, tb, :]),
                    inc="s_st", n=16, waits=[("s_ve", v_c)])
                v_pv = v_c
                v_pv_st = P.val("s_st")
        P.wait("gpsimd", "s_st", P.val("s_st"))
        return _finalize(nc, P, dbg)

    es1.close()
    sb2p, sb2a = mk_sb(es2p), mk_sb(es2a)
    x1_sb = sb2p("x1_sb", [128, NTILE, 1024], BF16)
    x2_sb = sb2a("x2_sb", [128, 2, 1024])
    a2_sb = sb2a("a2_sb", [128, 2, 1024], BF16)
    r_sb = sb2a("r_sb", [128, 2, 1024])
    rsq_sb = sb2a("rsq_sb", [128, 1024])
    xq_bf = sb2a("xq_bf", [128, 2, 1024], BF16)
    xqT = sb2a("xqT", [128, NTILE, 8, 128], BF16)
    wgt_sb = sb2a("wgt_sb", [128, 2, 8, 128], BF16)
    wut_sb = sb2a("wut_sb", [128, 2, 8, 128], BF16)
    sg_sb = sb2a("sg_sb", [128, 512], BF16)
    su_sb = sb2a("su_sb", [128, 512], BF16)
    h_bf = sb2a("h_bf", [128, 4096], BF16)

    v_r_prev = [0, 0]
    v_x1_hist = [0] * NTILE
    v_xqT_tile = [0] * NTILE
    for t in range(NTILE if phases >= 3 else 0):
        tb = t % 2
        v_xl = emit("sync", lambda e, t=t, tb=tb: e.dma_start(
            out=x2_sb[:, tb, :],
            in_=x2_in[t * 128:(t + 1) * 128, :]),
            inc="s_ld", n=16, waits=[("s_cc", V_BAR), ("s_ve", v_r_prev[tb])])
        v_al = emit("gpsimd", lambda e, t=t, tb=tb: e.dma_start(
            out=a2_sb[:, tb, :],
            in_=a_loc[t * 128:(t + 1) * 128, :]),
            inc="s_gld", n=16, waits=[("s_gld", V_ACP), ("s_ve", v_r_prev[tb])])
        emit("vector", lambda e, tb=tb: e.scalar_tensor_tensor(
            r_sb[:, tb, :], x2_sb[:, tb, :], 0.0, a2_sb[:, tb, :], OP.add, OP.add,
            accum_out=st2[:, 0:1]),
            waits=[("s_ld", v_xl), ("s_gld", v_al)])
        v_acc = emit("vector", lambda e, tb=tb: e.scalar_tensor_tensor(
            rsq_sb[:], r_sb[:, tb, :], 0.0, r_sb[:, tb, :], OP.add, OP.mult,
            accum_out=st2[:, 1:2]), inc="s_ve")
        v1 = emit("vector", lambda e: e.tensor_scalar(st2[:, 0:1], st2[:, 0:1], float(1.0 / D), None, OP.mult),
                  waits=[("s_ve", v_acc)], inc="s_ve")
        v2 = emit("vector", lambda e: e.tensor_tensor(st2[:, 3:4], st2[:, 0:1], st2[:, 0:1], OP.mult),
                  waits=[("s_ve", v1)], inc="s_ve")
        v3 = emit("vector", lambda e: e.scalar_tensor_tensor(st2[:, 2:3], st2[:, 1:2], float(1.0 / D),
                                                             st2[:, 3:4], OP.mult, OP.subtract),
                  waits=[("s_ve", v2)], inc="s_ve")
        v4 = emit("vector", lambda e: e.tensor_scalar(st2[:, 2:3], st2[:, 2:3], EPS_LN, None, OP.add),
                  waits=[("s_ve", v3)], inc="s_ve")
        v_var = emit("vector", lambda e: e.reciprocal(out=st2[:, 2:3], in_=st2[:, 2:3]), inc="s_ve",
                     waits=[("s_ve", v4)])
        v_rst = emit("scalar", lambda e: e.activation(t_rstd[:, 0:1], st2[:, 2:3], AF.Sqrt),
                     inc="s_se", waits=[("s_ve", v_var)])
        v_bias = emit("vector", lambda e: e.tensor_scalar(t_rstd[:, 1:2], st2[:, 0:1], t_rstd[:, 0:1],
                                                          -1.0, OP.mult, OP.mult),
                      inc="s_ve", waits=[("s_se", v_rst)])
        v_x1t = emit("scalar", lambda e, t=t, tb=tb: e.activation(
            x1_sb[:, t, :], r_sb[:, tb, :], AF.Identity,
            bias=t_rstd[:, 1:2], scale=t_rstd[:, 0:1]),
            inc="s_se", waits=[("s_se", v_rst), ("s_ve", v_bias)])
        v_x1_hist[t] = v_x1t
        emit("vector", lambda e, t=t: e.tensor_scalar(
            rsq_sb[:], x1_sb[:, t, :], 1.0, None, OP.mult, OP.max,
            accum_out=xmx[:, 2 * t:2 * t + 1]), waits=[("s_se", v_x1t)])
        emit("vector", lambda e, t=t: e.tensor_scalar(
            rsq_sb[:], x1_sb[:, t, :], -1.0, None, OP.mult, OP.max,
            accum_out=xmx[:, 2 * t + 1:2 * t + 2]), inc="s_ve")
        v_r_prev[tb] = P.val("s_ve")
        if debug and t == 0:
            emit("gpsimd", lambda e: e.dma_start(out=dbg["r0"], in_=r_sb[:, 0, :]), inc="s_st", n=16,
                 waits=[("s_ve", v_r_prev[0])])

    if phases < 3:
        P.wait("gpsimd", "s_cc", P.val("s_cc"))
        P.wait("gpsimd", "s_gld", P.val("s_gld"))
        P.wait("gpsimd", "s_st", P.val("s_st"))
        return _finalize(nc, P, dbg)
    v_xm = emit("vector", lambda e: e.tensor_reduce(xmx[:, 0:1], xmx[:], AX.X, OP.max), inc="s_ve",
                waits=[("s_ve", P.val("s_ve"))])
    v_sw = emit("gpsimd", lambda e: e.dma_start(
        out=scr[0:1, 64:192].rearrange("o (p u) -> p (o u)", p=128), in_=xmx[:, 0:1]),
        inc="s_ex", n=16, waits=[("s_ve", v_xm)])
    v_sr = emit("gpsimd", lambda e: e.dma_start(out=row128[:], in_=scr[0:1, 64:192]),
                inc="s_gld", n=16, waits=[("s_ex", v_sw)])
    vl1 = emit("vector", lambda e: e.tensor_reduce(lmax[0:1, 0:1], row128[:], AX.X, OP.max),
               waits=[("s_gld", v_sr)], inc="s_ve")
    vl2 = emit("vector", lambda e: e.tensor_copy(out=lmax[0:1, 1:2], in_=lmax[0:1, 0:1]),
               waits=[("s_ve", vl1)], inc="s_ve")
    vl3 = emit("vector", lambda e: e.tensor_copy(out=lmax[0:1, 2:4], in_=lmax[0:1, 0:2]),
               waits=[("s_ve", vl2)], inc="s_ve")
    v_lm = emit("vector", lambda e: e.tensor_copy(out=lmax[0:1, 4:8], in_=lmax[0:1, 0:4]), inc="s_ve",
                waits=[("s_ve", vl3)])
    v_mxw = emit("gpsimd", lambda e: e.dma_start(out=mx_in.ap(), in_=lmax[:]),
                 inc="s_ex", n=16, waits=[("s_ve", v_lm)])
    emit("gpsimd", lambda e: e.collective_compute(
        "AllReduce", OP.max, replica_groups=[list(range(8))],
        ins=[mx_in.ap().opt()], outs=[mx_out.ap().opt()]),
        inc="s_cc", waits=[("s_ex", v_mxw)])
    V_MX = P.val("s_cc")
    v_gm = emit("gpsimd", lambda e: e.dma_start(out=lmax[0:1, 0:1], in_=mx_out.ap()[0:1, 0:1]),
                inc="s_gld", n=16, waits=[("s_cc", V_MX)])
    if debug:
        emit("gpsimd", lambda e: e.dma_start(out=dbg["gmax"], in_=lmax[:]), inc="s_st", n=16,
             waits=[("s_gld", v_gm)])
    vsx1 = emit("vector", lambda e: e.tensor_scalar(lmax[0:1, 1:2], lmax[0:1, 0:1], sgc[0:1, 6:7],
                                                    INV127, OP.mult, OP.mult),
                waits=[("s_gld", v_gm)], inc="s_ve")
    v_sx = emit("vector", lambda e: e.reciprocal(out=lmax[0:1, 2:3], in_=lmax[0:1, 1:2]), inc="s_ve",
                waits=[("s_ve", vsx1)])
    v_sxw = emit("gpsimd", lambda e: e.dma_start(out=scr[0:1, 8:10], in_=lmax[0:1, 1:3]),
                 inc="s_ex", n=16, waits=[("s_ve", v_sx)])
    emit("gpsimd", lambda e: e.dma_start(out=sx_bc[:], in_=scr[0:1, 8:9].broadcast_to([128, 1])),
         inc="s_gld", n=16, waits=[("s_ex", v_sxw)])
    v_sxb = emit("gpsimd", lambda e: e.dma_start(out=rx_bc[:], in_=scr[0:1, 9:10].broadcast_to([128, 1])),
                 inc="s_gld", n=16)

    v_tp_hist2 = [0, 0]
    for t in range(NTILE):
        tb = t % 2
        emit("vector", lambda e, t=t: e.tensor_scalar(rsq_sb[:], x1_sb[:, t, :], rx_bc[:], MAGIC, OP.mult, OP.add),
             waits=[("s_gld", v_sxb), ("s_tp", v_tp_hist2[tb]), ("s_se", v_x1_hist[t])])
        emit("vector", lambda e: e.tensor_scalar(rsq_sb[:], rsq_sb[:], MAGIC + 127.0, MAGIC - 128.0, OP.min, OP.max))
        v_q2 = emit("vector", lambda e, tb=tb: e.tensor_scalar(xq_bf[:, tb, :], rsq_sb[:], MAGIC, None, OP.subtract),
                    inc="s_ve")
        for half in range(2):
            emit("sync", lambda e, t=t, tb=tb, half=half: e.dma_start(
                out=xqT[:, t, half * 4:half * 4 + 4, :],
                in_=xq_bf[:, tb, half * 512:(half + 1) * 512], transpose=True),
                inc="s_tp", n=16, waits=[("s_ve", v_q2)])
        v_tp_hist2[tb] = P.val("s_tp")
        v_xqT_tile[t] = P.val("s_tp")
    v_xqT = P.val("s_tp")

    if ph_dump == "x1":
        tmp_f = nc.sbuf_tensor("dumpX_f", [128, 2048], F32)
        with tmp_f as tf:
            v_c = emit("vector", lambda e: e.tensor_copy(out=tf[:, 0:1024], in_=x1_sb[:, 0, :]),
                       inc="s_ve", waits=[("s_tp", v_xqT)])
            emit("vector", lambda e: e.tensor_copy(out=tf[:, 1024:2048], in_=x1_sb[:, 31, :]),
                 inc="s_ve")
            v_d1 = emit("gpsimd", lambda e: e.dma_start(out=y_out[0:128, 0:1024], in_=tf[:, 0:1024]),
                        inc="s_st", n=16, waits=[("s_ve", P.val("s_ve"))])
            emit("gpsimd", lambda e: e.dma_start(out=y_out[128:256, 0:1024], in_=tf[:, 1024:2048]),
                 inc="s_st", n=16)
            emit("gpsimd", lambda e: e.dma_start(out=y_out[256:257, 0:8], in_=lmax[0:1, :]),
                 inc="s_st", n=16)
            v_c2 = emit("vector", lambda e: e.tensor_copy(
                out=tf[:, 0:1024].rearrange("p (a bc) -> p a bc", a=2),
                in_=xqT[:, 0:2, 0:4, :].rearrange("p a b c -> p a (b c)")),
                inc="s_ve", waits=[("s_st", v_d1)])
            emit("gpsimd", lambda e: e.dma_start(out=y_out[384:512, 0:1024], in_=tf[:, 0:1024]),
                 inc="s_st", n=16, waits=[("s_ve", v_c2)])
            v_c3 = emit("vector", lambda e: e.tensor_copy(out=tf[:, 1024:2048], in_=r_sb[:, 0, :]),
                        inc="s_ve")
            emit("gpsimd", lambda e: e.dma_start(out=y_out[512:640, 0:1024], in_=tf[:, 1024:2048]),
                 inc="s_st", n=16, waits=[("s_ve", v_c3)])
            v_c4 = emit("vector", lambda e: e.tensor_copy(out=tf[:, 0:2], in_=t_rstd[:]), inc="s_ve",
                        waits=[("s_st", P.val("s_st"))])
            emit("vector", lambda e: e.tensor_copy(out=tf[:, 2:6], in_=st2[:]))
            v_c5 = emit("vector", lambda e: e.tensor_copy(out=tf[:, 6:70], in_=xmx[:]), inc="s_ve")
            emit("gpsimd", lambda e: e.dma_start(out=y_out[640:768, 0:70], in_=tf[:, 0:70]),
                 inc="s_st", n=16, waits=[("s_ve", v_c5)])
        P.wait("gpsimd", "s_st", P.val("s_st"))
        return _finalize(nc, P, dbg)

    chunk_slots = []
    for sslot in range(PREP_SLOTS):
        for r in range(8):
            ci = r + 8 * sslot
            if ci < NCH:
                chunk_slots.append((ci, r, sslot))
    chunk_slots.sort()

    if phases < 4:
        P.wait("gpsimd", "s_cc", P.val("s_cc"))
        P.wait("gpsimd", "s_tp", P.val("s_tp"))
        P.wait("gpsimd", "s_ve", P.val("s_ve"))
        return _finalize(nc, P, dbg)
    v_spill_prev = 0
    v_gse_slot = [0, 0]
    v_gu_hist = [0, 0]
    v_httail_prev = 0
    for idx, (ci, r_, s_) in enumerate(chunk_slots):
        cb = idx % 2
        grow = r_ * CCR + s_ * 1024
        urow = r_ * CCR + (PREP_SLOTS + s_) * 1024
        emit("sync", lambda e, grow=grow, cb=cb: e.dma_start(
            out=wgt_sb[:, cb, :, :],
            in_=cc_out.ap()[grow:grow + 1024, :].rearrange("(c p) i -> p c i", p=128)),
            inc="s_ld", n=16,
            waits=[("s_cc", max(V_AG, V_BAR)), ("s_mm", v_gu_hist[cb])])
        v_wu = emit("sync", lambda e, urow=urow, cb=cb: e.dma_start(
            out=wut_sb[:, cb, :, :],
            in_=cc_out.ap()[urow:urow + 1024, :].rearrange("(c p) i -> p c i", p=128)),
            inc="s_ld", n=16)
        for tt in range(8):
            goff = (tt % 2) * 1024

            def gu_group(e, cb, tt, goff, which):
                w = wgt_sb if which == 0 else wut_sb
                for hc in range(8):
                    inst = e.matmul(psB[:, goff + which * 512: goff + which * 512 + 512],
                                    w[:, cb, hc, :],
                                    xqT[:, 4 * tt:4 * tt + 4, hc, :],
                                    start=(hc == 0), stop=(hc == 7))
                return inst
            emit("tensor", lambda e, cb=cb, tt=tt, goff=goff: gu_group(e, cb, tt, goff, 0),
                 inc="s_mm", waits=[("s_ld", v_wu), ("s_tp", v_xqT_tile[4 * tt + 3]),
                                    ("s_se", v_gse_slot[tt % 2])])
            v_gu = emit("tensor", lambda e, cb=cb, tt=tt, goff=goff: gu_group(e, cb, tt, goff, 1),
                        inc="s_mm")
            silu_w = [("s_mm", v_gu), ("s_gld", v_sxb), ("s_ve", v_httail_prev)]
            if debug and idx == 0 and tt == 1:
                silu_w.append(("s_st", P.val("s_st")))
            emit("scalar", lambda e, goff=goff: e.activation(sg_sb[:], psB[:, goff:goff + 512],
                                                             AF.Silu, scale=sx_bc[:]),
                 waits=silu_w)
            v_gse = emit("scalar", lambda e, goff=goff: e.activation(
                su_sb[:], psB[:, goff + 512:goff + 1024], AF.Identity, scale=sx_bc[:]),
                inc="s_se")
            v_gse_slot[tt % 2] = v_gse
            if debug and idx == 0 and tt == 0:
                v_cap = emit("vector", lambda e: e.tensor_copy(out=rsq_sb[:, 0:512], in_=psB[:, 0:512]),
                             inc="s_ve", waits=[("s_se", v_gse)])
                v_cap2 = emit("vector", lambda e: e.tensor_copy(out=x2_sb[:, 0, 0:512], in_=psB[:, 512:1024]),
                              inc="s_ve")
                emit("gpsimd", lambda e: e.dma_start(out=dbg["gps"], in_=rsq_sb[:, 0:512]),
                     inc="s_st", n=16, waits=[("s_ve", v_cap2)])
                emit("gpsimd", lambda e: e.dma_start(out=dbg["ups"], in_=x2_sb[:, 0, 0:512]),
                     inc="s_st", n=16)
                emit("gpsimd", lambda e: e.dma_start(out=dbg["sg0"], in_=sg_sb[:]),
                     inc="s_st", n=16, waits=[("s_se", v_gse)])
                emit("gpsimd", lambda e: e.dma_start(out=dbg["su0"], in_=su_sb[:]),
                     inc="s_st", n=16)
                emit("gpsimd", lambda e, cb=cb: e.dma_start(out=dbg["wgt0"], in_=wgt_sb[:, cb, :, :]),
                     inc="s_st", n=16)
                emit("gpsimd", lambda e: e.dma_start(out=dbg["xqT0"], in_=xqT[:, :, 0, :]),
                     inc="s_st", n=16)
            col = 2 * (idx * 8 + tt)
            emit("vector", lambda e, tt=tt: e.tensor_tensor(
                h_bf[:, tt * 512:(tt + 1) * 512], sg_sb[:], su_sb[:], OP.mult),
                waits=[("s_se", v_gse), ("s_st", v_spill_prev)])
            emit("vector", lambda e, tt=tt, col=col: e.tensor_scalar(
                rsq_sb[:, 0:512], h_bf[:, tt * 512:(tt + 1) * 512], 1.0, None, OP.mult, OP.max,
                accum_out=hmx[:, col:col + 1]))
            v_httail_prev = emit("vector", lambda e, tt=tt, col=col: e.tensor_scalar(
                rsq_sb[:, 0:512], h_bf[:, tt * 512:(tt + 1) * 512], -1.0, None, OP.mult, OP.max,
                accum_out=hmx[:, col + 1:col + 2]), inc="s_ve")
        v_gu_hist[cb] = P.val("s_mm")
        v_hprev = P.val("s_ve")
        emit("gpsimd", lambda e, ci=ci: e.dma_start(
            out=h_spill[ci, :, :], in_=h_bf[:]),
            inc="s_st", n=16, waits=[("s_ve", v_hprev)])
        v_spill_prev = P.val("s_st")
        if debug and idx == 0:
            emit("gpsimd", lambda e: e.dma_start(out=dbg["h0"], in_=h_bf[:, 0:512]),
                 inc="s_st", n=16, waits=[("s_ve", v_hprev)])
            v_spill_prev = P.val("s_st")
    v_hlast = P.val("s_ve")
    v_gu_end = P.val("s_mm")

    v_hm = emit("vector", lambda e: e.tensor_reduce(hmx[:, 0:1], hmx[:], AX.X, OP.max), inc="s_ve",
                waits=[("s_ve", v_hlast)])
    v_hw = emit("gpsimd", lambda e: e.dma_start(
        out=scr[0:1, 64:192].rearrange("o (p u) -> p (o u)", p=128), in_=hmx[:, 0:1]),
        inc="s_ex", n=16, waits=[("s_ve", v_hm)])
    v_hr = emit("gpsimd", lambda e: e.dma_start(out=row128[:], in_=scr[0:1, 64:192]),
                inc="s_gld", n=16, waits=[("s_ex", v_hw)])
    vh1 = emit("vector", lambda e: e.tensor_reduce(lmax[0:1, 0:1], row128[:], AX.X, OP.max),
               waits=[("s_gld", v_hr)], inc="s_ve")
    vh2 = emit("vector", lambda e: e.tensor_copy(out=lmax[0:1, 1:2], in_=lmax[0:1, 0:1]),
               waits=[("s_ve", vh1)], inc="s_ve")
    vh3 = emit("vector", lambda e: e.tensor_copy(out=lmax[0:1, 2:4], in_=lmax[0:1, 0:2]),
               waits=[("s_ve", vh2)], inc="s_ve")
    v_lh = emit("vector", lambda e: e.tensor_copy(out=lmax[0:1, 4:8], in_=lmax[0:1, 0:4]), inc="s_ve",
                waits=[("s_ve", vh3)])
    v_hmw = emit("gpsimd", lambda e: e.dma_start(out=hm_in.ap(), in_=lmax[:]),
                 inc="s_ex", n=16, waits=[("s_ve", v_lh)])
    emit("gpsimd", lambda e: e.collective_compute(
        "AllReduce", OP.max, replica_groups=[list(range(8))],
        ins=[hm_in.ap().opt()], outs=[hm_out.ap().opt()]),
        inc="s_cc", waits=[("s_ex", v_hmw)])
    V_HM = P.val("s_cc")
    v_ghm = emit("gpsimd", lambda e: e.dma_start(out=lmax[0:1, 0:1], in_=hm_out.ap()[0:1, 0:1]),
                 inc="s_gld", n=16, waits=[("s_cc", V_HM)])
    if debug:
        emit("gpsimd", lambda e: e.dma_start(out=dbg["hmax"], in_=lmax[:]), inc="s_st", n=16,
             waits=[("s_gld", v_ghm)])
    vsh1 = emit("vector", lambda e: e.tensor_scalar(lmax[0:1, 1:2], lmax[0:1, 0:1], sgc[0:1, 7:8],
                                                    INV127, OP.mult, OP.mult),
                waits=[("s_gld", v_ghm)], inc="s_ve")
    v_sh = emit("vector", lambda e: e.reciprocal(out=lmax[0:1, 2:3], in_=lmax[0:1, 1:2]), inc="s_ve",
                waits=[("s_ve", vsh1)])
    v_shw = emit("gpsimd", lambda e: e.dma_start(out=scr[0:1, 10:12], in_=lmax[0:1, 1:3]),
                 inc="s_ex", n=16, waits=[("s_ve", v_sh)])
    emit("gpsimd", lambda e: e.dma_start(out=sh_bc[:], in_=scr[0:1, 10:11].broadcast_to([128, 1])),
         inc="s_gld", n=16, waits=[("s_ex", v_shw)])
    v_shb = emit("gpsimd", lambda e: e.dma_start(out=rh_bc[:], in_=scr[0:1, 11:12].broadcast_to([128, 1])),
                 inc="s_gld", n=16)

    es2a.close()
    sb2b = mk_sb(es2b)
    wdT_sb = sb2b("wdT_sb", [128, NCH, 1024], BF16)
    h2_sb = sb2b("h2_sb", [128, 2, NCH, 512], BF16)
    kh_bf = sb2b("kh_bf", [128, 2, NCH, 512], BF16)
    z_sb = sb2b("z_sb", [128, 1024])
    y_sb = sb2b("y_sb", [128, 1024])

    ccv = cc_out.ap().rearrange("(r q) j -> r q j", r=8)
    for ci, _, _ in chunk_slots:
        emit("gpsimd", lambda e, ci=ci: e.dma_start(
            out=wdT_sb[:, ci, :].rearrange("p (r j) -> p r j", r=8),
            in_=ccv[:, 6 * 1024 + ci * 128: 6 * 1024 + (ci + 1) * 128, :].transpose([1, 0, 2])),
            inc="s_gld", n=16,
            waits=[("s_cc", V_AG), ("s_mm", v_gu_end), ("s_ve", v_hlast),
                   ("s_se", P.val("s_se"))])
    v_wd = P.val("s_gld")

    v_all_spill = v_spill_prev
    v_yprev_s = 0
    v_dn_hist = 0
    v_dn_hist_cb = [0, 0]
    v_evac_slot = [0, 0, 0, 0]
    v_yt_prev = 0
    v_quant_hist = [0, 0]
    v_kh_cb = [0, 0]

    def emit_load_quant(g, v_zfree):
        cb = g % 2
        v_hl = emit("gpsimd", lambda e, g=g, cb=cb: e.dma_start(
            out=h2_sb[:, cb],
            in_=h_spill[:, :, g * 512:(g + 1) * 512].transpose([1, 0, 2])),
            inc="s_gld", n=16,
            waits=[("s_st", v_all_spill), ("s_ve", v_quant_hist[cb]),
                   ("s_mm", v_gu_end)])
        P.wait("vector", "s_gld", max(v_hl, v_shb))
        P.wait("vector", "s_mm", v_dn_hist_cb[cb])
        P.wait("vector", "s_se", v_zfree)
        for sub in range(11):
            emit("vector", lambda e, sub=sub, cb=cb: e.tensor_scalar(
                z_sb[:], h2_sb[:, cb, 2 * sub:2 * sub + 2, :].rearrange("p c t -> p (c t)"),
                rh_bc[:], MAGIC, OP.mult, OP.add))
            emit("vector", lambda e: e.tensor_scalar(
                z_sb[:], z_sb[:], MAGIC + 127.0, MAGIC - 128.0, OP.min, OP.max))
            last = (sub == 10)
            emit("vector", lambda e, sub=sub, cb=cb: e.tensor_scalar(
                kh_bf[:, cb, 2 * sub:2 * sub + 2, :].rearrange("p c t -> p (c t)"), z_sb[:],
                MAGIC, None, OP.subtract), inc="s_ve" if last else None)
        v_quant_hist[cb] = P.val("s_ve")
        v_kh_cb[cb] = v_quant_hist[cb]

    emit_load_quant(0, 0)
    for g in range(8):
        cb = g % 2
        v_kh = v_kh_cb[cb]
        for tc in range(4):
            t_glob = g * 4 + tc
            for hh in range(2):
                slot = (tc * 2 + hh) % 4

                def dn_group(e, tc=tc, hh=hh, slot=slot, cb=cb):
                    for ci in range(NCH):
                        inst = e.matmul(psA[:, slot * 512:(slot + 1) * 512],
                                        kh_bf[:, cb, ci, tc * 128:(tc + 1) * 128],
                                        wdT_sb[:, ci, hh * 512:(hh + 1) * 512],
                                        start=(ci == 0), stop=(ci == NCH - 1))
                    return inst
                v_dn = emit("tensor", dn_group, inc="s_mm",
                            waits=[("s_ve", max(v_kh, v_evac_slot[slot])), ("s_gld", v_wd)])
                if tc == 0 and hh == 0 and g + 1 < 8:
                    emit_load_quant(g + 1, v_yt_prev)
                emit("vector", lambda e, t_glob=t_glob, hh=hh, slot=slot: e.scalar_tensor_tensor(
                    z_sb[:, hh * 512:(hh + 1) * 512], psA[:, slot * 512:(slot + 1) * 512],
                    sh_bc[:], x1_sb[:, t_glob, hh * 512:(hh + 1) * 512], OP.mult, OP.add,
                    accum_out=st2[:, hh:hh + 1]),
                    inc="s_ve",
                    waits=[("s_mm", v_dn), ("s_gld", v_shb), ("s_se", v_yt_prev)])
                v_evac_slot[slot] = P.val("s_ve")
            w1 = emit("vector", lambda e: e.tensor_tensor(st2[:, 0:1], st2[:, 0:1], st2[:, 1:2], OP.add),
                      waits=[("s_ve", P.val("s_ve"))], inc="s_ve")
            v_zacc = emit("vector", lambda e, t_glob=t_glob: e.scalar_tensor_tensor(
                x1_sb[:, t_glob, :], z_sb[:], 0.0, z_sb[:], OP.add, OP.mult, accum_out=st2[:, 1:2]),
                inc="s_ve")
            w2 = emit("vector", lambda e: e.tensor_scalar(st2[:, 0:1], st2[:, 0:1], float(1.0 / D), None, OP.mult),
                      waits=[("s_ve", v_zacc)], inc="s_ve")
            w3 = emit("vector", lambda e: e.tensor_tensor(st2[:, 3:4], st2[:, 0:1], st2[:, 0:1], OP.mult),
                      waits=[("s_ve", w2)], inc="s_ve")
            w4 = emit("vector", lambda e: e.scalar_tensor_tensor(st2[:, 2:3], st2[:, 1:2], float(1.0 / D),
                                                                 st2[:, 3:4], OP.mult, OP.subtract),
                      waits=[("s_ve", w3)], inc="s_ve")
            w5 = emit("vector", lambda e: e.tensor_scalar(st2[:, 2:3], st2[:, 2:3], EPS_LN, None, OP.add),
                      waits=[("s_ve", w4)], inc="s_ve")
            v_v2 = emit("vector", lambda e: e.reciprocal(out=st2[:, 2:3], in_=st2[:, 2:3]), inc="s_ve",
                        waits=[("s_ve", w5)])
            v_rs2 = emit("scalar", lambda e: e.activation(t_rstd[:, 0:1], st2[:, 2:3], AF.Sqrt),
                         inc="s_se", waits=[("s_ve", v_v2)])
            v_b2 = emit("vector", lambda e: e.tensor_scalar(t_rstd[:, 1:2], st2[:, 0:1], t_rstd[:, 0:1],
                                                            -1.0, OP.mult, OP.mult),
                        inc="s_ve", waits=[("s_se", v_rs2)])
            v_yt = emit("scalar", lambda e: e.activation(
                y_sb[:], z_sb[:], AF.Identity, bias=t_rstd[:, 1:2], scale=t_rstd[:, 0:1]),
                inc="s_se", waits=[("s_se", v_rs2), ("s_ve", v_b2), ("s_st", v_yprev_s)])
            v_yt_prev = v_yt
            if debug and t_glob == 0:
                emit("gpsimd", lambda e: e.dma_start(out=dbg["z0"], in_=z_sb[:]),
                     inc="s_st", n=16, waits=[("s_ve", v_b2)])
            emit("gpsimd", lambda e, t_glob=t_glob: e.dma_start(
                out=y_out[t_glob * 128:(t_glob + 1) * 128, :], in_=y_sb[:]),
                inc="s_st", n=16, waits=[("s_se", v_yt)])
            v_yprev_s = P.val("s_st")
        v_dn_hist = P.val("s_mm")
        v_dn_hist_cb[cb] = v_dn_hist

    if debug in ("lite", "scr", "scr_nowrite"):
        lt_scr = nc.dram_tensor("lt_scr", [1, 256], F32, kind="ExternalOutput").ap()
        lt_aloc = (nc.dram_tensor("lt_aloc", [TPC, D], BF16, kind="ExternalOutput").ap()
                   if debug == "lite" else None)
        if debug == "lite":
            lt_x1 = nc.dram_tensor("lt_x1", [128, NTILE, 1024], BF16, kind="ExternalOutput").ap()
            lt_h = nc.dram_tensor("lt_h", [NCH, 128, TPC], BF16, kind="ExternalOutput").ap()
            lt_ax = nc.dram_tensor("lt_ax", [S, D], BF16, kind="ExternalOutput").ap()
        v_pre = P.val("s_st")
        if debug != "scr_nowrite":
            v_l1 = emit("gpsimd", lambda e: e.dma_start(out=row128[:], in_=scr[0:1, 0:128]),
                        inc="s_gld", n=16,
                        waits=[("s_gld", P.val("s_gld")), ("s_st", v_pre)])
            emit("gpsimd", lambda e: e.dma_start(out=lt_scr[0:1, 0:128], in_=row128[:]), inc="s_st", n=16,
                 waits=[("s_gld", v_l1)])
        if debug == "lite":
            emit("gpsimd", lambda e: e.dma_start(out=lt_aloc, in_=a_loc), inc="s_st", n=16)
            emit("gpsimd", lambda e: e.dma_start(out=lt_x1, in_=x1_sb[:]), inc="s_st", n=16,
                 waits=[("s_se", P.val("s_se"))])
            emit("gpsimd", lambda e: e.dma_start(out=lt_h, in_=h_spill), inc="s_st", n=16)
            emit("gpsimd", lambda e: e.dma_start(out=lt_ax, in_=a_xchg), inc="s_st", n=16)
    v_fin = P.val("s_st")
    P.wait("gpsimd", "s_st", v_fin)
    return _finalize(nc, P, dbg)


def _finalize(nc, P, dbg):
    with nc.Block() as block:
        @block.sync
        def _(e):
            for fn in P.ops["sync"]:
                fn(e)

        @block.gpsimd
        def _(e):
            for fn in P.ops["gpsimd"]:
                fn(e)

        @block.vector
        def _(e):
            for fn in P.ops["vector"]:
                fn(e)

        @block.scalar
        def _(e):
            for fn in P.ops["scalar"]:
                fn(e)

        @block.tensor
        def _(e):
            for fn in P.ops["tensor"]:
                fn(e)

    return nc, dbg


def make_in_maps(inputs):
    x = np.ascontiguousarray(inputs["L0_attn_x"], dtype=np.float32)
    Wq = np.asarray(inputs["L0_attn_Wq"], dtype=np.float32)
    Wk = np.asarray(inputs["L0_attn_Wk"], dtype=np.float32)
    Wv = np.asarray(inputs["L0_attn_Wv"], dtype=np.float32)
    bq = np.asarray(inputs["L0_attn_bq"], dtype=np.float32)
    bk = np.asarray(inputs["L0_attn_bk"], dtype=np.float32)
    bv = np.asarray(inputs["L0_attn_bv"], dtype=np.float32)
    beta = np.asarray(inputs["L0_attn_beta"], dtype=np.float32)
    clipx = np.asarray(inputs["L0_ffn_clip_x"], dtype=np.float32)
    clipd = np.asarray(inputs["L0_ffn_clip_down"], dtype=np.float32)
    wg = np.asarray(inputs["L0_ffn_w_gate"], dtype=np.float32)
    wu = np.asarray(inputs["L0_ffn_w_up"], dtype=np.float32)
    wd = np.asarray(inputs["L0_ffn_w_down"], dtype=np.float32)

    tril = np.tril(np.ones((128, 128))).astype(ml_dtypes.bfloat16)
    in_maps = []
    for c in range(8):
        b, p = c // 2, c % 2
        fs = slice(p * 512, (p + 1) * 512)

        def prep_rows(w, c=c):
            rows = []
            for sslot in range(PREP_SLOTS):
                ci = c + 8 * sslot
                if ci < NCH:
                    rows.append(w[ci * 128:(ci + 1) * 128, :])
                else:
                    rows.append(np.zeros((128, D), np.float32))
            return np.ascontiguousarray(np.concatenate(rows, axis=0))
        in_maps.append({
            "x": np.ascontiguousarray(x[b]),
            "x2": np.ascontiguousarray(x[b, p * TPC:(p + 1) * TPC, :]),
            "wq": np.ascontiguousarray(Wq[:, fs]),
            "wk": np.ascontiguousarray(Wk[:, fs]),
            "wv": np.ascontiguousarray(Wv[:, fs]),
            "bq": np.ascontiguousarray(bq[fs]).reshape(1, 512),
            "bk": np.ascontiguousarray(bk[fs]).reshape(1, 512),
            "bv": np.ascontiguousarray(bv[fs]).reshape(1, 512),
            "beta": np.ascontiguousarray(beta[p * 4:(p + 1) * 4]).reshape(1, 4),
            "clipx": np.ascontiguousarray(clipx).reshape(1, 1),
            "clipd": np.ascontiguousarray(clipd).reshape(1, 1),
            "tril": tril,
            "wg_prep": prep_rows(wg),
            "wu_prep": prep_rows(wu),
            "wd_prep": np.ascontiguousarray(wd[c * 128:(c + 1) * 128, :]),
        })
    return in_maps


def kernel(**inputs):
    in_maps = make_in_maps(inputs)
    nc, _ = build_program(debug=False)
    res = run_bass_kernel_spmd(nc, in_maps, core_ids=list(range(8)))
    out = np.empty((B, S, D), dtype=np.float32)
    for c in range(8):
        b, p = c // 2, c % 2
        out[b, p * TPC:(p + 1) * TPC, :] = res.results[c]["y"]
    return out
